# version 05
# baseline (speedup 1.0000x reference)
import numpy as np
import ml_dtypes
from contextlib import ExitStack

import concourse.bass as bass
import concourse.mybir as mybir
from concourse.bass_utils import run_bass_kernel_spmd

F32 = mybir.dt.float32
BF16 = mybir.dt.bfloat16
I32 = mybir.dt.int32
AF = mybir.ActivationFunctionType
ALU = mybir.AluOpType
AX = mybir.AxisListType.X

B, TFULL, H = 4, 8192, 1024
NCORES = 8
TCORE = 4096
TILE = 128
NT = TCORE // TILE
NCORR = 8
NH, DK, DV = 4, 128, 256
KD, VD = 512, 1024
IFFN = 2816
NGT = 22
CRND = 12582912.0
RSMAGIC = 1597463007
EPS = 1e-5
POOLB = 209920
SLABS = [(0, 512), (512, 1024), (1024, 1536), (1536, 2048), (2048, 2560), (2560, 2816)]


class CH:
    def __init__(self, sem, owner):
        self.sem = sem
        self.owner = owner
        self.n = 0


class Buf:
    __slots__ = ("name", "wr", "rd")

    def __init__(self, name):
        self.name = name
        self.wr = None
        self.rd = {}


class K:
    def __init__(self, active, handles, chans, sem_pool):
        self.active = active
        self.h = handles
        self.ch = chans
        self.sem_pool = sem_pool
        self.sem_next = 0
        self.buf_ch = {}
        self.qchans = {"d": [chans["d"]], "g": [chans["g"]]}
        self.waited = {e: {} for e in ("v", "s", "t", "d", "g")}

    def _buf_chan(self, name, qeng):
        if name not in self.buf_ch:
            assert self.sem_next < len(self.sem_pool), "sem pool exhausted"
            c_ = CH(self.sem_pool[self.sem_next], "none")
            self.sem_next += 1
            self.buf_ch[name] = c_
        c_ = self.buf_ch[name]
        if c_ not in self.qchans[qeng]:
            self.qchans[qeng].append(c_)
        return c_

    def _wait(self, eng, ticket):
        if ticket is None:
            return
        ch, val = ticket
        if ch.owner == eng and eng in ("v", "s", "t"):
            return
        w = self.waited[eng]
        if w.get(ch, 0) >= val:
            return
        w[ch] = val
        if self.active == eng:
            self.h[eng].wait_ge(ch.sem, val)

    def op(self, eng, f, *args, reads=(), writes=(), ch=None, force_drain=False, **kw):
        for b in reads:
            self._wait(eng, b.wr)
        for b in writes:
            self._wait(eng, b.wr)
            for c_, val in b.rd.items():
                self._wait(eng, (c_, val))
        chan = None
        if f == "dma_start" and ch is None and writes:
            try:
                if "DRam" not in type(kw["out"].tensor).__name__:
                    chan = self._buf_chan(writes[0].name, eng)
            except Exception:
                chan = None
        if chan is None:
            chan = self.ch[ch or eng]
        inc = 16 if f == "dma_start" else 1
        if self.active == eng:
            inst = getattr(self.h[eng], f)(*args, **kw)
            small = False
            if eng in ("v", "s", "t") and f != "dma_start":
                try:
                    small = args[0].free_size < 64
                except Exception:
                    small = True
            if small or force_drain:
                tail = self.h[eng].drain()
                tail.then_inc(chan.sem, inc)
            else:
                inst.then_inc(chan.sem, inc)
        chan.n += inc
        t = (chan, chan.n)
        for b in writes:
            b.wr = t
            b.rd = {}
        for b in reads:
            if b.rd.get(chan, 0) < chan.n:
                b.rd[chan] = chan.n
        return t

    def barrier(self):
        for qe, ech in (("d", "de"), ("g", "ge")):
            if self.active == qe:
                for c_ in self.qchans[qe]:
                    if c_.n > self.waited[qe].get(c_, 0):
                        self.waited[qe][c_] = c_.n
                        self.h[qe].wait_ge(c_.sem, c_.n)
                self.h[qe].sem_inc(self.ch[ech].sem, 1)
            self.ch[ech].n += 1
        for e in ("v", "s", "t", "d", "g"):
            for cn, c_ in self.ch.items():
                if cn in ("d", "g", "dl", "gl"):
                    continue
                self._wait(e, (c_, c_.n))

    def final_wait(self, eng):
        c_ = self.ch[eng]
        if self.active == eng:
            self.h[eng].wait_ge(c_.sem, c_.n)


def build_nc(dbg=False):
    nc = bass.Bass("TRN2", target_bir_lowering=False)
    d = {}

    def din(name, shape, dtype=F32):
        d[name] = nc.dram_tensor(name, list(shape), dtype, kind="ExternalInput").ap()
        return d[name]

    x_in = din("x", (TCORE, H))
    xprev_in = din("xprev", (1, H))
    cvec_in = din("cvec", (1, H))
    wmod_in = din("wmod", (H, 6 * H))
    gnw_in = din("gnw", (128, VD))
    gnb_in = din("gnb", (128, VD))
    mu0_in = din("mu0", (128, 8))
    bonus_in = din("bonus", (128, NH))
    pmask_in = din("pmask", (128, 1))
    cmb_in = din("cmb", (128, 1))
    clipx_in = din("clipx", (1, 1))
    clipd_in = din("clipd", (1, 1))
    identb_in = din("identb", (128, 128), BF16)
    identf_in = din("identf", (128, 128))
    mask128_in = din("mask128", (128, 512), mybir.dt.uint8)
    onesrow_in = din("onesrow", (1, 128))
    wx0_in = din("wx0T", (128, 8 * 160))
    wx2_in = din("wx2T", (33, 5 * H))
    wr_in = din("wrT", (128, 8 * KD))
    ww1_in = din("ww1T", (128, 8 * 64))
    ww2_in = din("ww2T", (65, KD))
    wk_in = din("wkT", (128, 8 * KD))
    wv_in = din("wvT", (128, 8 * VD))
    wg_in = din("wgT", (128, 8 * VD))
    wo_in = din("woT", (128, 8 * H))
    wgn_in = din("wg_nat", (384, H))
    wun_in = din("wu_nat", (384, H))
    wdn_in = din("wd_nat", (128, IFFN))

    dk = dict(kind="ExternalOutput") if dbg else {}
    out = nc.dram_tensor("out", [TCORE, H], F32, kind="ExternalOutput").ap()
    os_s = nc.dram_tensor("os_s", [TCORE, H], F32, **dk).ap()
    gs_s = nc.dram_tensor("gs_s", [TCORE, H], BF16, **dk).ap()
    rfs_s = nc.dram_tensor("rfs_s", [NCORR * TILE, KD], BF16, **dk).ap()
    x2_s = nc.dram_tensor("x2_s", [TCORE, H], F32, **dk).ap()
    h2_s = nc.dram_tensor("h2_s", [TCORE, H], F32, **dk).ap()
    t_s = nc.dram_tensor("t_s", [TCORE, IFFN], F32, **dk).ap()
    agp_i = nc.dram_tensor("agp_i", [NH * DK * DV], F32).ap()
    agp_o = nc.dram_tensor("agp_o", [2, NH * DK * DV], F32).ap()
    WQB = 6 * 1024 + NGT * 128
    agw_i = nc.dram_tensor("agw_i", [128, WQB], BF16).ap()
    agw_o = nc.dram_tensor("agw_o", [NCORES, 128, WQB], BF16).ap()
    ags_i = nc.dram_tensor("ags_i", [7 * 128], F32).ap()
    ags_o = nc.dram_tensor("ags_o", [NCORES, 7 * 128], F32).ap()
    if dbg:
        dbg_h = nc.dram_tensor("dbg_h", [NT, 128, H], BF16, kind="ExternalOutput").ap()
        dbg_xp = nc.dram_tensor("dbg_xp", [NT, 33, 640], BF16, kind="ExternalOutput").ap()
        dbg_xi5 = nc.dram_tensor("dbg_xi5", [NT, 128, H], BF16, kind="ExternalOutput").ap()
        dbg_pw = nc.dram_tensor("dbg_pw", [NT, 128, 512], F32, kind="ExternalOutput").ap()
        dbg_r = nc.dram_tensor("dbg_r", [NT, 128, 512], BF16, kind="ExternalOutput").ap()
        dbg_v = nc.dram_tensor("dbg_v", [NT, 128, H], BF16, kind="ExternalOutput").ap()
        dbg_xt = nc.dram_tensor("dbg_xt", [NT, 128, H], F32, kind="ExternalOutput").ap()
        dbg_cols = nc.dram_tensor("dbg_cols", [NT, 128, 48], F32, kind="ExternalOutput").ap()
        dbg_xh = nc.dram_tensor("dbg_xh", [NT, 128, H], BF16, kind="ExternalOutput").ap()
        dbg_am = nc.dram_tensor("dbg_am", [NT, 128, 512], BF16, kind="ExternalOutput").ap()
        dbg_kt = nc.dram_tensor("dbg_kt", [NT, 128, 512], BF16, kind="ExternalOutput").ap()
        dbg_rt = nc.dram_tensor("dbg_rt", [NT, 128, 512], BF16, kind="ExternalOutput").ap()
        dbg_kht = nc.dram_tensor("dbg_kht", [NT, 128, 512], BF16, kind="ExternalOutput").ap()
        dbg_rh = nc.dram_tensor("dbg_rh", [NT, 128, 512], BF16, kind="ExternalOutput").ap()
        dbg_ds = nc.dram_tensor("dbg_ds", [NT, 128, 16], F32, kind="ExternalOutput").ap()
        dbg_S = nc.dram_tensor("dbg_S", [NT, 128, H], BF16, kind="ExternalOutput").ap()
        dbg_pc = nc.dram_tensor("dbg_pc", [NT, 128, 520], F32, kind="ExternalOutput").ap()
        dbg_chq = nc.dram_tensor("dbg_chq", [4, 128, IFFN], BF16, kind="ExternalOutput").ap()
        dbg_wqd = nc.dram_tensor("dbg_wqd", [128, NGT * H], BF16, kind="ExternalOutput").ap()
        dbg_sdt = nc.dram_tensor("dbg_sdt", [128, H], BF16, kind="ExternalOutput").ap()
        dbg_cpx = nc.dram_tensor("dbg_cpx", [1, 16], F32, kind="ExternalOutput").ap()
        dbg_bxq = nc.dram_tensor("dbg_bxq", [4, 128, H], BF16, kind="ExternalOutput").ap()
        dbg_wqg = nc.dram_tensor("dbg_wqg", [128, 8 * IFFN], BF16, kind="ExternalOutput").ap()
        dbg_mod = nc.dram_tensor("dbg_mod", [6, 128, H], BF16, kind="ExternalOutput").ap()
        dbg_fm = nc.dram_tensor("dbg_fm", [128, 16], F32, kind="ExternalOutput").ap()
    arh_i = nc.dram_tensor("arh_i", [1], F32).ap()
    arh_o = nc.dram_tensor("arh_o", [1], F32).ap()
    art_i = nc.dram_tensor("art_i", [1], F32).ap()
    art_o = nc.dram_tensor("art_o", [1], F32).ap()

    def program(P, pool, PF, PB):
        def V(off, parts, n, dt):
            esz = 4 if dt in (F32, I32) else (1 if dt == mybir.dt.uint8 else 2)
            assert off + n * esz <= POOLB
            return pool[0:parts, off:off + n * esz].bitcast(dt)

        class Reg:
            def __init__(self, base):
                self.off = base

            def a(self, nbytes, align=64):
                o = (self.off + align - 1) // align * align
                self.off = o + nbytes
                assert self.off <= POOLB, f"pool overflow at {self.off}"
                return o

        pr = Reg(0)
        identb = V(pr.a(256), 128, 128, BF16)
        identf = V(pr.a(512), 128, 128, F32)
        mask128 = V(pr.a(512), 128, 512, mybir.dt.uint8)
        zeros64 = V(pr.a(256), 128, 64, F32)
        zb512 = V(pr.a(1024), 128, 512, BF16)
        ones128 = V(pr.a(512), 128, 128, F32)
        onesrow = V(pr.a(512), 1, 128, F32)
        mu0 = V(pr.a(32), 128, 8, F32)
        u_bf = V(pr.a(8), 128, NH, BF16)
        pmaskc = V(pr.a(4), 128, 1, F32)
        cmbc = V(pr.a(4), 128, 1, F32)
        sa_fm = V(pr.a(32), 128, 8, F32)
        ca1_fm = V(pr.a(32), 128, 8, F32)
        ga_b = V(pr.a(H * 2), 128, H, BF16)
        gb_b = V(pr.a(H * 2), 128, H, BF16)
        cb1_b = V(pr.a(H * 2), 128, H, BF16)
        sb_b = V(pr.a(H * 2), 128, H, BF16)
        gnwh_b = V(pr.a(H * 2), 128, H, BF16)
        gnbh_b = V(pr.a(H * 2), 128, H, BF16)
        oS0 = pr.a(H * 4); S0 = V(oS0, 128, H, F32)
        oS1 = pr.a(H * 4); S1 = V(oS1, 128, H, F32)
        S_bf = V(pr.a(H * 2), 128, H, BF16)
        Sin_bf = V(pr.a(H * 2), 128, H, BF16)
        pcum = V(pr.a(520 * 4), 128, 520, F32)
        eneg = V(pr.a(520 * 4), 128, 520, F32)
        negrun = V(pr.a(16), 128, 4, F32)
        prevcol = V(pr.a(16), 128, 8, BF16)
        ost = pr.a(256)
        stats = V(ost, 128, 64, F32)
        statsi = V(ost, 128, 64, I32)
        cols = V(pr.a(192), 128, 48, F32)
        runmax = V(pr.a(32), 128, 8, F32)
        osc = pr.a(64)
        scrow = V(osc, 1, 16, F32)
        scrowi = V(osc, 1, 16, I32)
        stage = V(pr.a(512), 1, 128, F32)
        cpx = V(pr.a(64), 1, 16, F32)

        WBASE = (pr.off + 63) // 64 * 64
        wa = Reg(WBASE)
        wx0 = V(wa.a(8 * 160 * 2), 128, 8 * 160, BF16)
        wx2 = V(wa.a(5 * H * 2), 33, 5 * H, BF16)
        wrT = V(wa.a(8 * KD * 2), 128, 8 * KD, BF16)
        ww1 = V(wa.a(8 * 64 * 2), 128, 8 * 64, BF16)
        ww2 = V(wa.a(KD * 2), 65, KD, BF16)
        wkT = V(wa.a(8 * KD * 2), 128, 8 * KD, BF16)
        wvT = V(wa.a(8 * VD * 2), 128, 8 * VD, BF16)
        wgT = V(wa.a(8 * VD * 2), 128, 8 * VD, BF16)
        assert wa.off <= WBASE + 90112, wa.off
        woT = V(WBASE + 90112, 128, 8 * H, BF16)
        wqg = V(WBASE, 128, 8 * IFFN, BF16)
        wqu = V(WBASE + 45056, 128, 8 * IFFN, BF16)
        wqd = V(WBASE, 128, NGT * H, BF16)
        WEND = WBASE + 90112 + 16384

        aa = Reg(WEND + 64)
        ABASE = aa.off
        oxt0 = aa.a(H * 4); xt0 = V(oxt0, 128, H, F32)
        oxt1 = aa.a(H * 4); xt1 = V(oxt1, 128, H, F32)
        xts = (xt0, xt1)
        osq = aa.a(H * 4); sqbuf = V(osq, 128, H, F32)
        oxh = aa.a(H * 2); xhat = V(oxh, 128, H, BF16)
        ohf = aa.a(H * 2); h_fm = V(ohf, 128, H, BF16)
        odl = aa.a(H * 2); delta = V(odl, 128, H, BF16)
        omu0 = aa.a(H * 2); omu1 = aa.a(H * 2)
        mu_fm = (V(omu0, 128, H, BF16), V(omu1, 128, H, BF16))
        oxi = [aa.a(H * 2) for _ in range(6)]
        xi = [V(o, 128, H, BF16) for o in oxi]
        xp_sb = V(aa.a(640 * 2), 33, 640, BF16)
        w1_sb = V(aa.a(128 * 2), 65, 128, BF16)
        pw = V(aa.a(512 * 4), 128, 512, F32)
        r_cm = V(aa.a(512 * 2), 128, 512, BF16)
        k_cm = V(aa.a(512 * 2), 128, 512, BF16)
        facR = V(aa.a(512 * 2), 128, 512, BF16)
        facK = V(aa.a(512 * 2), 128, 512, BF16)
        rt_b = V(aa.a(512 * 2), 128, 512, BF16)
        kt_b = V(aa.a(512 * 2), 128, 512, BF16)
        khc_b = V(aa.a(512 * 2), 128, 512, BF16)
        khat_tm = V(aa.a(512 * 2), 128, 512, BF16)
        rhat_b = V(aa.a(512 * 2), 128, 512, BF16)
        rf_b = V(aa.a(512 * 2), 128, 512, BF16)
        rk_b = V(aa.a(512 * 2), 128, 512, BF16)
        am_b = V(aa.a(512 * 2), 128, 512, BF16)
        v_tm = V(aa.a(H * 2), 128, H, BF16)
        g_tm = V(aa.a(H * 2), 128, H, BF16)
        o_sb = V(aa.a(H * 4), 128, H, F32)
        d_sb = V(aa.a(64), 128, 16, F32)
        modt = (V(oxt0, 128, 512, F32), V(oxt1, 128, 512, F32))
        srep = V(osq, 128, 1024, F32)
        crow = V(ohf, 1, H, F32)
        xprow = V(omu0, 1, H, F32)
        mscrA = V(oxi[0], 128, H, F32)
        mscrB = V(oxi[2], 128, H, F32)
        mscrC = V(oxi[4], 128, H, F32)
        qnat = V(oxt0, 128, IFFN, F32)
        qint = V(ohf, 128, IFFN, BF16)
        qtrn = V(oxi[0], 128, IFFN, BF16)
        a3 = Reg(ABASE)
        a3os = (V(a3.a(H * 4), 128, H, F32), V(a3.a(H * 4), 128, H, F32))
        a3gs = (V(a3.a(H * 2), 128, H, BF16), V(a3.a(H * 2), 128, H, BF16))
        a3xs = (V(a3.a(H * 4), 128, H, F32), V(a3.a(H * 4), 128, H, F32))
        a3rfs = (V(a3.a(KD * 2), 128, KD, BF16), V(a3.a(KD * 2), 128, KD, BF16))
        a3sq = V(a3.a(H * 4), 128, H, F32)
        xo_b = V(a3.a(H * 2), 128, H, BF16)
        th_b = V(a3.a(H * 2), 128, H, BF16)
        sg2_b = V(a3.a(H * 2), 128, H, BF16)
        of_b = V(a3.a(H * 2), 128, H, BF16)
        of_fm = V(a3.a(H * 2), 128, H, BF16)
        x2_t = V(a3.a(H * 4), 128, H, F32)
        h2_t = V(a3.a(H * 4), 128, H, F32)
        t1_s = V(a3.a(H * 4), 128, H, F32)
        bb_ = Reg(ABASE)
        bhs = (V(bb_.a(H * 4), 128, H, F32), V(bb_.a(H * 4), 128, H, F32))
        bxq = V(bb_.a(H * 2), 128, H, BF16)
        bxqf = V(bb_.a(H * 2), 128, H, BF16)
        obt0 = bb_.a(IFFN * 4)
        obt1 = bb_.a(IFFN * 4)
        bts = (V(obt0, 128, IFFN, F32), V(obt1, 128, IFFN, F32))
        sg_row = V(obt0, 1, IFFN, F32)
        su_row = V(obt1, 1, IFFN, F32)
        bgr = V(bb_.a(512 * 4), 128, 512, F32)
        bur = V(bb_.a(512 * 4), 128, 512, F32)
        bth = V(bb_.a(512 * 2), 128, 512, BF16)
        bq1 = V(bb_.a(H * 4), 128, H, F32)
        sxg_b = V(bb_.a(IFFN * 2), 128, IFFN, BF16)
        sxu_b = V(bb_.a(IFFN * 2), 128, IFFN, BF16)
        cr = Reg(ABASE)
        oct0 = cr.a(IFFN * 4)
        cts = (V(oct0, 128, IFFN, F32), V(cr.a(IFFN * 4), 128, IFFN, F32))
        sd_row = V(oct0, 1, H, F32)
        cxs = (V(cr.a(H * 4), 128, H, F32), V(cr.a(H * 4), 128, H, F32))
        chq = V(cr.a(IFFN * 2), 128, IFFN, BF16)
        chqf = V(cr.a(IFFN * 2), 128, IFFN, BF16)
        cq1 = V(cr.a(IFFN * 4), 128, IFFN, F32)
        co_t = V(cr.a(H * 4), 128, H, F32)
        sdtgb_b = V(cr.a(H * 2), 128, H, BF16)

        BUFS = {}

        def NB(name):
            if name not in BUFS:
                BUFS[name] = Buf(name)
            return BUFS[name]

        pfb = [NB(f"PF{i}") for i in range(6)]
        pbb = [NB(f"PB{i}") for i in range(2)]

        def T(f, *a, **k):
            return P.op("t", f, *a, **k)

        def VE(f, *a, **k):
            return P.op("v", f, *a, **k)

        def SE(f, *a, **k):
            return P.op("s", f, *a, **k)

        def GD(f, *a, **k):
            return P.op("g", f, *a, **k)

        def SD(f, *a, **k):
            return P.op("d", f, *a, **k)

        bW = NB("wA"); bWq = NB("wQ"); bConst = NB("const"); bMod = NB("mod")
        bS = [NB("S0"), NB("S1")]
        bSbf = NB("Sbf"); bSin = NB("Sin"); bPcum = NB("pcum"); bEneg = NB("eneg")
        bNrun = NB("nrun"); bPrev = NB("prev"); bCols = NB("cols"); bStats = NB("stats")
        bRmx = NB("rmx"); bRows = NB("rows"); bStage = NB("stage")
        bXp = NB("xp"); bW1 = NB("w1"); bPw = NB("pw"); bRc = NB("rc"); bKc = NB("kc")
        bFacR = NB("facR"); bFacK = NB("facK"); bRt = NB("rt"); bKt = NB("kt")
        bKhc = NB("khc"); bKht = NB("kht"); bRh = NB("rh"); bRf = NB("rf")
        bRk = NB("rk"); bAm = NB("am"); bVt = NB("vt"); bGt = NB("gt"); bOt = NB("ot")
        bDsb = NB("dsb"); bHfm = NB("hfm"); bDelta = NB("dl"); bXhat = NB("xh")
        bSq = NB("sq"); bMu = [NB("mu0b"), NB("mu1b")]
        bXi = [NB(f"xi{i}") for i in range(6)]
        bXt = [NB("xt0b"), NB("xt1b")]
        bOS = NB("osS"); bGS = NB("gsS"); bRFS = NB("rfsS")
        bX2S = NB("x2S"); bH2S = NB("h2S"); bTS = NB("tS")
        bAGpi = NB("agpi"); bAGpo = NB("agpo"); bAGwi = NB("agwi"); bAGwo = NB("agwo")
        bAGsi = NB("agsi"); bAGso = NB("agso"); bARh = NB("arh"); bARho = NB("arho")
        bARt = NB("art"); bARto = NB("arto")
        bQn = NB("qn"); bQi = NB("qi"); bQt = NB("qt")
        bMs = [NB("msA"), NB("msB"), NB("msC")]
        bMt = [NB("mt0"), NB("mt1")]
        bSrep = NB("srep"); bCrow = NB("crow"); bXpr = NB("xpr")
        bA3o = [NB("a3o0"), NB("a3o1")]
        bA3g = [NB("a3g0"), NB("a3g1")]
        bA3x = [NB("a3x0"), NB("a3x1")]
        bA3rf = [NB("a3rf0"), NB("a3rf1")]
        bA3sq = NB("a3sq"); bXo = NB("xob"); bTh = NB("thb"); bSg2 = NB("sg2")
        bOf = NB("ofb"); bOffm = NB("offm"); bX2t = NB("x2t"); bH2t = NB("h2t")
        bT1 = NB("t1s")
        bBh = [NB("bh0"), NB("bh1")]; bBt = [NB("bt0"), NB("bt1")]
        bBxq = NB("bxq"); bBxqf = NB("bxqf"); bBgr = NB("bgr"); bBur = NB("bur")
        bBth = NB("bth"); bBq1 = NB("bq1"); bSx = NB("sxb")
        bCt = [NB("ct0"), NB("ct1")]; bCx = [NB("cx0"), NB("cx1")]
        bChq = NB("chq"); bChqf = NB("chqf"); bCq1 = NB("cq1"); bCo = NB("co")
        bSdt = NB("sdt"); bOutD = NB("outd")

        def rsqrt_newton(dst_col, var_col, ncols, iters=11):
            u = stats[:, 0:ncols]
            r_ = stats[:, 8:8 + ncols]
            h_ = stats[:, 12:12 + ncols]
            a_ = stats[:, 16:16 + ncols]
            ra = stats[:, 24:24 + ncols]
            t_ = stats[:, 32:32 + ncols]
            VE("tensor_scalar", u, var_col, EPS, None, ALU.add, reads=(bCols,), writes=(bStats,))
            VE("reciprocal", r_, u, reads=(bStats,), writes=(bStats,))
            VE("tensor_scalar", h_, r_, 0.5, None, ALU.mult, reads=(bStats,), writes=(bStats,))
            VE("tensor_scalar", a_, r_, 0.5, 0.5, ALU.mult, ALU.add, reads=(bStats,), writes=(bStats,))
            for _ in range(iters):
                VE("reciprocal", ra, a_, reads=(bStats,), writes=(bStats,))
                VE("tensor_tensor", t_, h_, ra, ALU.mult, reads=(bStats,), writes=(bStats,))
                VE("scalar_tensor_tensor", a_, a_, 0.5, t_, ALU.mult, ALU.add,
                   reads=(bStats,), writes=(bStats,))
            VE("tensor_copy", dst_col, a_, reads=(bStats,), writes=(bCols,))

        def ln_stats(src, b_src, rstd_col, negm_col, sqb, b_sq, ln_iters=7):
            VE("reduce_sum", cols[:, 40:41], src, axis=AX, reads=(b_src,), writes=(bCols,))
            SE("activation", sqb, src, AF.Square, accum_out=cols[:, 41:42],
               reads=(b_src,), writes=(b_sq, bCols))
            VE("tensor_scalar", cols[:, 42:43], cols[:, 40:41], 1.0 / H, None, ALU.mult,
               reads=(bCols,), writes=(bCols,))
            VE("tensor_scalar", cols[:, 43:44], cols[:, 41:42], 1.0 / H, None, ALU.mult,
               reads=(bCols,), writes=(bCols,))
            VE("tensor_scalar", cols[:, 44:45], cols[:, 42:43], -1.0, None, ALU.mult,
               reads=(bCols,), writes=(bCols,))
            VE("scalar_tensor_tensor", cols[:, 45:46], cols[:, 42:43], cols[:, 44:45],
               cols[:, 43:44], ALU.mult, ALU.add, reads=(bCols,), writes=(bCols,))
            rsqrt_newton(rstd_col, cols[:, 45:46], 1, iters=ln_iters)
            VE("tensor_tensor", cols[:, 46:47], cols[:, 42:43], rstd_col, ALU.mult,
               reads=(bCols,), writes=(bCols,))
            VE("tensor_scalar", negm_col, cols[:, 46:47], -1.0, None, ALU.mult,
               reads=(bCols,), writes=(bCols,))

        SD("dma_start", out=identb, in_=identb_in, writes=(bConst,))
        SD("dma_start", out=identf, in_=identf_in, writes=(bConst,))
        SD("dma_start", out=mask128, in_=mask128_in, writes=(bConst,))
        SD("dma_start", out=onesrow, in_=onesrow_in, writes=(bConst,))
        SD("dma_start", out=mu0, in_=mu0_in, writes=(bConst,))
        SD("dma_start", out=pmaskc, in_=pmask_in, writes=(bConst,))
        SD("dma_start", out=cmbc, in_=cmb_in, writes=(bConst,))
        SD("dma_start", out=crow, in_=cvec_in, writes=(bCrow,))
        SD("dma_start", out=xprow, in_=xprev_in, writes=(bXpr,))
        GD("dma_start", out=gnwh_b, in_=gnw_in, writes=(bMod,))
        GD("dma_start", out=gnbh_b, in_=gnb_in, writes=(bMod,))
        SD("dma_start", out=cpx[0:1, 0:1], in_=clipx_in, writes=(bRows,))
        SD("dma_start", out=cpx[0:1, 1:2], in_=clipd_in, writes=(bRows,))
        GD("dma_start", out=wx0, in_=wx0_in, writes=(bW,))
        GD("dma_start", out=wx2, in_=wx2_in, writes=(bW,))
        GD("dma_start", out=wrT, in_=wr_in, writes=(bW,))
        GD("dma_start", out=ww1, in_=ww1_in, writes=(bW,))
        GD("dma_start", out=ww2, in_=ww2_in, writes=(bW,))
        GD("dma_start", out=wkT, in_=wk_in, writes=(bW,))
        GD("dma_start", out=wvT, in_=wv_in, writes=(bW,))
        GD("dma_start", out=wgT, in_=wg_in, writes=(bW,))
        GD("dma_start", out=woT, in_=wo_in, writes=(bW,))
        GD("dma_start", out=u_bf, in_=bonus_in, writes=(bW,))
        VE("memset", zeros64, 0.0, writes=(bConst,))
        VE("memset", zb512, 0.0, writes=(bConst,))
        VE("memset", ones128, 1.0, writes=(bConst,))
        VE("memset", S0, 0.0, writes=(bS[0],))
        VE("memset", S_bf, 0.0, writes=(bSbf,))
        VE("memset", runmax, 0.0, writes=(bRmx,))
        VE("memset", xp_sb[32:33, :], 1.0, writes=(bXp,))
        VE("memset", w1_sb[64:65, :], 1.0, writes=(bW1,))
        for g in range(8):
            VE("memset", pcum[:, g * 65:g * 65 + 1], 0.0, writes=(bPcum,))
        for hh in range(NH):
            VE("tensor_copy", negrun[:, hh:hh + 1], cmbc, reads=(bConst,), writes=(bNrun,))

        for c in range(8):
            T("transpose", PF[0][0:128, c:c + 1], crow[0:1, c * 128:(c + 1) * 128],
              identf[0:1, 0:1], reads=(bCrow, bConst), writes=(pfb[0],))
        SE("copy", stats[:, 48:56], PF[0][:, 0:8], reads=(pfb[0],), writes=(bStats,))
        cfm = stats[:, 48:56]
        SE("activation", stats[:, 56:64], cfm, AF.Tanh, scale=0.5, reads=(bStats,), writes=(bStats,))
        VE("tensor_scalar", cols[:, 0:8], stats[:, 56:64], 0.5, 0.5, ALU.mult, ALU.add,
           reads=(bStats,), writes=(bCols,))
        VE("tensor_tensor", cols[:, 8:16], cols[:, 0:8], cfm, ALU.mult,
           reads=(bCols, bStats), writes=(bCols,))
        for c in range(8):
            VE("tensor_scalar", srep[:, c * 128:(c + 1) * 128], ones128, cols[:, 8 + c:9 + c],
               None, ALU.mult, reads=(bConst, bCols), writes=(bSrep,))

        for s in range(12):
            mt = modt[s % 2]
            bmt = bMt[s % 2]
            for kc in range(8):
                SD("dma_start", out=mt,
                   in_=wmod_in[kc * 128:(kc + 1) * 128, s * 512:(s + 1) * 512], writes=(bmt,))
                if s < 4:
                    for grp in range(4):
                        T("matmul", PF[1 + grp][:, s * 8:(s + 1) * 8],
                          mt[:, grp * 128:(grp + 1) * 128], srep[:, 0:8],
                          start=(kc == 0), stop=(kc == 7),
                          reads=(bmt, bSrep), writes=(pfb[1 + grp],))
                else:
                    bankk = pfb[2 + (s % 2)]
                    T("matmul", PF[2 + (s % 2)][:, 0:512], srep[:, kc * 128:(kc + 1) * 128],
                      mt[:, 0:512], start=(kc == 0), stop=(kc == 7),
                      reads=(bmt, bSrep), writes=(bankk,))
            if s == 3:
                for c in range(8):
                    s_, grp = divmod(c, 4)
                    SE("copy", sa_fm[:, c:c + 1], PF[1 + grp][:, s_ * 8:s_ * 8 + 1],
                       reads=(pfb[1 + grp],), writes=(bMod,))
                for c in range(8):
                    s_, grp = divmod(8 + c, 4)
                    SE("activation", ca1_fm[:, c:c + 1], PF[1 + grp][:, s_ * 8:s_ * 8 + 1],
                       AF.Identity, bias=1.0, reads=(pfb[1 + grp],), writes=(bMod,))
            if s >= 4:
                bankk = pfb[2 + (s % 2)]
                pf = PF[2 + (s % 2)]
                j0 = (s - 4) * 512
                if j0 < 1024:
                    SE("copy", mscrA[:, j0:j0 + 512], pf[:, 0:512], reads=(bankk,), writes=(bMs[0],))
                elif j0 < 2048:
                    SE("copy", sb_b[:, j0 - 1024:j0 - 512], pf[:, 0:512], reads=(bankk,), writes=(bMod,))
                elif j0 < 3072:
                    SE("activation", cb1_b[:, j0 - 2048:j0 - 1536], pf[:, 0:512],
                       AF.Identity, bias=1.0, reads=(bankk,), writes=(bMod,))
                else:
                    SE("copy", mscrB[:, j0 - 3072:j0 - 2560], pf[:, 0:512], reads=(bankk,), writes=(bMs[1],))
        for (src, bsrc, dstb) in ((mscrA, bMs[0], ga_b), (mscrB, bMs[1], gb_b)):
            SE("activation", mscrC, src, AF.Abs, reads=(bsrc,), writes=(bMs[2],))
            VE("tensor_scalar", mscrC, mscrC, 0.25, None, ALU.max, reads=(bMs[2],), writes=(bMs[2],))
            VE("tensor_scalar", src, src, 0.0, None, ALU.is_ge, reads=(bsrc,), writes=(bsrc,))
            VE("tensor_scalar", src, src, 2.0, -1.0, ALU.mult, ALU.add, reads=(bsrc,), writes=(bsrc,))
            VE("tensor_tensor", dstb, mscrC, src, ALU.mult, reads=(bMs[2], bsrc), writes=(bMod,))

        VE("reduce_sum", scrow[0:1, 0:1], xprow, axis=AX, reads=(bXpr,), writes=(bRows,))
        SE("activation", mscrA[0:1, :], xprow, AF.Square, accum_out=scrow[0:1, 1:2],
           reads=(bXpr,), writes=(bMs[0], bRows))
        VE("tensor_scalar", scrow[0:1, 2:3], scrow[0:1, 0:1], 1.0 / H, None, ALU.mult,
           reads=(bRows,), writes=(bRows,))
        VE("tensor_scalar", scrow[0:1, 3:4], scrow[0:1, 1:2], 1.0 / H, None, ALU.mult,
           reads=(bRows,), writes=(bRows,))
        VE("tensor_scalar", scrow[0:1, 4:5], scrow[0:1, 2:3], -1.0, None, ALU.mult,
           reads=(bRows,), writes=(bRows,))
        VE("scalar_tensor_tensor", scrow[0:1, 5:6], scrow[0:1, 2:3], scrow[0:1, 4:5],
           scrow[0:1, 3:4], ALU.mult, ALU.add, reads=(bRows,), writes=(bRows,))
        VE("tensor_scalar", scrow[0:1, 6:7], scrow[0:1, 5:6], EPS, None, ALU.add,
           reads=(bRows,), writes=(bRows,))
        VE("reciprocal", scrow[0:1, 7:8], scrow[0:1, 6:7], reads=(bRows,), writes=(bRows,))
        VE("tensor_scalar", scrow[0:1, 8:9], scrow[0:1, 7:8], 0.5, 0.5, ALU.mult, ALU.add,
           reads=(bRows,), writes=(bRows,))
        for _ in range(11):
            VE("reciprocal", scrow[0:1, 10:11], scrow[0:1, 8:9], reads=(bRows,), writes=(bRows,))
            VE("tensor_tensor", scrow[0:1, 10:11], scrow[0:1, 10:11], scrow[0:1, 7:8], ALU.mult,
               reads=(bRows,), writes=(bRows,))
            VE("tensor_tensor", scrow[0:1, 8:9], scrow[0:1, 8:9], scrow[0:1, 10:11], ALU.add,
               reads=(bRows,), writes=(bRows,))
            VE("tensor_scalar", scrow[0:1, 8:9], scrow[0:1, 8:9], 0.5, None, ALU.mult,
               reads=(bRows,), writes=(bRows,))
        VE("tensor_tensor", scrow[0:1, 12:13], scrow[0:1, 2:3], scrow[0:1, 8:9], ALU.mult,
           reads=(bRows,), writes=(bRows,))
        VE("tensor_scalar", scrow[0:1, 13:14], scrow[0:1, 12:13], -1.0, None, ALU.mult,
           reads=(bRows,), writes=(bRows,))
        SE("activation", mscrB[0:1, :], xprow, AF.Identity, scale=scrow[0:1, 8:9],
           bias=scrow[0:1, 13:14], reads=(bXpr, bRows), writes=(bMs[1],))
        for c in range(8):
            T("transpose", PF[5][0:128, c:c + 1], mscrB[0:1, c * 128:(c + 1) * 128],
              identf[0:1, 0:1], reads=(bMs[1], bConst), writes=(pfb[5],))
        SE("copy", stats[:, 40:48], PF[5][:, 0:8], reads=(pfb[5],), writes=(bStats,))
        for c in range(8):
            VE("scalar_tensor_tensor", stats[:, 32 + c:33 + c], stats[:, 40 + c:41 + c],
               ca1_fm[:, c:c + 1], sa_fm[:, c:c + 1], ALU.mult, ALU.add,
               reads=(bStats, bMod), writes=(bStats,))
        VE("tensor_scalar", prevcol, stats[:, 32:40], pmaskc[:, 0:1], None, ALU.mult,
           reads=(bStats, bConst), writes=(bPrev,))

        if dbg:
            for j, (tb, bb2) in enumerate(((ga_b, bMod), (gb_b, bMod), (cb1_b, bMod),
                                           (sb_b, bMod), (gnwh_b, bMod), (gnbh_b, bMod))):
                SD("dma_start", out=dbg_mod[j], in_=tb, reads=(bb2,), writes=(NB("dbgd"),))
            SD("dma_start", out=dbg_fm[:, 0:8], in_=sa_fm, reads=(bMod,), writes=(NB("dbgd"),))
            SD("dma_start", out=dbg_fm[:, 8:16], in_=ca1_fm, reads=(bMod,), writes=(NB("dbgd"),))

        P.barrier()

        SD("dma_start", out=xts[0], in_=x_in[0:TILE, :], writes=(bXt[0],))
        sparity = 0
        Ss = (S0, S1)
        for i in range(NT):
            xt = xts[i % 2]
            bX = bXt[i % 2]
            if i + 1 < NT:
                SD("dma_start", out=xts[(i + 1) % 2], in_=x_in[(i + 1) * TILE:(i + 2) * TILE, :],
                   writes=(bXt[(i + 1) % 2],))
            if dbg:
                SD("dma_start", out=dbg_xt[i], in_=xt, reads=(bX,), writes=(NB("dbgd"),))
            ln_stats(xt, bX, cols[:, 20:21], cols[:, 21:22], sqbuf, bSq, ln_iters=5)
            SE("activation", xhat, xt, AF.Identity, scale=cols[:, 20:21], bias=cols[:, 21:22],
               reads=(bX, bCols), writes=(bXhat,))
            if dbg:
                SD("dma_start", out=dbg_cols[i], in_=cols, reads=(bCols,), writes=(NB("dbgd"),))
                SD("dma_start", out=dbg_xh[i], in_=xhat, reads=(bXhat,), writes=(NB("dbgd"),))
            for c in range(8):
                T("transpose", PB[0][:, c * 128:(c + 1) * 128], xhat[:, c * 128:(c + 1) * 128],
                  identb, reads=(bXhat, bConst), writes=(pbb[0],))
            for c in range(8):
                SE("activation", h_fm[:, c * 128:(c + 1) * 128], PB[0][:, c * 128:(c + 1) * 128],
                   AF.Identity, scale=ca1_fm[:, c:c + 1], bias=sa_fm[:, c:c + 1],
                   reads=(pbb[0], bMod), writes=(bHfm,))
            hh2 = h_fm.rearrange("p (c e) -> p c e", c=8)
            dd2 = delta.rearrange("p (c e) -> p c e", c=8)
            pc2 = prevcol.rearrange("p (c e) -> p c e", e=1)
            VE("tensor_tensor", dd2[:, :, 0:1], pc2, hh2[:, :, 0:1], ALU.subtract,
               reads=(bPrev, bHfm), writes=(bDelta,))
            VE("tensor_tensor", dd2[:, :, 1:128], hh2[:, :, 0:127], hh2[:, :, 1:128],
               ALU.subtract, reads=(bHfm,), writes=(bDelta,))
            VE("tensor_copy", prevcol, hh2[:, :, 127], reads=(bHfm,), writes=(bPrev,))
            for c in range(8):
                VE("scalar_tensor_tensor", xi[0][:, c * 128:(c + 1) * 128],
                   delta[:, c * 128:(c + 1) * 128], mu0[:, c:c + 1],
                   h_fm[:, c * 128:(c + 1) * 128], ALU.mult, ALU.add,
                   reads=(bDelta, bConst, bHfm), writes=(bXi[0],))
            for n in range(5):
                bankk = pfb[0] if n < 4 else pfb[1]
                pf = PF[0] if n < 4 else PF[1]
                col0 = (n % 4) * 128 if n < 4 else 0
                for kc in range(8):
                    T("matmul", pf[0:32, col0:col0 + 128],
                      wx0[:, kc * 160 + n * 32:kc * 160 + (n + 1) * 32],
                      xi[0][:, kc * 128:(kc + 1) * 128], start=(kc == 0), stop=(kc == 7),
                      reads=(bW, bXi[0]), writes=(bankk,))
            SE("activation", xp_sb[0:32, 0:512], PF[0][0:32, 0:512], AF.Tanh,
               reads=(pfb[0],), writes=(bXp,))
            SE("activation", xp_sb[0:32, 512:640], PF[1][0:32, 0:128], AF.Tanh,
               reads=(pfb[1],), writes=(bXp,))
            XIMAP = (1, 2, 3, 4, 5)

            def emit_mu_xi(n):
                bankk = pfb[2 + (n % 2)]
                pf = PF[2 + (n % 2)]
                mu = mu_fm[n % 2]
                bM = bMu[n % 2]
                for c8 in range(8):
                    T("matmul", pf[:, (c8 % 4) * 128:(c8 % 4 + 1) * 128],
                      wx2[0:33, n * H + c8 * 128:n * H + (c8 + 1) * 128],
                      xp_sb[0:33, n * 128:(n + 1) * 128], start=True, stop=True,
                      reads=(bW, bXp), writes=(bankk,))
                    if c8 == 3 or c8 == 7:
                        SE("copy", mu[:, (c8 - 3) * 128:(c8 + 1) * 128], pf[:, 0:512],
                           reads=(bankk,), writes=(bM,))
                xin = xi[XIMAP[n]]
                bxin = bXi[XIMAP[n]]
                VE("tensor_tensor", xin, delta, mu, ALU.mult, reads=(bDelta, bM), writes=(bxin,))
                VE("tensor_tensor", xin, xin, h_fm, ALU.add, reads=(bxin, bHfm), writes=(bxin,))

            def emit_rk(which):
                xin, bxin, wt, banki, dst, bdst = (
                    (xi[1], bXi[1], wrT, 4, r_cm, bRc) if which == 0 else
                    (xi[3], bXi[3], wkT, 5, k_cm, bKc))
                bankk = pfb[banki]
                pf = PF[banki]
                for fc in range(4):
                    for kc in range(8):
                        T("matmul", pf[:, fc * 128:(fc + 1) * 128],
                          wt[:, kc * KD + fc * 128:kc * KD + (fc + 1) * 128],
                          xin[:, kc * 128:(kc + 1) * 128], start=(kc == 0), stop=(kc == 7),
                          reads=(bW, bxin), writes=(bankk,))
                SE("copy", dst, pf[:, 0:512], reads=(bankk,), writes=(bdst,))

            def emit_w():
                for kc in range(8):
                    T("matmul", PF[0][0:64, 0:128], ww1[:, kc * 64:(kc + 1) * 64],
                      xi[2][:, kc * 128:(kc + 1) * 128], start=(kc == 0), stop=(kc == 7),
                      reads=(bW, bXi[2]), writes=(pfb[0],))
                SE("activation", w1_sb[0:64, :], PF[0][0:64, 0:128], AF.Tanh,
                   reads=(pfb[0],), writes=(bW1,))
                for fc in range(4):
                    T("matmul", PF[1][:, fc * 128:(fc + 1) * 128],
                      ww2[0:65, fc * 128:(fc + 1) * 128], w1_sb[0:65, :], start=True, stop=True,
                      reads=(bW, bW1), writes=(pfb[1],))
                SE("activation", pw, PF[1][:, 0:512], AF.Exp, reads=(pfb[1],), writes=(bPw,))

            def emit_vg(which, half):
                xin, bxin, wt, dst, bdst = (
                    (xi[4], bXi[4], wvT, v_tm, bVt) if which == 0 else
                    (xi[5], bXi[5], wgT, g_tm, bGt))
                bankk = pfb[2 + half]
                pf = PF[2 + half]
                for kc in range(8):
                    T("matmul", pf[:, 0:512], xin[:, kc * 128:(kc + 1) * 128],
                      wt[:, kc * VD + half * 512:kc * VD + (half + 1) * 512],
                      start=(kc == 0), stop=(kc == 7), reads=(bW, bxin), writes=(bankk,))
                SE("copy", dst[:, half * 512:(half + 1) * 512], pf[:, 0:512],
                   reads=(bankk,), writes=(bdst,))

            emit_mu_xi(0)
            emit_mu_xi(1)
            emit_rk(0)
            emit_mu_xi(2)
            emit_w()
            emit_mu_xi(3)
            emit_rk(1)
            emit_mu_xi(4)
            emit_vg(0, 0)
            emit_vg(0, 1)
            emit_vg(1, 0)
            emit_vg(1, 1)
            for g in range(8):
                hh, sub = divmod(g, 2)
                VE("tensor_tensor_scan", pcum[:, g * 65 + 1:g * 65 + 65],
                   pw[:, hh * 128 + sub * 64:hh * 128 + (sub + 1) * 64], zeros64, 0.0,
                   ALU.add, ALU.add, reads=(bPw, bConst), writes=(bPcum,))
            SE("activation", eneg, pcum, AF.Exp, scale=-1.0, reads=(bPcum,), writes=(bEneg,))
            plc = pcum.rearrange("p (g e) -> p g e", g=8)[:, :, 64]
            VE("tensor_scalar", cols[:, 0:8], plc, 0.5, None, ALU.mult, reads=(bPcum,), writes=(bCols,))
            VE("tensor_scalar", cols[:, 8:16], plc, -0.5, None, ALU.mult, reads=(bPcum,), writes=(bCols,))
            SE("activation", cols[:, 24:32], cols[:, 8:16], AF.Exp, reads=(bCols,), writes=(bCols,))
            for g in range(8):
                SE("activation", facR[:, g * 64:(g + 1) * 64], pcum[:, g * 65:g * 65 + 64],
                   AF.Exp, scale=-1.0, bias=cols[:, g:g + 1], reads=(bPcum, bCols), writes=(bFacR,))
                SE("activation", facK[:, g * 64:(g + 1) * 64], pcum[:, g * 65 + 1:g * 65 + 65],
                   AF.Exp, scale=1.0, bias=cols[:, 8 + g:9 + g], reads=(bPcum, bCols), writes=(bFacK,))
            VE("tensor_tensor", rt_b, r_cm, facR, ALU.mult, reads=(bRc, bFacR), writes=(bRt,))
            VE("tensor_tensor", kt_b, k_cm, facK, ALU.mult, reads=(bKc, bFacK), writes=(bKt,))
            for g in range(8):
                VE("tensor_scalar", khc_b[:, g * 64:(g + 1) * 64], kt_b[:, g * 64:(g + 1) * 64],
                   cols[:, 24 + g:25 + g], None, ALU.mult, reads=(bKt, bCols), writes=(bKhc,))
                VE("tensor_tensor", rhat_b[:, g * 64:(g + 1) * 64], r_cm[:, g * 64:(g + 1) * 64],
                   eneg[:, g * 65:g * 65 + 64], ALU.mult, reads=(bRc, bEneg), writes=(bRh,))
            if i < NCORR:
                for hh in range(NH):
                    VE("tensor_tensor", cols[:, 32 + hh:33 + hh], negrun[:, hh:hh + 1],
                       pcum[:, (hh * 2) * 65 + 64:(hh * 2) * 65 + 65], ALU.subtract,
                       reads=(bNrun, bPcum), writes=(bCols,))
                for g in range(8):
                    hh, sub = divmod(g, 2)
                    bias = negrun[:, hh:hh + 1] if sub == 0 else cols[:, 32 + hh:33 + hh]
                    br = bNrun if sub == 0 else bCols
                    SE("activation", facR[:, g * 64:(g + 1) * 64], pcum[:, g * 65:g * 65 + 64],
                       AF.Exp, scale=-1.0, bias=bias, reads=(bPcum, br), writes=(bFacR,))
                VE("tensor_tensor", rf_b, r_cm, facR, ALU.mult, reads=(bRc, bFacR), writes=(bRf,))
                for hh in range(NH):
                    VE("tensor_tensor", negrun[:, hh:hh + 1], cols[:, 32 + hh:33 + hh],
                       pcum[:, (hh * 2 + 1) * 65 + 64:(hh * 2 + 1) * 65 + 65], ALU.subtract,
                       reads=(bCols, bPcum), writes=(bNrun,))
                SD("dma_start", out=rfs_s[i * TILE:(i + 1) * TILE, :], in_=rf_b,
                   reads=(bRf,), writes=(bRFS,))
            VE("tensor_tensor", rk_b, r_cm, k_cm, ALU.mult, reads=(bRc, bKc), writes=(bRk,))
            for g in range(8):
                hh, sub = divmod(g, 2)
                sl = slice(hh * 128 + sub * 64, hh * 128 + (sub + 1) * 64)
                kw = {"tile_position": (0, 64)} if sub == 1 else {}
                T("matmul", PF[0][sub * 64:(sub + 1) * 64, g * 64:(g + 1) * 64],
                  kt_b[:, sl], rt_b[:, sl], start=True, stop=True,
                  reads=(bKt, bRt), writes=(pfb[0],), **kw)
            VE("select", am_b, mask128, PF[0][:, 0:512], zb512,
               reads=(pfb[0], bConst), writes=(bAm,))
            for g in range(8):
                hh, sub = divmod(g, 2)
                sl = slice(hh * 128 + sub * 64, hh * 128 + (sub + 1) * 64)
                kw = {"tile_position": (0, 64)} if sub == 1 else {}
                T("matmul", PF[1][sub * 64:(sub + 1) * 64, hh:hh + 1], rk_b[:, sl],
                  u_bf[:, hh:hh + 1], start=True, stop=True, reads=(bRk, bW),
                  writes=(pfb[1],), **kw)
            for g in range(8):
                hh, sub = divmod(g, 2)
                sl = slice(hh * 128 + sub * 64, hh * 128 + (sub + 1) * 64)
                kw = {"tile_position": (0, 64)} if sub == 1 else {}
                T("transpose", PB[1][sub * 64:(sub + 1) * 64, hh * 128:(hh + 1) * 128],
                  khc_b[:, sl], identb, reads=(bKhc, bConst), writes=(pbb[1],), **kw)
            SE("copy", khat_tm, PB[1][:, 0:512], reads=(pbb[1],), writes=(bKht,))
            SE("copy", d_sb[:, 0:NH], PF[1][:, 0:NH], reads=(pfb[1], pbb[1]), writes=(bDsb,))
            for sub in range(2):
                Scur, bScur = Ss[sparity], bS[sparity]
                Snew, bSnew = Ss[1 - sparity], bS[1 - sparity]
                r0, r1 = sub * 64, sub * 64 + 64
                for hh in range(NH):
                    g = hh * 2 + sub
                    bankk = pfb[2 + hh // 2]
                    pf = PF[2 + hh // 2]
                    ocol = (hh % 2) * 256
                    sl = slice(hh * 128 + sub * 64, hh * 128 + (sub + 1) * 64)
                    kw1 = {"tile_position": (0, 64)} if sub == 1 else {}
                    kw2 = {"tile_position": (64, 64)} if sub == 1 else {}
                    T("matmul", pf[r0:r1, ocol:ocol + 256], rhat_b[:, sl],
                      S_bf[:, hh * 256:(hh + 1) * 256], start=True, stop=False,
                      reads=(bRh, bSbf), writes=(bankk,), **kw1)
                    T("matmul", pf[r0:r1, ocol:ocol + 256], am_b[r0:r1, g * 64:(g + 1) * 64],
                      v_tm[r0:r1, hh * 256:(hh + 1) * 256], start=False, stop=True,
                      reads=(bAm, bVt), writes=(bankk,), **kw2)
                    sbank = pfb[4 + hh // 2]
                    spf = PF[4 + hh // 2]
                    kw3 = {"tile_position": (64, 0)} if sub == 1 else {}
                    T("matmul", spf[:, (hh % 2) * 256:(hh % 2) * 256 + 256],
                      khat_tm[r0:r1, hh * 128:(hh + 1) * 128],
                      v_tm[r0:r1, hh * 256:(hh + 1) * 256], start=True, stop=True,
                      reads=(bKht, bVt), writes=(sbank,), **kw3)
                for hh in range(NH):
                    pf = PF[2 + hh // 2]
                    bankk = pfb[2 + hh // 2]
                    ocol = (hh % 2) * 256
                    VE("scalar_tensor_tensor", o_sb[r0:r1, hh * 256:(hh + 1) * 256],
                       v_tm[r0:r1, hh * 256:(hh + 1) * 256], d_sb[r0:r1, hh:hh + 1],
                       pf[r0:r1, ocol:ocol + 256], ALU.mult, ALU.add,
                       reads=(bVt, bDsb, bankk), writes=(bOt,))
                for hh in range(NH):
                    g = hh * 2 + sub
                    sbank = pfb[4 + hh // 2]
                    spf = PF[4 + hh // 2]
                    VE("scalar_tensor_tensor", Snew[:, hh * 256:(hh + 1) * 256],
                       Scur[:, hh * 256:(hh + 1) * 256], eneg[:, g * 65 + 64:g * 65 + 65],
                       spf[:, (hh % 2) * 256:(hh % 2) * 256 + 256], ALU.mult, ALU.add,
                       reads=(bScur, bEneg, sbank), writes=(bSnew,))
                VE("tensor_copy", S_bf, Snew, reads=(bSnew,), writes=(bSbf,))
                sparity = 1 - sparity
            if dbg:
                SD("dma_start", out=dbg_am[i], in_=am_b, reads=(bAm,), writes=(NB("dbgd"),))
                SD("dma_start", out=dbg_kt[i], in_=kt_b, reads=(bKt,), writes=(NB("dbgd"),))
                SD("dma_start", out=dbg_rt[i], in_=rt_b, reads=(bRt,), writes=(NB("dbgd"),))
                SD("dma_start", out=dbg_kht[i], in_=khat_tm, reads=(bKht,), writes=(NB("dbgd"),))
                SD("dma_start", out=dbg_rh[i], in_=rhat_b, reads=(bRh,), writes=(NB("dbgd"),))
                SD("dma_start", out=dbg_ds[i], in_=d_sb, reads=(bDsb,), writes=(NB("dbgd"),))
                SD("dma_start", out=dbg_S[i], in_=S_bf, reads=(bSbf,), writes=(NB("dbgd"),))
                SD("dma_start", out=dbg_pc[i], in_=pcum, reads=(bPcum,), writes=(NB("dbgd"),))
            SD("dma_start", out=os_s[i * TILE:(i + 1) * TILE, :], in_=o_sb, reads=(bOt,), writes=(bOS,))
            SD("dma_start", out=gs_s[i * TILE:(i + 1) * TILE, :], in_=g_tm, reads=(bGt,), writes=(bGS,))

        GD("dma_start", out=agp_i, in_=Ss[sparity], reads=(bS[sparity],), writes=(bAGpi,))
        GD("collective_compute", "AllGather", ALU.bypass, [[0, 1], [2, 3], [4, 5], [6, 7]],
           [agp_i], [agp_o], reads=(bAGpi,), writes=(bAGpo,), ch="c")

        P.barrier()

        for w in range(7):
            if w < 3:
                src = wgn_in[w * 128:(w + 1) * 128, :]
                kk = H
            elif w < 6:
                src = wun_in[(w - 3) * 128:(w - 2) * 128, :]
                kk = H
            else:
                src = wdn_in
                kk = IFFN
            GD("dma_start", out=qnat[:, 0:kk], in_=src, writes=(bQn,))
            VE("reduce_max", cols[:, 36:37], qnat[:, 0:kk], axis=AX,
               apply_absolute_value=True, reads=(bQn,), writes=(bCols,))
            VE("reciprocal", cols[:, 37:38], cols[:, 36:37], reads=(bCols,), writes=(bCols,))
            VE("tensor_scalar", cols[:, 38:39], cols[:, 37:38], 127.0, None, ALU.mult,
               reads=(bCols,), writes=(bCols,))
            VE("tensor_scalar", cols[:, 39:40], cols[:, 36:37], 1.0 / 127.0, None, ALU.mult,
               reads=(bCols,), writes=(bCols,))
            T("transpose", PF[0][0:1, 0:128], cols[:, 39:40], identf,
              reads=(bCols, bConst), writes=(pfb[0],), force_drain=True)
            VE("tensor_scalar", qnat[:, 0:kk], qnat[:, 0:kk], cols[:, 38:39], CRND,
               ALU.mult, ALU.add, reads=(bQn, bCols), writes=(bQn,))
            VE("tensor_scalar", qint[:, 0:kk], qnat[:, 0:kk], CRND, None, ALU.subtract,
               reads=(bQn,), writes=(bQi,))
            for c in range(kk // 128):
                pbi = c % 2
                slot = (c // 2) % 4
                T("transpose", PB[pbi][:, slot * 128:(slot + 1) * 128],
                  qint[:, c * 128:(c + 1) * 128], identb, reads=(bQi, bConst), writes=(pbb[pbi],))
                SE("copy", qtrn[:, c * 128:(c + 1) * 128], PB[pbi][:, slot * 128:(slot + 1) * 128],
                   reads=(pbb[pbi],), writes=(bQt,))
            SE("copy", stage, PF[0][0:1, 0:128], reads=(pfb[0], pbb[0]), writes=(bStage,),
               force_drain=True)
            GD("dma_start", out=ags_i[w * 128:(w + 1) * 128], in_=stage,
               reads=(bStage,), writes=(bAGsi,))
            if w < 6:
                GD("dma_start", out=agw_i[:, w * 1024:(w + 1) * 1024], in_=qtrn[:, 0:1024],
                   reads=(bQt,), writes=(bAGwi,))
            else:
                GD("dma_start", out=agw_i[:, 6144:6144 + NGT * 128], in_=qtrn[:, 0:NGT * 128],
                   reads=(bQt,), writes=(bAGwi,))
        GD("collective_compute", "AllGather", ALU.bypass, [list(range(NCORES))],
           [agw_i], [agw_o], reads=(bAGwi,), writes=(bAGwo,), ch="c")
        GD("collective_compute", "AllGather", ALU.bypass, [list(range(NCORES))],
           [ags_i], [ags_o], reads=(bAGsi,), writes=(bAGso,), ch="c")
        GD("dma_start", out=qnat[:, 0:1024],
           in_=agp_o[0:1, :].rearrange("o (p f) -> (o p) f", p=128),
           reads=(bAGpo,), writes=(bQn,))
        VE("tensor_copy", Sin_bf, qnat[:, 0:1024], reads=(bQn,), writes=(bSin,))

        P.barrier()

        order = list(range(NCORR, NT)) + list(range(NCORR))
        for idx, i in enumerate(order):
            par = idx % 2
            ot, bOt3 = a3os[par], bA3o[par]
            gt_, bGt3 = a3gs[par], bA3g[par]
            xt3, bXt3 = a3xs[par], bA3x[par]
            SD("dma_start", out=ot, in_=os_s[i * TILE:(i + 1) * TILE, :], reads=(bOS,), writes=(bOt3,))
            SD("dma_start", out=gt_, in_=gs_s[i * TILE:(i + 1) * TILE, :], reads=(bGS,), writes=(bGt3,))
            SD("dma_start", out=xt3, in_=x_in[i * TILE:(i + 1) * TILE, :], writes=(bXt3,))
            if i < NCORR:
                rft, bRft = a3rfs[par], bA3rf[par]
                SD("dma_start", out=rft, in_=rfs_s[i * TILE:(i + 1) * TILE, :],
                   reads=(bRFS,), writes=(bRft,))
                for sub in range(2):
                    for hh in range(NH):
                        bankk = pfb[hh // 2]
                        pf = PF[hh // 2]
                        kw = {"tile_position": (0, 64)} if sub == 1 else {}
                        T("matmul", pf[sub * 64:(sub + 1) * 64, (hh % 2) * 256:(hh % 2) * 256 + 256],
                          rft[:, hh * 128 + sub * 64:hh * 128 + (sub + 1) * 64],
                          Sin_bf[:, hh * 256:(hh + 1) * 256], start=True, stop=True,
                          reads=(bRft, bSin), writes=(bankk,), **kw)
                for hh in range(NH):
                    bankk = pfb[hh // 2]
                    pf = PF[hh // 2]
                    VE("tensor_tensor", ot[:, hh * 256:(hh + 1) * 256],
                       ot[:, hh * 256:(hh + 1) * 256], pf[:, (hh % 2) * 256:(hh % 2) * 256 + 256],
                       ALU.add, reads=(bankk, bOt3), writes=(bOt3,))
            for hh in range(NH):
                VE("reduce_sum", cols[:, 40 + hh:41 + hh], ot[:, hh * 256:(hh + 1) * 256],
                   axis=AX, reads=(bOt3,), writes=(bCols,))
                SE("activation", a3sq[:, hh * 256:(hh + 1) * 256], ot[:, hh * 256:(hh + 1) * 256],
                   AF.Square, accum_out=cols[:, 44 + hh:45 + hh], reads=(bOt3,),
                   writes=(bA3sq, bCols))
            VE("tensor_scalar", cols[:, 40:44], cols[:, 40:44], 1.0 / DV, None, ALU.mult,
               reads=(bCols,), writes=(bCols,))
            VE("tensor_scalar", cols[:, 44:48], cols[:, 44:48], 1.0 / DV, None, ALU.mult,
               reads=(bCols,), writes=(bCols,))
            VE("tensor_scalar", cols[:, 36:40], cols[:, 40:44], -1.0, None, ALU.mult,
               reads=(bCols,), writes=(bCols,))
            for hh in range(NH):
                VE("scalar_tensor_tensor", cols[:, 32 + hh:33 + hh], cols[:, 40 + hh:41 + hh],
                   cols[:, 36 + hh:37 + hh], cols[:, 44 + hh:45 + hh], ALU.mult, ALU.add,
                   reads=(bCols,), writes=(bCols,))
            rsqrt_newton(cols[:, 28:32], cols[:, 32:36], 4, iters=10)
            VE("tensor_tensor", cols[:, 24:28], cols[:, 40:44], cols[:, 28:32], ALU.mult,
               reads=(bCols,), writes=(bCols,))
            VE("tensor_scalar", cols[:, 20:24], cols[:, 24:28], -1.0, None, ALU.mult,
               reads=(bCols,), writes=(bCols,))
            for hh in range(NH):
                SE("activation", xo_b[:, hh * 256:(hh + 1) * 256], ot[:, hh * 256:(hh + 1) * 256],
                   AF.Identity, scale=cols[:, 28 + hh:29 + hh], bias=cols[:, 20 + hh:21 + hh],
                   reads=(bOt3, bCols), writes=(bXo,))
            SE("activation", th_b, gt_, AF.Tanh, scale=0.5, reads=(bGt3,), writes=(bTh,))
            VE("tensor_scalar", th_b, th_b, 0.5, 0.5, ALU.mult, ALU.add,
               reads=(bTh,), writes=(bTh,))
            VE("tensor_tensor", sg2_b, th_b, gt_, ALU.mult, reads=(bTh, bGt3), writes=(bSg2,))
            VE("tensor_tensor", of_b, xo_b, gnwh_b, ALU.mult, reads=(bXo, bMod), writes=(bOf,))
            VE("tensor_tensor", of_b, of_b, gnbh_b, ALU.add, reads=(bOf, bMod), writes=(bOf,))
            VE("tensor_tensor", of_b, of_b, sg2_b, ALU.mult, reads=(bOf, bSg2), writes=(bOf,))
            for c in range(8):
                T("transpose", PB[0][:, c * 128:(c + 1) * 128], of_b[:, c * 128:(c + 1) * 128],
                  identb, reads=(bOf, bConst), writes=(pbb[0],))
            SE("copy", of_fm, PB[0][:, :], reads=(pbb[0],), writes=(bOffm,))
            for half in range(2):
                bankk = pfb[4 + half]
                pf = PF[4 + half]
                for kc in range(8):
                    T("matmul", pf[:, 0:512], of_fm[:, kc * 128:(kc + 1) * 128],
                      woT[:, kc * H + half * 512:kc * H + (half + 1) * 512],
                      start=(kc == 0), stop=(kc == 7), reads=(bOffm, bW), writes=(bankk,))
            for half in range(2):
                bankk = pfb[4 + half]
                pf = PF[4 + half]
                VE("tensor_tensor", t1_s[:, half * 512:(half + 1) * 512], pf[:, 0:512],
                   ga_b[:, half * 512:(half + 1) * 512], ALU.mult,
                   reads=(bankk, bMod), writes=(bT1,))
            VE("tensor_tensor", x2_t, t1_s, xt3, ALU.add, reads=(bT1, bXt3), writes=(bX2t,))
            ln_stats(x2_t, bX2t, cols[:, 18:19], cols[:, 19:20], a3sq, bA3sq)
            SE("activation", t1_s, x2_t, AF.Identity, scale=cols[:, 18:19], bias=cols[:, 19:20],
               reads=(bX2t, bCols), writes=(bT1,))
            VE("tensor_tensor", h2_t, t1_s, cb1_b, ALU.mult, reads=(bT1, bMod), writes=(bH2t,))
            VE("tensor_tensor", h2_t, h2_t, sb_b, ALU.add, reads=(bH2t, bMod), writes=(bH2t,))
            VE("reduce_max", cols[:, 17:18], h2_t, axis=AX, apply_absolute_value=True,
               reads=(bH2t,), writes=(bCols,))
            VE("tensor_tensor", runmax[:, 0:1], runmax[:, 0:1], cols[:, 17:18], ALU.max,
               reads=(bCols, bRmx), writes=(bRmx,))
            SD("dma_start", out=x2_s[i * TILE:(i + 1) * TILE, :], in_=x2_t,
               reads=(bX2t,), writes=(bX2S,))
            SD("dma_start", out=h2_s[i * TILE:(i + 1) * TILE, :], in_=h2_t,
               reads=(bH2t,), writes=(bH2S,))

        T("transpose", PF[0][0:1, 0:128], runmax[:, 0:1], identf, reads=(bRmx, bConst),
          writes=(pfb[0],), force_drain=True)
        SE("copy", stage, PF[0][0:1, 0:128], reads=(pfb[0],), writes=(bStage,), force_drain=True)
        VE("reduce_max", scrow[0:1, 14:15], stage, axis=AX, reads=(bStage,), writes=(bRows,))
        GD("dma_start", out=arh_i, in_=scrow[0:1, 14:15], reads=(bRows,), writes=(bARh,))
        GD("collective_compute", "AllReduce", ALU.max, [list(range(NCORES))],
           [arh_i], [arh_o], reads=(bARh,), writes=(bARho,), ch="c")

        P.barrier()

        for r in range(NCORES):
            for l in range(3):
                gt_i = 3 * r + l
                if gt_i >= NGT:
                    continue
                SD("dma_start",
                   out=wqg.rearrange("p (c f) -> p c f", c=8)[:, :, gt_i * 128:(gt_i + 1) * 128],
                   in_=agw_o[r, :, l * 1024:(l + 1) * 1024].rearrange("p (c j) -> p c j", c=8),
                   reads=(bAGwo,), writes=(bWq,))
                SD("dma_start",
                   out=wqu.rearrange("p (c f) -> p c f", c=8)[:, :, gt_i * 128:(gt_i + 1) * 128],
                   in_=agw_o[r, :, (3 + l) * 1024:(4 + l) * 1024].rearrange("p (c j) -> p c j", c=8),
                   reads=(bAGwo,), writes=(bWq,))
                SD("dma_start", out=sg_row[0:1, gt_i * 128:(gt_i + 1) * 128],
                   in_=ags_o[r, l * 128:(l + 1) * 128], reads=(bAGso,), writes=(bRows,))
                SD("dma_start", out=su_row[0:1, gt_i * 128:(gt_i + 1) * 128],
                   in_=ags_o[r, (3 + l) * 128:(4 + l) * 128], reads=(bAGso,), writes=(bRows,))
        SD("dma_start", out=cpx[0:1, 2:3], in_=arh_o, reads=(bARho,), writes=(bRows,))
        SE("activation", cpx[0:1, 4:5], cpx[0:1, 0:1], AF.Tanh, scale=0.5,
           reads=(bRows,), writes=(bRows,))
        VE("tensor_scalar", cpx[0:1, 5:6], cpx[0:1, 4:5], 0.5, 0.5, ALU.mult, ALU.add,
           reads=(bRows,), writes=(bRows,))
        VE("tensor_tensor", cpx[0:1, 6:7], cpx[0:1, 5:6], cpx[0:1, 2:3], ALU.mult,
           reads=(bRows,), writes=(bRows,))
        VE("tensor_scalar", cpx[0:1, 6:7], cpx[0:1, 6:7], 1.0 / 127.0, None, ALU.mult,
           reads=(bRows,), writes=(bRows,))
        VE("reciprocal", cpx[0:1, 7:8], cpx[0:1, 6:7], reads=(bRows,), writes=(bRows,))
        T("matmul", PF[0][:, 0:1], onesrow, cpx[0:1, 7:8], start=True, stop=True,
          reads=(bRows, bConst), writes=(pfb[0],))
        SE("copy", cols[:, 16:17], PF[0][:, 0:1], reads=(pfb[0],), writes=(bCols,))
        VE("tensor_scalar", sg_row, sg_row, cpx[0:1, 6:7], None, ALU.mult,
           reads=(bRows,), writes=(bRows,))
        VE("tensor_scalar", su_row, su_row, cpx[0:1, 6:7], None, ALU.mult,
           reads=(bRows,), writes=(bRows,))
        VE("tensor_scalar", su_row, su_row, 0.5, None, ALU.mult, reads=(bRows,), writes=(bRows,))
        for (row, dstb) in ((sg_row, sxg_b), (su_row, sxu_b)):
            for (s0, s1) in SLABS:
                T("matmul", PF[1][:, 0:s1 - s0], onesrow, row[0:1, s0:s1],
                  start=True, stop=True, reads=(bRows, bConst), writes=(pfb[1],))
                SE("copy", dstb[:, s0:s1], PF[1][:, 0:s1 - s0], reads=(pfb[1],), writes=(bSx,))

        SD("dma_start", out=bhs[0], in_=h2_s[0:TILE, :], reads=(bH2S,), writes=(bBh[0],))
        for i in range(NT):
            par = i % 2
            ht, bHt = bhs[par], bBh[par]
            tt, bTt = bts[par], bBt[par]
            if i + 1 < NT:
                SD("dma_start", out=bhs[(i + 1) % 2], in_=h2_s[(i + 1) * TILE:(i + 2) * TILE, :],
                   reads=(bH2S,), writes=(bBh[(i + 1) % 2],))
            VE("tensor_scalar", bq1, ht, cols[:, 16:17], CRND, ALU.mult, ALU.add,
               reads=(bHt, bCols), writes=(bBq1,))
            VE("tensor_scalar", bq1, bq1, CRND, 127.0, ALU.subtract, ALU.min,
               reads=(bBq1,), writes=(bBq1,))
            VE("tensor_scalar", bxq, bq1, -128.0, None, ALU.max, reads=(bBq1,), writes=(bBxq,))
            for c in range(8):
                T("transpose", PB[0][:, c * 128:(c + 1) * 128], bxq[:, c * 128:(c + 1) * 128],
                  identb, reads=(bBxq, bConst), writes=(pbb[0],))
            SE("copy", bxqf, PB[0][:, :], reads=(pbb[0],), writes=(bBxqf,))
            if dbg and i < 4:
                SD("dma_start", out=dbg_bxq[i], in_=bxq, reads=(bBxq,), writes=(NB("dbgd"),))
            if dbg and i == 0:
                SD("dma_start", out=dbg_wqg, in_=wqg, reads=(bWq,), writes=(NB("dbgd"),))
            for si_, (s0, s1) in enumerate(SLABS):
                n = s1 - s0
                for kc in range(8):
                    T("matmul", PF[2][:, 0:n], bxqf[:, kc * 128:(kc + 1) * 128],
                      wqg.rearrange("p (c f) -> p c f", c=8)[:, kc, s0:s1],
                      start=(kc == 0), stop=(kc == 7), reads=(bBxqf, bWq), writes=(pfb[2],))
                for kc in range(8):
                    T("matmul", PF[3][:, 0:n], bxqf[:, kc * 128:(kc + 1) * 128],
                      wqu.rearrange("p (c f) -> p c f", c=8)[:, kc, s0:s1],
                      start=(kc == 0), stop=(kc == 7), reads=(bBxqf, bWq), writes=(pfb[3],))
                VE("tensor_tensor", bgr[:, 0:n], PF[2][:, 0:n], sxg_b[:, s0:s1], ALU.mult,
                   reads=(pfb[2], bSx), writes=(bBgr,))
                VE("tensor_tensor", bur[:, 0:n], PF[3][:, 0:n], sxu_b[:, s0:s1], ALU.mult,
                   reads=(pfb[3], bSx), writes=(bBur,))
                SE("activation", bth[:, 0:n], bgr[:, 0:n], AF.Tanh, scale=0.5,
                   reads=(bBgr,), writes=(bBth,))
                VE("tensor_tensor", bur[:, 0:n], bur[:, 0:n], bgr[:, 0:n], ALU.mult,
                   reads=(bBur, bBgr), writes=(bBur,))
                VE("scalar_tensor_tensor", tt[:, s0:s1], bth[:, 0:n], 1.0, bur[:, 0:n],
                   ALU.add, ALU.mult, reads=(bBth, bBur),
                   writes=(bTt, bRows) if i < 2 and s0 == 0 else (bTt,))
                VE("reduce_max", runmax[:, 2 + si_:3 + si_], tt[:, s0:s1], axis=AX,
                   apply_absolute_value=True, reads=(bTt,), writes=(bRmx,))
            for si_ in range(6):
                VE("tensor_tensor", runmax[:, 1:2], runmax[:, 1:2], runmax[:, 2 + si_:3 + si_],
                   ALU.max, reads=(bRmx,), writes=(bRmx,))
            SD("dma_start", out=t_s[i * TILE:(i + 1) * TILE, :], in_=tt, reads=(bTt,), writes=(bTS,))
        T("transpose", PF[0][0:1, 0:128], runmax[:, 1:2], identf, reads=(bRmx, bConst),
          writes=(pfb[0],), force_drain=True)
        SE("copy", stage, PF[0][0:1, 0:128], reads=(pfb[0],), writes=(bStage,), force_drain=True)
        VE("reduce_max", scrow[0:1, 15:16], stage, axis=AX, reads=(bStage,), writes=(bRows,))
        GD("dma_start", out=art_i, in_=scrow[0:1, 15:16], reads=(bRows,), writes=(bARt,))
        GD("collective_compute", "AllReduce", ALU.max, [list(range(NCORES))],
           [art_i], [art_o], reads=(bARt,), writes=(bARto,), ch="c")

        P.barrier()

        for r in range(NCORES):
            SD("dma_start",
               out=wqd.rearrange("p (c f) -> p c f", c=NGT)[:, :, r * 128:(r + 1) * 128],
               in_=agw_o[r, :, 6144:6144 + NGT * 128].rearrange("p (c j) -> p c j", c=NGT),
               reads=(bAGwo,), writes=(bWq,))
        for r in range(NCORES):
            SD("dma_start", out=sd_row[0:1, r * 128:(r + 1) * 128],
               in_=ags_o[r, 6 * 128:7 * 128], reads=(bAGso,), writes=(bRows,))
        SD("dma_start", out=cpx[0:1, 3:4], in_=art_o, reads=(bARto,), writes=(bRows,))
        SE("activation", cpx[0:1, 8:9], cpx[0:1, 1:2], AF.Tanh, scale=0.5, reads=(bRows,), writes=(bRows,))
        VE("tensor_scalar", cpx[0:1, 9:10], cpx[0:1, 8:9], 0.5, 0.5, ALU.mult, ALU.add,
           reads=(bRows,), writes=(bRows,))
        VE("tensor_tensor", cpx[0:1, 10:11], cpx[0:1, 9:10], cpx[0:1, 3:4], ALU.mult,
           reads=(bRows,), writes=(bRows,))
        VE("tensor_scalar", cpx[0:1, 10:11], cpx[0:1, 10:11], 1.0 / 127.0, None, ALU.mult,
           reads=(bRows,), writes=(bRows,))
        VE("reciprocal", cpx[0:1, 11:12], cpx[0:1, 10:11], reads=(bRows,), writes=(bRows,))
        T("matmul", PF[0][:, 0:1], onesrow, cpx[0:1, 11:12], start=True, stop=True,
          reads=(bRows, bConst), writes=(pfb[0],))
        SE("copy", cols[:, 15:16], PF[0][:, 0:1], reads=(pfb[0],), writes=(bCols,))
        VE("tensor_scalar", sd_row, sd_row, cpx[0:1, 10:11], None, ALU.mult,
           reads=(bRows,), writes=(bRows,))
        for sl in range(2):
            T("matmul", PF[1][:, 0:512], onesrow, sd_row[0:1, sl * 512:(sl + 1) * 512],
              start=True, stop=True, reads=(bRows, bConst), writes=(pfb[1],))
            SE("copy", co_t[:, sl * 512:(sl + 1) * 512], PF[1][:, 0:512],
               reads=(pfb[1],), writes=(bCo,))
        VE("tensor_tensor", sdtgb_b, co_t, gb_b, ALU.mult, reads=(bCo, bMod), writes=(bSdt,))

        SD("dma_start", out=cts[0], in_=t_s[0:TILE, :], reads=(bTS,), writes=(bCt[0], bRows))
        SD("dma_start", out=cxs[0], in_=x2_s[0:TILE, :], reads=(bX2S,), writes=(bCx[0],))
        for i in range(NT):
            par = i % 2
            tt, bTt = cts[par], bCt[par]
            xx, bXx = cxs[par], bCx[par]
            if i + 1 < NT:
                SD("dma_start", out=cts[(i + 1) % 2], in_=t_s[(i + 1) * TILE:(i + 2) * TILE, :],
                   reads=(bTS,), writes=(bCt[(i + 1) % 2],))
                SD("dma_start", out=cxs[(i + 1) % 2], in_=x2_s[(i + 1) * TILE:(i + 2) * TILE, :],
                   reads=(bX2S,), writes=(bCx[(i + 1) % 2],))
            VE("tensor_scalar", cq1, tt, cols[:, 15:16], CRND, ALU.mult, ALU.add,
               reads=(bTt, bCols), writes=(bCq1,))
            VE("tensor_scalar", cq1, cq1, CRND, 127.0, ALU.subtract, ALU.min,
               reads=(bCq1,), writes=(bCq1,))
            VE("tensor_scalar", chq, cq1, -128.0, None, ALU.max, reads=(bCq1,), writes=(bChq,))
            for c in range(NGT):
                pbi = c % 2
                slot = (c // 2) % 4
                T("transpose", PB[pbi][:, slot * 128:(slot + 1) * 128],
                  chq[:, c * 128:(c + 1) * 128], identb, reads=(bChq, bConst), writes=(pbb[pbi],))
                SE("copy", chqf[:, c * 128:(c + 1) * 128], PB[pbi][:, slot * 128:(slot + 1) * 128],
                   reads=(pbb[pbi],), writes=(bChqf,))
            for half in range(2):
                bankk = pfb[2 + half]
                pf = PF[2 + half]
                for kc in range(NGT):
                    T("matmul", pf[:, 0:512], chqf[:, kc * 128:(kc + 1) * 128],
                      wqd.rearrange("p (c f) -> p c f", c=NGT)[:, kc, half * 512:(half + 1) * 512],
                      start=(kc == 0), stop=(kc == NGT - 1), reads=(bChqf, bWq), writes=(bankk,))
                VE("tensor_tensor", co_t[:, half * 512:(half + 1) * 512], pf[:, 0:512],
                   sdtgb_b[:, half * 512:(half + 1) * 512], ALU.mult,
                   reads=(bankk, bSdt), writes=(bCo,))
            if dbg and i < 4:
                SD("dma_start", out=dbg_chq[i], in_=chq, reads=(bChq,), writes=(NB("dbgd"),))
            if dbg and i == 0:
                SD("dma_start", out=dbg_wqd, in_=wqd, reads=(bWq,), writes=(NB("dbgd"),))
                SD("dma_start", out=dbg_sdt, in_=sdtgb_b, reads=(bSdt,), writes=(NB("dbgd"),))
                SD("dma_start", out=dbg_cpx, in_=cpx, reads=(bRows,), writes=(NB("dbgd"),))
            VE("tensor_tensor", co_t, co_t, xx, ALU.add, reads=(bCo, bXx), writes=(bCo,))
            SD("dma_start", out=out[i * TILE:(i + 1) * TILE, :], in_=co_t,
               reads=(bCo,), writes=(bOutD,))
        for e in ("v", "s", "t", "g"):
            P._wait(e, bOutD.wr)
        P.final_wait("d")

    with ExitStack() as ctx:
        e = ctx.enter_context
        pool_t = e(nc.sbuf_tensor("pool", [128, POOLB], mybir.dt.uint8))
        pool = pool_t
        PF = [e(nc.psum_tensor(f"pf{i}", [128, 512], F32)) for i in range(6)]
        PB = [e(nc.psum_tensor(f"pb{i}", [128, 1024], BF16)) for i in range(2)]
        sems = {n: e(nc.semaphore(f"sem_{n}"))
                for n in ("v", "s", "t", "d", "g", "c", "de", "ge")}
        sem_pool = [e(nc.semaphore(f"sp{i}")) for i in range(40)]
        block = e(nc.Block())

        def mk_chans():
            ch = {n: CH(sems[n], n) for n in
                  ("v", "s", "t", "d", "g", "c", "de", "ge")}
            ch["de"].owner = "none"
            ch["ge"].owner = "none"
            return ch

        @block.tensor
        def _(eng):
            program(K("t", {"t": eng}, mk_chans(), sem_pool), pool, PF, PB)

        @block.vector
        def _(eng):
            program(K("v", {"v": eng}, mk_chans(), sem_pool), pool, PF, PB)

        @block.scalar
        def _(eng):
            program(K("s", {"s": eng}, mk_chans(), sem_pool), pool, PF, PB)

        @block.sync
        def _(eng):
            program(K("d", {"d": eng}, mk_chans(), sem_pool), pool, PF, PB)

        @block.gpsimd
        def _(eng):
            program(K("g", {"g": eng}, mk_chans(), sem_pool), pool, PF, PB)

    return nc


_NC_CACHE = {}


def _get_nc():
    if "nc" not in _NC_CACHE:
        _NC_CACHE["nc"] = build_nc(dbg=_NC_CACHE.get("dbg", False))
    return _NC_CACHE["nc"]


_RES_STASH = {}


def kernel(**inputs):
    f32 = np.float32
    x = np.asarray(inputs["L0_attn_hidden_states"], dtype=f32)
    W_x0 = np.asarray(inputs["L0_attn_W_x0"], f32)
    W_x2 = np.asarray(inputs["L0_attn_W_x2"], f32)
    x_bias = np.asarray(inputs["L0_attn_x_bias"], f32)
    W_r = np.asarray(inputs["L0_attn_W_r"], f32)
    W_w1 = np.asarray(inputs["L0_attn_W_w1"], f32)
    W_w2 = np.asarray(inputs["L0_attn_W_w2"], f32)
    b_w2 = np.asarray(inputs["L0_attn_b_w2"], f32)
    W_k = np.asarray(inputs["L0_attn_W_k"], f32)
    W_v = np.asarray(inputs["L0_attn_W_v"], f32)
    W_g = np.asarray(inputs["L0_attn_W_g"], f32)
    W_o = np.asarray(inputs["L0_attn_W_o"], f32)
    mu0 = np.asarray(inputs["L0_attn_mu0"], f32)
    bonus = np.asarray(inputs["L0_attn_bonus"], f32)
    gnw = np.asarray(inputs["L0_attn_gn_weight"], f32)
    gnb = np.asarray(inputs["L0_attn_gn_bias"], f32)
    clip_x = np.asarray(inputs["L0_ffn_clip_x"], f32)
    clip_d = np.asarray(inputs["L0_ffn_clip_down"], f32)
    w_gate = np.asarray(inputs["L0_ffn_w_gate"], f32)
    w_up = np.asarray(inputs["L0_ffn_w_up"], f32)
    w_down = np.asarray(inputs["L0_ffn_w_down"], f32)
    L0_c = np.asarray(inputs["L0_c"], f32)
    wmod = np.asarray(inputs["L0_wmod"], f32)

    def kchunk(wT):
        Kd, Fd = wT.shape
        return np.ascontiguousarray(
            wT.reshape(Kd // 128, 128, Fd).transpose(1, 0, 2).reshape(128, -1))

    wx0T = kchunk(np.ascontiguousarray(W_x0.T))
    wx2T = np.ascontiguousarray(
        np.concatenate([W_x2.reshape(H, 5, 32).transpose(1, 2, 0),
                        x_bias[:, None, :]], axis=1).transpose(1, 0, 2).reshape(33, 5 * H))
    wrT = kchunk(np.ascontiguousarray(W_r.T))
    ww1T = kchunk(np.ascontiguousarray(W_w1.T))
    ww2T = np.ascontiguousarray(np.concatenate([W_w2.T, b_w2[None, :]], axis=0))
    wkT = kchunk(np.ascontiguousarray(W_k.T))
    wvT = kchunk(np.ascontiguousarray(W_v.T))
    wgT = kchunk(np.ascontiguousarray(W_g.T))
    woT = kchunk(np.ascontiguousarray(W_o.T))
    mu0fm = np.ascontiguousarray(mu0.reshape(8, 128).T)
    bonusT = np.ascontiguousarray(bonus.T)
    identb = np.eye(128, dtype=ml_dtypes.bfloat16)
    identf = np.eye(128, dtype=np.float32)
    base = np.tril(np.ones((64, 64), dtype=f32), -1).T
    m128 = np.zeros((128, 512), dtype=f32)
    for g in range(8):
        m128[0:64, g * 64:(g + 1) * 64] = base
        m128[64:128, g * 64:(g + 1) * 64] = base
    m128 = m128.astype(np.uint8)
    onesrow = np.ones((1, 128), dtype=f32)
    wgp = np.concatenate([w_gate, np.ones((24 * 128 - IFFN, H), f32)], axis=0)
    wup = np.concatenate([w_up, np.ones((24 * 128 - IFFN, H), f32)], axis=0)

    nc = _get_nc()
    in_maps = []
    for c in range(NCORES):
        b, hf = c // 2, c % 2
        t0 = hf * TCORE
        xprev = np.zeros((1, H), f32) if hf == 0 else np.ascontiguousarray(x[b, t0 - 1:t0, :])
        in_maps.append({
            "x": np.ascontiguousarray(x[b, t0:t0 + TCORE, :]),
            "xprev": xprev,
            "cvec": np.ascontiguousarray(L0_c[b:b + 1, :]),
            "wmod": wmod,
            "gnw": np.ascontiguousarray(np.broadcast_to(gnw[None, :], (128, VD))),
            "gnb": np.ascontiguousarray(np.broadcast_to(gnb[None, :], (128, VD))),
            "mu0": mu0fm, "bonus": bonusT,
            "pmask": np.full((128, 1), float(hf), f32),
            "cmb": np.full((128, 1), 0.0 if hf == 1 else -1e9, f32),
            "clipx": clip_x.reshape(1, 1), "clipd": clip_d.reshape(1, 1),
            "identb": identb, "identf": identf, "mask128": m128, "onesrow": onesrow,
            "wx0T": wx0T, "wx2T": wx2T, "wrT": wrT, "ww1T": ww1T, "ww2T": ww2T,
            "wkT": wkT, "wvT": wvT, "wgT": wgT, "woT": woT,
            "wg_nat": np.ascontiguousarray(wgp[384 * c:384 * (c + 1)]),
            "wu_nat": np.ascontiguousarray(wup[384 * c:384 * (c + 1)]),
            "wd_nat": np.ascontiguousarray(w_down[128 * c:128 * (c + 1)]),
        })
    res = run_bass_kernel_spmd(nc, in_maps, core_ids=list(range(NCORES)))
    _RES_STASH["res"] = res
    outp = np.empty((B, TFULL, H), f32)
    for c in range(NCORES):
        b, hf = c // 2, c % 2
        outp[b, hf * TCORE:(hf + 1) * TCORE, :] = res.results[c]["out"]
    return outp


# version 06
# speedup vs baseline: 1.0011x; 1.0011x over previous
import numpy as np
import ml_dtypes
from contextlib import ExitStack

import concourse.bass as bass
import concourse.mybir as mybir
from concourse.bass_utils import run_bass_kernel_spmd

F32 = mybir.dt.float32
BF16 = mybir.dt.bfloat16
I32 = mybir.dt.int32
AF = mybir.ActivationFunctionType
ALU = mybir.AluOpType
AX = mybir.AxisListType.X

B, TFULL, H = 4, 8192, 1024
NCORES = 8
TCORE = 4096
TILE = 128
NT = TCORE // TILE
NCORR = 8
NH, DK, DV = 4, 128, 256
KD, VD = 512, 1024
IFFN = 2816
NGT = 22
CRND = 12582912.0
RSMAGIC = 1597463007
EPS = 1e-5
POOLB = 209920
SLABS = [(0, 512), (512, 1024), (1024, 1536), (1536, 2048), (2048, 2560), (2560, 2816)]


class CH:
    def __init__(self, sem, owner):
        self.sem = sem
        self.owner = owner
        self.n = 0


class Buf:
    __slots__ = ("name", "wr", "rd")

    def __init__(self, name):
        self.name = name
        self.wr = None
        self.rd = {}


class K:
    def __init__(self, active, handles, chans, sem_pool):
        self.active = active
        self.h = handles
        self.ch = chans
        self.sem_pool = sem_pool
        self.sem_next = 0
        self.buf_ch = {}
        self.qchans = {"d": [chans["d"]], "g": [chans["g"]]}
        self.waited = {e: {} for e in ("v", "s", "t", "d", "g")}

    def _buf_chan(self, name, qeng):
        if name not in self.buf_ch:
            assert self.sem_next < len(self.sem_pool), "sem pool exhausted"
            c_ = CH(self.sem_pool[self.sem_next], "none")
            self.sem_next += 1
            self.buf_ch[name] = c_
        c_ = self.buf_ch[name]
        if c_ not in self.qchans[qeng]:
            self.qchans[qeng].append(c_)
        return c_

    def _wait(self, eng, ticket):
        if ticket is None:
            return
        ch, val = ticket
        if ch.owner == eng and eng in ("v", "s", "t"):
            return
        w = self.waited[eng]
        if w.get(ch, 0) >= val:
            return
        w[ch] = val
        if self.active == eng:
            self.h[eng].wait_ge(ch.sem, val)

    def op(self, eng, f, *args, reads=(), writes=(), ch=None, force_drain=False, **kw):
        for b in reads:
            self._wait(eng, b.wr)
        for b in writes:
            self._wait(eng, b.wr)
            for c_, val in b.rd.items():
                self._wait(eng, (c_, val))
        chan = None
        if f == "dma_start" and ch is None and writes:
            try:
                if "DRam" not in type(kw["out"].tensor).__name__:
                    chan = self._buf_chan(writes[0].name, eng)
            except Exception:
                chan = None
        if chan is None:
            chan = self.ch[ch or eng]
        inc = 16 if f == "dma_start" else 1
        if self.active == eng:
            inst = getattr(self.h[eng], f)(*args, **kw)
            small = False
            if eng in ("v", "s", "t") and f != "dma_start":
                try:
                    small = args[0].free_size < 64
                except Exception:
                    small = True
            if small or force_drain:
                tail = self.h[eng].drain()
                tail.then_inc(chan.sem, inc)
            else:
                inst.then_inc(chan.sem, inc)
        chan.n += inc
        t = (chan, chan.n)
        for b in writes:
            b.wr = t
            b.rd = {}
        for b in reads:
            if b.rd.get(chan, 0) < chan.n:
                b.rd[chan] = chan.n
        return t

    def barrier(self):
        for qe, ech in (("d", "de"), ("g", "ge")):
            if self.active == qe:
                for c_ in self.qchans[qe]:
                    if c_.n > self.waited[qe].get(c_, 0):
                        self.waited[qe][c_] = c_.n
                        self.h[qe].wait_ge(c_.sem, c_.n)
                self.h[qe].sem_inc(self.ch[ech].sem, 1)
            self.ch[ech].n += 1
        for e in ("v", "s", "t", "d", "g"):
            for cn, c_ in self.ch.items():
                if cn in ("d", "g", "dl", "gl"):
                    continue
                self._wait(e, (c_, c_.n))

    def final_wait(self, eng):
        c_ = self.ch[eng]
        if self.active == eng:
            self.h[eng].wait_ge(c_.sem, c_.n)


def build_nc(dbg=False):
    nc = bass.Bass("TRN2", target_bir_lowering=False)
    d = {}

    def din(name, shape, dtype=F32):
        d[name] = nc.dram_tensor(name, list(shape), dtype, kind="ExternalInput").ap()
        return d[name]

    x_in = din("x", (TCORE, H))
    xprev_in = din("xprev", (1, H))
    cvec_in = din("cvec", (1, H))
    wmod_in = din("wmod", (H, 6 * H))
    gnw_in = din("gnw", (128, VD))
    gnb_in = din("gnb", (128, VD))
    mu0_in = din("mu0", (128, 8))
    bonus_in = din("bonus", (128, NH))
    pmask_in = din("pmask", (128, 1))
    cmb_in = din("cmb", (128, 1))
    clipx_in = din("clipx", (1, 1))
    clipd_in = din("clipd", (1, 1))
    identb_in = din("identb", (128, 128), BF16)
    identf_in = din("identf", (128, 128))
    mask128_in = din("mask128", (128, 512), mybir.dt.uint8)
    onesrow_in = din("onesrow", (1, 128))
    wx0_in = din("wx0T", (128, 8 * 160))
    wx2_in = din("wx2T", (33, 5 * H))
    wr_in = din("wrT", (128, 8 * KD))
    ww1_in = din("ww1T", (128, 8 * 64))
    ww2_in = din("ww2T", (65, KD))
    wk_in = din("wkT", (128, 8 * KD))
    wv_in = din("wvT", (128, 8 * VD))
    wg_in = din("wgT", (128, 8 * VD))
    wo_in = din("woT", (128, 8 * H))
    wgn_in = din("wg_nat", (384, H))
    wun_in = din("wu_nat", (384, H))
    wdn_in = din("wd_nat", (128, IFFN))

    dk = dict(kind="ExternalOutput") if dbg else {}
    out = nc.dram_tensor("out", [TCORE, H], F32, kind="ExternalOutput").ap()
    os_s = nc.dram_tensor("os_s", [TCORE, H], F32, **dk).ap()
    gs_s = nc.dram_tensor("gs_s", [TCORE, H], BF16, **dk).ap()
    rfs_s = nc.dram_tensor("rfs_s", [NCORR * TILE, KD], BF16, **dk).ap()
    x2_s = nc.dram_tensor("x2_s", [TCORE, H], F32, **dk).ap()
    h2_s = nc.dram_tensor("h2_s", [TCORE, H], F32, **dk).ap()
    t_s = nc.dram_tensor("t_s", [TCORE, IFFN], F32, **dk).ap()
    agp_i = nc.dram_tensor("agp_i", [NH * DK * DV], F32).ap()
    agp_o = nc.dram_tensor("agp_o", [2, NH * DK * DV], F32).ap()
    WQB = 6 * 1024 + NGT * 128
    agw_i = nc.dram_tensor("agw_i", [128, WQB], BF16).ap()
    agw_o = nc.dram_tensor("agw_o", [NCORES, 128, WQB], BF16).ap()
    ags_i = nc.dram_tensor("ags_i", [7 * 128], F32).ap()
    ags_o = nc.dram_tensor("ags_o", [NCORES, 7 * 128], F32).ap()
    if dbg:
        dbg_h = nc.dram_tensor("dbg_h", [NT, 128, H], BF16, kind="ExternalOutput").ap()
        dbg_xp = nc.dram_tensor("dbg_xp", [NT, 33, 640], BF16, kind="ExternalOutput").ap()
        dbg_xi5 = nc.dram_tensor("dbg_xi5", [NT, 128, H], BF16, kind="ExternalOutput").ap()
        dbg_pw = nc.dram_tensor("dbg_pw", [NT, 128, 512], F32, kind="ExternalOutput").ap()
        dbg_r = nc.dram_tensor("dbg_r", [NT, 128, 512], BF16, kind="ExternalOutput").ap()
        dbg_v = nc.dram_tensor("dbg_v", [NT, 128, H], BF16, kind="ExternalOutput").ap()
        dbg_xt = nc.dram_tensor("dbg_xt", [NT, 128, H], F32, kind="ExternalOutput").ap()
        dbg_cols = nc.dram_tensor("dbg_cols", [NT, 128, 48], F32, kind="ExternalOutput").ap()
        dbg_xh = nc.dram_tensor("dbg_xh", [NT, 128, H], BF16, kind="ExternalOutput").ap()
        dbg_am = nc.dram_tensor("dbg_am", [NT, 128, 512], BF16, kind="ExternalOutput").ap()
        dbg_kt = nc.dram_tensor("dbg_kt", [NT, 128, 512], BF16, kind="ExternalOutput").ap()
        dbg_rt = nc.dram_tensor("dbg_rt", [NT, 128, 512], BF16, kind="ExternalOutput").ap()
        dbg_kht = nc.dram_tensor("dbg_kht", [NT, 128, 512], BF16, kind="ExternalOutput").ap()
        dbg_rh = nc.dram_tensor("dbg_rh", [NT, 128, 512], BF16, kind="ExternalOutput").ap()
        dbg_ds = nc.dram_tensor("dbg_ds", [NT, 128, 16], F32, kind="ExternalOutput").ap()
        dbg_S = nc.dram_tensor("dbg_S", [NT, 128, H], BF16, kind="ExternalOutput").ap()
        dbg_pc = nc.dram_tensor("dbg_pc", [NT, 128, 520], F32, kind="ExternalOutput").ap()
        dbg_chq = nc.dram_tensor("dbg_chq", [4, 128, IFFN], BF16, kind="ExternalOutput").ap()
        dbg_wqd = nc.dram_tensor("dbg_wqd", [128, NGT * H], BF16, kind="ExternalOutput").ap()
        dbg_sdt = nc.dram_tensor("dbg_sdt", [128, H], BF16, kind="ExternalOutput").ap()
        dbg_cpx = nc.dram_tensor("dbg_cpx", [1, 16], F32, kind="ExternalOutput").ap()
        dbg_bxq = nc.dram_tensor("dbg_bxq", [4, 128, H], BF16, kind="ExternalOutput").ap()
        dbg_wqg = nc.dram_tensor("dbg_wqg", [128, 8 * IFFN], BF16, kind="ExternalOutput").ap()
        dbg_mod = nc.dram_tensor("dbg_mod", [6, 128, H], BF16, kind="ExternalOutput").ap()
        dbg_fm = nc.dram_tensor("dbg_fm", [128, 16], F32, kind="ExternalOutput").ap()
    arh_i = nc.dram_tensor("arh_i", [1], F32).ap()
    arh_o = nc.dram_tensor("arh_o", [1], F32).ap()
    art_i = nc.dram_tensor("art_i", [1], F32).ap()
    art_o = nc.dram_tensor("art_o", [1], F32).ap()

    def program(P, pool, PF, PB):
        def V(off, parts, n, dt):
            esz = 4 if dt in (F32, I32) else (1 if dt == mybir.dt.uint8 else 2)
            assert off + n * esz <= POOLB
            return pool[0:parts, off:off + n * esz].bitcast(dt)

        class Reg:
            def __init__(self, base):
                self.off = base

            def a(self, nbytes, align=64):
                o = (self.off + align - 1) // align * align
                self.off = o + nbytes
                assert self.off <= POOLB, f"pool overflow at {self.off}"
                return o

        pr = Reg(0)
        identb = V(pr.a(256), 128, 128, BF16)
        identf = V(pr.a(512), 128, 128, F32)
        mask128 = V(pr.a(512), 128, 512, mybir.dt.uint8)
        zeros64 = V(pr.a(256), 128, 64, F32)
        zb512 = V(pr.a(1024), 128, 512, BF16)
        ones128 = V(pr.a(512), 128, 128, F32)
        onesrow = V(pr.a(512), 1, 128, F32)
        mu0 = V(pr.a(32), 128, 8, F32)
        u_bf = V(pr.a(8), 128, NH, BF16)
        pmaskc = V(pr.a(4), 128, 1, F32)
        cmbc = V(pr.a(4), 128, 1, F32)
        sa_fm = V(pr.a(32), 128, 8, F32)
        ca1_fm = V(pr.a(32), 128, 8, F32)
        ga_b = V(pr.a(H * 2), 128, H, BF16)
        gb_b = V(pr.a(H * 2), 128, H, BF16)
        cb1_b = V(pr.a(H * 2), 128, H, BF16)
        sb_b = V(pr.a(H * 2), 128, H, BF16)
        gnwh_b = V(pr.a(H * 2), 128, H, BF16)
        gnbh_b = V(pr.a(H * 2), 128, H, BF16)
        oS0 = pr.a(H * 4); S0 = V(oS0, 128, H, F32)
        oS1 = pr.a(H * 4); S1 = V(oS1, 128, H, F32)
        S_bf = V(pr.a(H * 2), 128, H, BF16)
        Sin_bf = V(pr.a(H * 2), 128, H, BF16)
        pcum = V(pr.a(520 * 4), 128, 520, F32)
        eneg = V(pr.a(520 * 4), 128, 520, F32)
        negrun = V(pr.a(16), 128, 4, F32)
        prevcol = V(pr.a(16), 128, 8, BF16)
        ost = pr.a(256)
        stats = V(ost, 128, 64, F32)
        statsi = V(ost, 128, 64, I32)
        cols = V(pr.a(192), 128, 48, F32)
        runmax = V(pr.a(32), 128, 8, F32)
        osc = pr.a(64)
        scrow = V(osc, 1, 16, F32)
        scrowi = V(osc, 1, 16, I32)
        stage = V(pr.a(512), 1, 128, F32)
        cpx = V(pr.a(64), 1, 16, F32)

        WBASE = (pr.off + 63) // 64 * 64
        wa = Reg(WBASE)
        wx0 = V(wa.a(8 * 160 * 2), 128, 8 * 160, BF16)
        wx2 = V(wa.a(5 * H * 2), 33, 5 * H, BF16)
        wrT = V(wa.a(8 * KD * 2), 128, 8 * KD, BF16)
        ww1 = V(wa.a(8 * 64 * 2), 128, 8 * 64, BF16)
        ww2 = V(wa.a(KD * 2), 65, KD, BF16)
        wkT = V(wa.a(8 * KD * 2), 128, 8 * KD, BF16)
        wvT = V(wa.a(8 * VD * 2), 128, 8 * VD, BF16)
        wgT = V(wa.a(8 * VD * 2), 128, 8 * VD, BF16)
        assert wa.off <= WBASE + 90112, wa.off
        woT = V(WBASE + 90112, 128, 8 * H, BF16)
        wqg = V(WBASE, 128, 8 * IFFN, BF16)
        wqu = V(WBASE + 45056, 128, 8 * IFFN, BF16)
        wqd = V(WBASE, 128, NGT * H, BF16)
        WEND = WBASE + 90112 + 16384

        aa = Reg(WEND + 64)
        ABASE = aa.off
        oxt0 = aa.a(H * 4); xt0 = V(oxt0, 128, H, F32)
        oxt1 = aa.a(H * 4); xt1 = V(oxt1, 128, H, F32)
        xts = (xt0, xt1)
        osq = aa.a(H * 4); sqbuf = V(osq, 128, H, F32)
        oxh = aa.a(H * 2); xhat = V(oxh, 128, H, BF16)
        ohf = aa.a(H * 2); h_fm = V(ohf, 128, H, BF16)
        odl = aa.a(H * 2); delta = V(odl, 128, H, BF16)
        omu0 = aa.a(H * 2); omu1 = aa.a(H * 2)
        mu_fm = (V(omu0, 128, H, BF16), V(omu1, 128, H, BF16))
        oxi = [aa.a(H * 2) for _ in range(6)]
        xi = [V(o, 128, H, BF16) for o in oxi]
        xp_sb = V(aa.a(640 * 2), 33, 640, BF16)
        w1_sb = V(aa.a(128 * 2), 65, 128, BF16)
        pw = V(aa.a(512 * 4), 128, 512, F32)
        r_cm = V(aa.a(512 * 2), 128, 512, BF16)
        k_cm = V(aa.a(512 * 2), 128, 512, BF16)
        facR = V(aa.a(512 * 2), 128, 512, BF16)
        facK = V(aa.a(512 * 2), 128, 512, BF16)
        rt_b = V(aa.a(512 * 2), 128, 512, BF16)
        kt_b = V(aa.a(512 * 2), 128, 512, BF16)
        khc_b = V(aa.a(512 * 2), 128, 512, BF16)
        khat_tm = V(aa.a(512 * 2), 128, 512, BF16)
        rhat_b = V(aa.a(512 * 2), 128, 512, BF16)
        rf_b = V(aa.a(512 * 2), 128, 512, BF16)
        rk_b = V(aa.a(512 * 2), 128, 512, BF16)
        am_b = V(aa.a(512 * 2), 128, 512, BF16)
        v_tm = V(aa.a(H * 2), 128, H, BF16)
        g_tm = V(aa.a(H * 2), 128, H, BF16)
        o_sb = V(aa.a(H * 4), 128, H, F32)
        d_sb = V(aa.a(64), 128, 16, F32)
        modt = (V(oxt0, 128, 512, F32), V(oxt1, 128, 512, F32))
        srep = V(osq, 128, 1024, F32)
        crow = V(ohf, 1, H, F32)
        xprow = V(omu0, 1, H, F32)
        mscrA = V(oxi[0], 128, H, F32)
        mscrB = V(oxi[2], 128, H, F32)
        mscrC = V(oxi[4], 128, H, F32)
        qnat = V(oxt0, 128, IFFN, F32)
        qint = V(ohf, 128, IFFN, BF16)
        qtrn = V(oxi[0], 128, IFFN, BF16)
        a3 = Reg(ABASE)
        a3os = (V(a3.a(H * 4), 128, H, F32), V(a3.a(H * 4), 128, H, F32))
        a3gs = (V(a3.a(H * 2), 128, H, BF16), V(a3.a(H * 2), 128, H, BF16))
        a3xs = (V(a3.a(H * 4), 128, H, F32), V(a3.a(H * 4), 128, H, F32))
        a3rfs = (V(a3.a(KD * 2), 128, KD, BF16), V(a3.a(KD * 2), 128, KD, BF16))
        a3sq = V(a3.a(H * 4), 128, H, F32)
        xo_b = V(a3.a(H * 2), 128, H, BF16)
        th_b = V(a3.a(H * 2), 128, H, BF16)
        sg2_b = V(a3.a(H * 2), 128, H, BF16)
        of_b = V(a3.a(H * 2), 128, H, BF16)
        of_fm = V(a3.a(H * 2), 128, H, BF16)
        x2_t = V(a3.a(H * 4), 128, H, F32)
        h2_t = V(a3.a(H * 4), 128, H, F32)
        t1_s = V(a3.a(H * 4), 128, H, F32)
        bb_ = Reg(ABASE)
        bhs = (V(bb_.a(H * 4), 128, H, F32), V(bb_.a(H * 4), 128, H, F32))
        bxq = V(bb_.a(H * 2), 128, H, BF16)
        bxqf = V(bb_.a(H * 2), 128, H, BF16)
        obt0 = bb_.a(IFFN * 4)
        obt1 = bb_.a(IFFN * 4)
        bts = (V(obt0, 128, IFFN, F32), V(obt1, 128, IFFN, F32))
        sg_row = V(obt0, 1, IFFN, F32)
        su_row = V(obt1, 1, IFFN, F32)
        bgr = V(bb_.a(512 * 4), 128, 512, F32)
        bur = V(bb_.a(512 * 4), 128, 512, F32)
        bth = V(bb_.a(512 * 2), 128, 512, BF16)
        bq1 = V(bb_.a(H * 4), 128, H, F32)
        sxg_b = V(bb_.a(IFFN * 2), 128, IFFN, BF16)
        sxu_b = V(bb_.a(IFFN * 2), 128, IFFN, BF16)
        cr = Reg(ABASE)
        oct0 = cr.a(IFFN * 4)
        cts = (V(oct0, 128, IFFN, F32), V(cr.a(IFFN * 4), 128, IFFN, F32))
        sd_row = V(oct0, 1, H, F32)
        cxs = (V(cr.a(H * 4), 128, H, F32), V(cr.a(H * 4), 128, H, F32))
        chq = V(cr.a(IFFN * 2), 128, IFFN, BF16)
        chqf = V(cr.a(IFFN * 2), 128, IFFN, BF16)
        cq1 = V(cr.a(IFFN * 4), 128, IFFN, F32)
        co_t = V(cr.a(H * 4), 128, H, F32)
        sdtgb_b = V(cr.a(H * 2), 128, H, BF16)

        BUFS = {}

        def NB(name):
            if name not in BUFS:
                BUFS[name] = Buf(name)
            return BUFS[name]

        pfb = [NB(f"PF{i}") for i in range(6)]
        pbb = [NB(f"PB{i}") for i in range(2)]

        def T(f, *a, **k):
            return P.op("t", f, *a, **k)

        def VE(f, *a, **k):
            return P.op("v", f, *a, **k)

        def SE(f, *a, **k):
            return P.op("s", f, *a, **k)

        def GD(f, *a, **k):
            return P.op("g", f, *a, **k)

        def SD(f, *a, **k):
            return P.op("d", f, *a, **k)

        bW = NB("wA"); bWq = NB("wQ"); bConst = NB("const"); bMod = NB("mod")
        bS = [NB("S0"), NB("S1")]
        bSbf = NB("Sbf"); bSin = NB("Sin"); bPcum = NB("pcum"); bEneg = NB("eneg")
        bNrun = NB("nrun"); bPrev = NB("prev"); bCols = NB("cols"); bStats = NB("stats")
        bRmx = NB("rmx"); bRows = NB("rows"); bStage = NB("stage")
        bXp = NB("xp"); bW1 = NB("w1"); bPw = NB("pw"); bRc = NB("rc"); bKc = NB("kc")
        bFacR = NB("facR"); bFacK = NB("facK"); bRt = NB("rt"); bKt = NB("kt")
        bKhc = NB("khc"); bKht = NB("kht"); bRh = NB("rh"); bRf = NB("rf")
        bRk = NB("rk"); bAm = NB("am"); bVt = NB("vt"); bGt = NB("gt"); bOt = NB("ot")
        bDsb = NB("dsb"); bHfm = NB("hfm"); bDelta = NB("dl"); bXhat = NB("xh")
        bSq = NB("sq"); bMu = [NB("mu0b"), NB("mu1b")]
        bXi = [NB(f"xi{i}") for i in range(6)]
        bXt = [NB("xt0b"), NB("xt1b")]
        bOS = NB("osS"); bGS = NB("gsS"); bRFS = NB("rfsS")
        bX2S = NB("x2S"); bH2S = NB("h2S"); bTS = NB("tS")
        bAGpi = NB("agpi"); bAGpo = NB("agpo"); bAGwi = NB("agwi"); bAGwo = NB("agwo")
        bAGsi = NB("agsi"); bAGso = NB("agso"); bARh = NB("arh"); bARho = NB("arho")
        bARt = NB("art"); bARto = NB("arto")
        bQn = NB("qn"); bQi = NB("qi"); bQt = NB("qt")
        bMs = [NB("msA"), NB("msB"), NB("msC")]
        bMt = [NB("mt0"), NB("mt1")]
        bSrep = NB("srep"); bCrow = NB("crow"); bXpr = NB("xpr")
        bA3o = [NB("a3o0"), NB("a3o1")]
        bA3g = [NB("a3g0"), NB("a3g1")]
        bA3x = [NB("a3x0"), NB("a3x1")]
        bA3rf = [NB("a3rf0"), NB("a3rf1")]
        bA3sq = NB("a3sq"); bXo = NB("xob"); bTh = NB("thb"); bSg2 = NB("sg2")
        bOf = NB("ofb"); bOffm = NB("offm"); bX2t = NB("x2t"); bH2t = NB("h2t")
        bT1 = NB("t1s")
        bBh = [NB("bh0"), NB("bh1")]; bBt = [NB("bt0"), NB("bt1")]
        bBxq = NB("bxq"); bBxqf = NB("bxqf"); bBgr = NB("bgr"); bBur = NB("bur")
        bBth = NB("bth"); bBq1 = NB("bq1"); bSx = NB("sxb")
        bCt = [NB("ct0"), NB("ct1")]; bCx = [NB("cx0"), NB("cx1")]
        bChq = NB("chq"); bChqf = NB("chqf"); bCq1 = NB("cq1"); bCo = NB("co")
        bSdt = NB("sdt"); bOutD = NB("outd")

        def rsqrt_newton(dst_col, var_col, ncols, iters=11):
            u = stats[:, 0:ncols]
            r_ = stats[:, 8:8 + ncols]
            h_ = stats[:, 12:12 + ncols]
            a_ = stats[:, 16:16 + ncols]
            ra = stats[:, 24:24 + ncols]
            t_ = stats[:, 32:32 + ncols]
            VE("tensor_scalar", u, var_col, EPS, None, ALU.add, reads=(bCols,), writes=(bStats,))
            VE("reciprocal", r_, u, reads=(bStats,), writes=(bStats,))
            VE("tensor_scalar", h_, r_, 0.5, None, ALU.mult, reads=(bStats,), writes=(bStats,))
            VE("tensor_scalar", a_, r_, 0.5, 0.5, ALU.mult, ALU.add, reads=(bStats,), writes=(bStats,))
            for _ in range(iters):
                VE("reciprocal", ra, a_, reads=(bStats,), writes=(bStats,))
                VE("tensor_tensor", t_, h_, ra, ALU.mult, reads=(bStats,), writes=(bStats,))
                VE("scalar_tensor_tensor", a_, a_, 0.5, t_, ALU.mult, ALU.add,
                   reads=(bStats,), writes=(bStats,))
            VE("tensor_copy", dst_col, a_, reads=(bStats,), writes=(bCols,))

        def ln_stats(src, b_src, rstd_col, negm_col, sqb, b_sq, ln_iters=7):
            VE("reduce_sum", cols[:, 40:41], src, axis=AX, reads=(b_src,), writes=(bCols,))
            SE("activation", sqb, src, AF.Square, accum_out=cols[:, 41:42],
               reads=(b_src,), writes=(b_sq, bCols))
            VE("tensor_scalar", cols[:, 42:43], cols[:, 40:41], 1.0 / H, None, ALU.mult,
               reads=(bCols,), writes=(bCols,))
            VE("tensor_scalar", cols[:, 43:44], cols[:, 41:42], 1.0 / H, None, ALU.mult,
               reads=(bCols,), writes=(bCols,))
            VE("tensor_scalar", cols[:, 44:45], cols[:, 42:43], -1.0, None, ALU.mult,
               reads=(bCols,), writes=(bCols,))
            VE("scalar_tensor_tensor", cols[:, 45:46], cols[:, 42:43], cols[:, 44:45],
               cols[:, 43:44], ALU.mult, ALU.add, reads=(bCols,), writes=(bCols,))
            rsqrt_newton(rstd_col, cols[:, 45:46], 1, iters=ln_iters)
            VE("tensor_tensor", cols[:, 46:47], cols[:, 42:43], rstd_col, ALU.mult,
               reads=(bCols,), writes=(bCols,))
            VE("tensor_scalar", negm_col, cols[:, 46:47], -1.0, None, ALU.mult,
               reads=(bCols,), writes=(bCols,))

        SD("dma_start", out=identb, in_=identb_in, writes=(bConst,))
        SD("dma_start", out=identf, in_=identf_in, writes=(bConst,))
        SD("dma_start", out=mask128, in_=mask128_in, writes=(bConst,))
        SD("dma_start", out=onesrow, in_=onesrow_in, writes=(bConst,))
        SD("dma_start", out=mu0, in_=mu0_in, writes=(bConst,))
        SD("dma_start", out=pmaskc, in_=pmask_in, writes=(bConst,))
        SD("dma_start", out=cmbc, in_=cmb_in, writes=(bConst,))
        SD("dma_start", out=crow, in_=cvec_in, writes=(bCrow,))
        SD("dma_start", out=xprow, in_=xprev_in, writes=(bXpr,))
        GD("dma_start", out=gnwh_b, in_=gnw_in, writes=(bMod,))
        GD("dma_start", out=gnbh_b, in_=gnb_in, writes=(bMod,))
        SD("dma_start", out=cpx[0:1, 0:1], in_=clipx_in, writes=(bRows,))
        SD("dma_start", out=cpx[0:1, 1:2], in_=clipd_in, writes=(bRows,))
        GD("dma_start", out=wx0, in_=wx0_in, writes=(bW,))
        GD("dma_start", out=wx2, in_=wx2_in, writes=(bW,))
        GD("dma_start", out=wrT, in_=wr_in, writes=(bW,))
        GD("dma_start", out=ww1, in_=ww1_in, writes=(bW,))
        GD("dma_start", out=ww2, in_=ww2_in, writes=(bW,))
        GD("dma_start", out=wkT, in_=wk_in, writes=(bW,))
        GD("dma_start", out=wvT, in_=wv_in, writes=(bW,))
        GD("dma_start", out=wgT, in_=wg_in, writes=(bW,))
        GD("dma_start", out=woT, in_=wo_in, writes=(bW,))
        GD("dma_start", out=u_bf, in_=bonus_in, writes=(bW,))
        VE("memset", zeros64, 0.0, writes=(bConst,))
        VE("memset", zb512, 0.0, writes=(bConst,))
        VE("memset", ones128, 1.0, writes=(bConst,))
        VE("memset", S0, 0.0, writes=(bS[0],))
        VE("memset", S_bf, 0.0, writes=(bSbf,))
        VE("memset", runmax, 0.0, writes=(bRmx,))
        VE("memset", xp_sb[32:33, :], 1.0, writes=(bXp,))
        VE("memset", w1_sb[64:65, :], 1.0, writes=(bW1,))
        for g in range(8):
            VE("memset", pcum[:, g * 65:g * 65 + 1], 0.0, writes=(bPcum,))
        for hh in range(NH):
            VE("tensor_copy", negrun[:, hh:hh + 1], cmbc, reads=(bConst,), writes=(bNrun,))

        for c in range(8):
            T("transpose", PF[0][0:128, c:c + 1], crow[0:1, c * 128:(c + 1) * 128],
              identf[0:1, 0:1], reads=(bCrow, bConst), writes=(pfb[0],))
        SE("copy", stats[:, 48:56], PF[0][:, 0:8], reads=(pfb[0],), writes=(bStats,))
        cfm = stats[:, 48:56]
        SE("activation", stats[:, 56:64], cfm, AF.Tanh, scale=0.5, reads=(bStats,), writes=(bStats,))
        VE("tensor_scalar", cols[:, 0:8], stats[:, 56:64], 0.5, 0.5, ALU.mult, ALU.add,
           reads=(bStats,), writes=(bCols,))
        VE("tensor_tensor", cols[:, 8:16], cols[:, 0:8], cfm, ALU.mult,
           reads=(bCols, bStats), writes=(bCols,))
        for c in range(8):
            VE("tensor_scalar", srep[:, c * 128:(c + 1) * 128], ones128, cols[:, 8 + c:9 + c],
               None, ALU.mult, reads=(bConst, bCols), writes=(bSrep,))

        for s in range(12):
            mt = modt[s % 2]
            bmt = bMt[s % 2]
            for kc in range(8):
                SD("dma_start", out=mt,
                   in_=wmod_in[kc * 128:(kc + 1) * 128, s * 512:(s + 1) * 512], writes=(bmt,))
                if s < 4:
                    for grp in range(4):
                        T("matmul", PF[1 + grp][:, s * 8:(s + 1) * 8],
                          mt[:, grp * 128:(grp + 1) * 128], srep[:, 0:8],
                          start=(kc == 0), stop=(kc == 7),
                          reads=(bmt, bSrep), writes=(pfb[1 + grp],))
                else:
                    bankk = pfb[2 + (s % 2)]
                    T("matmul", PF[2 + (s % 2)][:, 0:512], srep[:, kc * 128:(kc + 1) * 128],
                      mt[:, 0:512], start=(kc == 0), stop=(kc == 7),
                      reads=(bmt, bSrep), writes=(bankk,))
            if s == 3:
                for c in range(8):
                    s_, grp = divmod(c, 4)
                    SE("copy", sa_fm[:, c:c + 1], PF[1 + grp][:, s_ * 8:s_ * 8 + 1],
                       reads=(pfb[1 + grp],), writes=(bMod,))
                for c in range(8):
                    s_, grp = divmod(8 + c, 4)
                    SE("activation", ca1_fm[:, c:c + 1], PF[1 + grp][:, s_ * 8:s_ * 8 + 1],
                       AF.Identity, bias=1.0, reads=(pfb[1 + grp],), writes=(bMod,))
            if s >= 4:
                bankk = pfb[2 + (s % 2)]
                pf = PF[2 + (s % 2)]
                j0 = (s - 4) * 512
                if j0 < 1024:
                    SE("copy", mscrA[:, j0:j0 + 512], pf[:, 0:512], reads=(bankk,), writes=(bMs[0],))
                elif j0 < 2048:
                    SE("copy", sb_b[:, j0 - 1024:j0 - 512], pf[:, 0:512], reads=(bankk,), writes=(bMod,))
                elif j0 < 3072:
                    SE("activation", cb1_b[:, j0 - 2048:j0 - 1536], pf[:, 0:512],
                       AF.Identity, bias=1.0, reads=(bankk,), writes=(bMod,))
                else:
                    SE("copy", mscrB[:, j0 - 3072:j0 - 2560], pf[:, 0:512], reads=(bankk,), writes=(bMs[1],))
        for (src, bsrc, dstb) in ((mscrA, bMs[0], ga_b), (mscrB, bMs[1], gb_b)):
            SE("activation", mscrC, src, AF.Abs, reads=(bsrc,), writes=(bMs[2],))
            VE("tensor_scalar", mscrC, mscrC, 0.25, None, ALU.max, reads=(bMs[2],), writes=(bMs[2],))
            VE("tensor_scalar", src, src, 0.0, None, ALU.is_ge, reads=(bsrc,), writes=(bsrc,))
            VE("tensor_scalar", src, src, 2.0, -1.0, ALU.mult, ALU.add, reads=(bsrc,), writes=(bsrc,))
            VE("tensor_tensor", dstb, mscrC, src, ALU.mult, reads=(bMs[2], bsrc), writes=(bMod,))

        VE("reduce_sum", scrow[0:1, 0:1], xprow, axis=AX, reads=(bXpr,), writes=(bRows,))
        SE("activation", mscrA[0:1, :], xprow, AF.Square, accum_out=scrow[0:1, 1:2],
           reads=(bXpr,), writes=(bMs[0], bRows))
        VE("tensor_scalar", scrow[0:1, 2:3], scrow[0:1, 0:1], 1.0 / H, None, ALU.mult,
           reads=(bRows,), writes=(bRows,))
        VE("tensor_scalar", scrow[0:1, 3:4], scrow[0:1, 1:2], 1.0 / H, None, ALU.mult,
           reads=(bRows,), writes=(bRows,))
        VE("tensor_scalar", scrow[0:1, 4:5], scrow[0:1, 2:3], -1.0, None, ALU.mult,
           reads=(bRows,), writes=(bRows,))
        VE("scalar_tensor_tensor", scrow[0:1, 5:6], scrow[0:1, 2:3], scrow[0:1, 4:5],
           scrow[0:1, 3:4], ALU.mult, ALU.add, reads=(bRows,), writes=(bRows,))
        VE("tensor_scalar", scrow[0:1, 6:7], scrow[0:1, 5:6], EPS, None, ALU.add,
           reads=(bRows,), writes=(bRows,))
        VE("reciprocal", scrow[0:1, 7:8], scrow[0:1, 6:7], reads=(bRows,), writes=(bRows,))
        VE("tensor_scalar", scrow[0:1, 8:9], scrow[0:1, 7:8], 0.5, 0.5, ALU.mult, ALU.add,
           reads=(bRows,), writes=(bRows,))
        for _ in range(11):
            VE("reciprocal", scrow[0:1, 10:11], scrow[0:1, 8:9], reads=(bRows,), writes=(bRows,))
            VE("tensor_tensor", scrow[0:1, 10:11], scrow[0:1, 10:11], scrow[0:1, 7:8], ALU.mult,
               reads=(bRows,), writes=(bRows,))
            VE("tensor_tensor", scrow[0:1, 8:9], scrow[0:1, 8:9], scrow[0:1, 10:11], ALU.add,
               reads=(bRows,), writes=(bRows,))
            VE("tensor_scalar", scrow[0:1, 8:9], scrow[0:1, 8:9], 0.5, None, ALU.mult,
               reads=(bRows,), writes=(bRows,))
        VE("tensor_tensor", scrow[0:1, 12:13], scrow[0:1, 2:3], scrow[0:1, 8:9], ALU.mult,
           reads=(bRows,), writes=(bRows,))
        VE("tensor_scalar", scrow[0:1, 13:14], scrow[0:1, 12:13], -1.0, None, ALU.mult,
           reads=(bRows,), writes=(bRows,))
        SE("activation", mscrB[0:1, :], xprow, AF.Identity, scale=scrow[0:1, 8:9],
           bias=scrow[0:1, 13:14], reads=(bXpr, bRows), writes=(bMs[1],))
        for c in range(8):
            T("transpose", PF[5][0:128, c:c + 1], mscrB[0:1, c * 128:(c + 1) * 128],
              identf[0:1, 0:1], reads=(bMs[1], bConst), writes=(pfb[5],))
        SE("copy", stats[:, 40:48], PF[5][:, 0:8], reads=(pfb[5],), writes=(bStats,))
        for c in range(8):
            VE("scalar_tensor_tensor", stats[:, 32 + c:33 + c], stats[:, 40 + c:41 + c],
               ca1_fm[:, c:c + 1], sa_fm[:, c:c + 1], ALU.mult, ALU.add,
               reads=(bStats, bMod), writes=(bStats,))
        VE("tensor_scalar", prevcol, stats[:, 32:40], pmaskc[:, 0:1], None, ALU.mult,
           reads=(bStats, bConst), writes=(bPrev,))

        if dbg:
            for j, (tb, bb2) in enumerate(((ga_b, bMod), (gb_b, bMod), (cb1_b, bMod),
                                           (sb_b, bMod), (gnwh_b, bMod), (gnbh_b, bMod))):
                SD("dma_start", out=dbg_mod[j], in_=tb, reads=(bb2,), writes=(NB("dbgd"),))
            SD("dma_start", out=dbg_fm[:, 0:8], in_=sa_fm, reads=(bMod,), writes=(NB("dbgd"),))
            SD("dma_start", out=dbg_fm[:, 8:16], in_=ca1_fm, reads=(bMod,), writes=(NB("dbgd"),))

        P.barrier()

        SD("dma_start", out=xts[0], in_=x_in[0:TILE, :], writes=(bXt[0],))
        sparity = 0
        Ss = (S0, S1)
        for i in range(NT):
            xt = xts[i % 2]
            bX = bXt[i % 2]
            if i + 1 < NT:
                SD("dma_start", out=xts[(i + 1) % 2], in_=x_in[(i + 1) * TILE:(i + 2) * TILE, :],
                   writes=(bXt[(i + 1) % 2],))
            if dbg:
                SD("dma_start", out=dbg_xt[i], in_=xt, reads=(bX,), writes=(NB("dbgd"),))
            ln_stats(xt, bX, cols[:, 20:21], cols[:, 21:22], sqbuf, bSq, ln_iters=5)
            SE("activation", xhat, xt, AF.Identity, scale=cols[:, 20:21], bias=cols[:, 21:22],
               reads=(bX, bCols), writes=(bXhat,))
            if dbg:
                SD("dma_start", out=dbg_cols[i], in_=cols, reads=(bCols,), writes=(NB("dbgd"),))
                SD("dma_start", out=dbg_xh[i], in_=xhat, reads=(bXhat,), writes=(NB("dbgd"),))
            for c in range(8):
                T("transpose", PB[0][:, c * 128:(c + 1) * 128], xhat[:, c * 128:(c + 1) * 128],
                  identb, reads=(bXhat, bConst), writes=(pbb[0],))
            for c in range(8):
                SE("activation", h_fm[:, c * 128:(c + 1) * 128], PB[0][:, c * 128:(c + 1) * 128],
                   AF.Identity, scale=ca1_fm[:, c:c + 1], bias=sa_fm[:, c:c + 1],
                   reads=(pbb[0], bMod), writes=(bHfm,))
            hh2 = h_fm.rearrange("p (c e) -> p c e", c=8)
            dd2 = delta.rearrange("p (c e) -> p c e", c=8)
            pc2 = prevcol.rearrange("p (c e) -> p c e", e=1)
            VE("tensor_tensor", dd2[:, :, 0:1], pc2, hh2[:, :, 0:1], ALU.subtract,
               reads=(bPrev, bHfm), writes=(bDelta,))
            VE("tensor_tensor", dd2[:, :, 1:128], hh2[:, :, 0:127], hh2[:, :, 1:128],
               ALU.subtract, reads=(bHfm,), writes=(bDelta,))
            VE("tensor_copy", prevcol, hh2[:, :, 127], reads=(bHfm,), writes=(bPrev,))
            for c in range(8):
                VE("scalar_tensor_tensor", xi[0][:, c * 128:(c + 1) * 128],
                   delta[:, c * 128:(c + 1) * 128], mu0[:, c:c + 1],
                   h_fm[:, c * 128:(c + 1) * 128], ALU.mult, ALU.add,
                   reads=(bDelta, bConst, bHfm), writes=(bXi[0],))
            for n in range(5):
                bankk = pfb[0] if n < 4 else pfb[1]
                pf = PF[0] if n < 4 else PF[1]
                col0 = (n % 4) * 128 if n < 4 else 0
                for kc in range(8):
                    T("matmul", pf[0:32, col0:col0 + 128],
                      wx0[:, kc * 160 + n * 32:kc * 160 + (n + 1) * 32],
                      xi[0][:, kc * 128:(kc + 1) * 128], start=(kc == 0), stop=(kc == 7),
                      reads=(bW, bXi[0]), writes=(bankk,))
            SE("activation", xp_sb[0:32, 0:512], PF[0][0:32, 0:512], AF.Tanh,
               reads=(pfb[0],), writes=(bXp,))
            SE("activation", xp_sb[0:32, 512:640], PF[1][0:32, 0:128], AF.Tanh,
               reads=(pfb[1],), writes=(bXp,))
            XIMAP = (1, 2, 3, 4, 5)

            def emit_mu_xi(n):
                bankk = pfb[2 + (n % 2)]
                pf = PF[2 + (n % 2)]
                mu = mu_fm[n % 2]
                bM = bMu[n % 2]
                for c8 in range(8):
                    T("matmul", pf[:, (c8 % 4) * 128:(c8 % 4 + 1) * 128],
                      wx2[0:33, n * H + c8 * 128:n * H + (c8 + 1) * 128],
                      xp_sb[0:33, n * 128:(n + 1) * 128], start=True, stop=True,
                      reads=(bW, bXp), writes=(bankk,))
                    if c8 == 3 or c8 == 7:
                        SE("copy", mu[:, (c8 - 3) * 128:(c8 + 1) * 128], pf[:, 0:512],
                           reads=(bankk,), writes=(bM,))
                xin = xi[XIMAP[n]]
                bxin = bXi[XIMAP[n]]
                VE("tensor_tensor", xin, delta, mu, ALU.mult, reads=(bDelta, bM), writes=(bxin,))
                VE("tensor_tensor", xin, xin, h_fm, ALU.add, reads=(bxin, bHfm), writes=(bxin,))

            def emit_rk(which):
                xin, bxin, wt, banki, dst, bdst = (
                    (xi[1], bXi[1], wrT, 4, r_cm, bRc) if which == 0 else
                    (xi[3], bXi[3], wkT, 5, k_cm, bKc))
                bankk = pfb[banki]
                pf = PF[banki]
                for fc in range(4):
                    for kc in range(8):
                        T("matmul", pf[:, fc * 128:(fc + 1) * 128],
                          wt[:, kc * KD + fc * 128:kc * KD + (fc + 1) * 128],
                          xin[:, kc * 128:(kc + 1) * 128], start=(kc == 0), stop=(kc == 7),
                          reads=(bW, bxin), writes=(bankk,))
                SE("copy", dst, pf[:, 0:512], reads=(bankk,), writes=(bdst,))

            def emit_w():
                for kc in range(8):
                    T("matmul", PF[0][0:64, 0:128], ww1[:, kc * 64:(kc + 1) * 64],
                      xi[2][:, kc * 128:(kc + 1) * 128], start=(kc == 0), stop=(kc == 7),
                      reads=(bW, bXi[2]), writes=(pfb[0],))
                SE("activation", w1_sb[0:64, :], PF[0][0:64, 0:128], AF.Tanh,
                   reads=(pfb[0],), writes=(bW1,))
                for fc in range(4):
                    T("matmul", PF[1][:, fc * 128:(fc + 1) * 128],
                      ww2[0:65, fc * 128:(fc + 1) * 128], w1_sb[0:65, :], start=True, stop=True,
                      reads=(bW, bW1), writes=(pfb[1],))
                SE("activation", pw, PF[1][:, 0:512], AF.Exp, reads=(pfb[1],), writes=(bPw,))

            def emit_vg(which, half):
                xin, bxin, wt, dst, bdst = (
                    (xi[4], bXi[4], wvT, v_tm, bVt) if which == 0 else
                    (xi[5], bXi[5], wgT, g_tm, bGt))
                bankk = pfb[2 + half]
                pf = PF[2 + half]
                for kc in range(8):
                    T("matmul", pf[:, 0:512], xin[:, kc * 128:(kc + 1) * 128],
                      wt[:, kc * VD + half * 512:kc * VD + (half + 1) * 512],
                      start=(kc == 0), stop=(kc == 7), reads=(bW, bxin), writes=(bankk,))
                SE("copy", dst[:, half * 512:(half + 1) * 512], pf[:, 0:512],
                   reads=(bankk,), writes=(bdst,))

            emit_mu_xi(0)
            emit_mu_xi(1)
            emit_rk(0)
            emit_mu_xi(2)
            emit_w()
            emit_mu_xi(3)
            emit_rk(1)
            emit_mu_xi(4)
            emit_vg(0, 0)
            emit_vg(0, 1)
            emit_vg(1, 0)
            emit_vg(1, 1)
            for g in range(8):
                hh, sub = divmod(g, 2)
                VE("tensor_tensor_scan", pcum[:, g * 65 + 1:g * 65 + 65],
                   pw[:, hh * 128 + sub * 64:hh * 128 + (sub + 1) * 64], zeros64, 0.0,
                   ALU.add, ALU.add, reads=(bPw, bConst), writes=(bPcum,))
            SE("activation", eneg, pcum, AF.Exp, scale=-1.0, reads=(bPcum,), writes=(bEneg,))
            plc = pcum.rearrange("p (g e) -> p g e", g=8)[:, :, 64]
            VE("tensor_scalar", cols[:, 0:8], plc, 0.5, None, ALU.mult, reads=(bPcum,), writes=(bCols,))
            VE("tensor_scalar", cols[:, 8:16], plc, -0.5, None, ALU.mult, reads=(bPcum,), writes=(bCols,))
            SE("activation", cols[:, 24:32], cols[:, 8:16], AF.Exp, reads=(bCols,), writes=(bCols,))
            for g in range(8):
                SE("activation", facR[:, g * 64:(g + 1) * 64], pcum[:, g * 65:g * 65 + 64],
                   AF.Exp, scale=-1.0, bias=cols[:, g:g + 1], reads=(bPcum, bCols), writes=(bFacR,))
                SE("activation", facK[:, g * 64:(g + 1) * 64], pcum[:, g * 65 + 1:g * 65 + 65],
                   AF.Exp, scale=1.0, bias=cols[:, 8 + g:9 + g], reads=(bPcum, bCols), writes=(bFacK,))
            VE("tensor_tensor", rt_b, r_cm, facR, ALU.mult, reads=(bRc, bFacR), writes=(bRt,))
            VE("tensor_tensor", kt_b, k_cm, facK, ALU.mult, reads=(bKc, bFacK), writes=(bKt,))
            for g in range(8):
                VE("tensor_scalar", khc_b[:, g * 64:(g + 1) * 64], kt_b[:, g * 64:(g + 1) * 64],
                   cols[:, 24 + g:25 + g], None, ALU.mult, reads=(bKt, bCols), writes=(bKhc,))
                VE("tensor_tensor", rhat_b[:, g * 64:(g + 1) * 64], r_cm[:, g * 64:(g + 1) * 64],
                   eneg[:, g * 65:g * 65 + 64], ALU.mult, reads=(bRc, bEneg), writes=(bRh,))
            if i < NCORR:
                for hh in range(NH):
                    VE("tensor_tensor", cols[:, 32 + hh:33 + hh], negrun[:, hh:hh + 1],
                       pcum[:, (hh * 2) * 65 + 64:(hh * 2) * 65 + 65], ALU.subtract,
                       reads=(bNrun, bPcum), writes=(bCols,))
                for g in range(8):
                    hh, sub = divmod(g, 2)
                    bias = negrun[:, hh:hh + 1] if sub == 0 else cols[:, 32 + hh:33 + hh]
                    br = bNrun if sub == 0 else bCols
                    SE("activation", facR[:, g * 64:(g + 1) * 64], pcum[:, g * 65:g * 65 + 64],
                       AF.Exp, scale=-1.0, bias=bias, reads=(bPcum, br), writes=(bFacR,))
                VE("tensor_tensor", rf_b, r_cm, facR, ALU.mult, reads=(bRc, bFacR), writes=(bRf,))
                for hh in range(NH):
                    VE("tensor_tensor", negrun[:, hh:hh + 1], cols[:, 32 + hh:33 + hh],
                       pcum[:, (hh * 2 + 1) * 65 + 64:(hh * 2 + 1) * 65 + 65], ALU.subtract,
                       reads=(bCols, bPcum), writes=(bNrun,))
                SD("dma_start", out=rfs_s[i * TILE:(i + 1) * TILE, :], in_=rf_b,
                   reads=(bRf,), writes=(bRFS,))
            VE("tensor_tensor", rk_b, r_cm, k_cm, ALU.mult, reads=(bRc, bKc), writes=(bRk,))
            for g in range(8):
                hh, sub = divmod(g, 2)
                sl = slice(hh * 128 + sub * 64, hh * 128 + (sub + 1) * 64)
                kw = {"tile_position": (0, 64)} if sub == 1 else {}
                T("matmul", PF[0][sub * 64:(sub + 1) * 64, g * 64:(g + 1) * 64],
                  kt_b[:, sl], rt_b[:, sl], start=True, stop=True,
                  reads=(bKt, bRt), writes=(pfb[0],), **kw)
            VE("select", am_b, mask128, PF[0][:, 0:512], zb512,
               reads=(pfb[0], bConst), writes=(bAm,))
            for g in range(8):
                hh, sub = divmod(g, 2)
                sl = slice(hh * 128 + sub * 64, hh * 128 + (sub + 1) * 64)
                kw = {"tile_position": (0, 64)} if sub == 1 else {}
                T("matmul", PF[1][sub * 64:(sub + 1) * 64, hh:hh + 1], rk_b[:, sl],
                  u_bf[:, hh:hh + 1], start=True, stop=True, reads=(bRk, bW),
                  writes=(pfb[1],), **kw)
            for g in range(8):
                hh, sub = divmod(g, 2)
                sl = slice(hh * 128 + sub * 64, hh * 128 + (sub + 1) * 64)
                kw = {"tile_position": (0, 64)} if sub == 1 else {}
                T("transpose", PB[1][sub * 64:(sub + 1) * 64, hh * 128:(hh + 1) * 128],
                  khc_b[:, sl], identb, reads=(bKhc, bConst), writes=(pbb[1],), **kw)
            SE("copy", khat_tm, PB[1][:, 0:512], reads=(pbb[1],), writes=(bKht,))
            SE("copy", d_sb[:, 0:NH], PF[1][:, 0:NH], reads=(pfb[1], pbb[1]), writes=(bDsb,))
            for sub in range(2):
                Scur, bScur = Ss[sparity], bS[sparity]
                Snew, bSnew = Ss[1 - sparity], bS[1 - sparity]
                r0, r1 = sub * 64, sub * 64 + 64
                for hh in range(NH):
                    g = hh * 2 + sub
                    bankk = pfb[2 + hh // 2]
                    pf = PF[2 + hh // 2]
                    ocol = (hh % 2) * 256
                    sl = slice(hh * 128 + sub * 64, hh * 128 + (sub + 1) * 64)
                    kw1 = {"tile_position": (0, 64)} if sub == 1 else {}
                    kw2 = {"tile_position": (64, 64)} if sub == 1 else {}
                    T("matmul", pf[r0:r1, ocol:ocol + 256], rhat_b[:, sl],
                      S_bf[:, hh * 256:(hh + 1) * 256], start=True, stop=False,
                      reads=(bRh, bSbf), writes=(bankk,), **kw1)
                    T("matmul", pf[r0:r1, ocol:ocol + 256], am_b[r0:r1, g * 64:(g + 1) * 64],
                      v_tm[r0:r1, hh * 256:(hh + 1) * 256], start=False, stop=True,
                      reads=(bAm, bVt), writes=(bankk,), **kw2)
                    sbank = pfb[4 + hh // 2]
                    spf = PF[4 + hh // 2]
                    kw3 = {"tile_position": (64, 0)} if sub == 1 else {}
                    T("matmul", spf[:, (hh % 2) * 256:(hh % 2) * 256 + 256],
                      khat_tm[r0:r1, hh * 128:(hh + 1) * 128],
                      v_tm[r0:r1, hh * 256:(hh + 1) * 256], start=True, stop=True,
                      reads=(bKht, bVt), writes=(sbank,), **kw3)
                for hh in range(NH):
                    pf = PF[2 + hh // 2]
                    bankk = pfb[2 + hh // 2]
                    ocol = (hh % 2) * 256
                    VE("scalar_tensor_tensor", o_sb[r0:r1, hh * 256:(hh + 1) * 256],
                       v_tm[r0:r1, hh * 256:(hh + 1) * 256], d_sb[r0:r1, hh:hh + 1],
                       pf[r0:r1, ocol:ocol + 256], ALU.mult, ALU.add,
                       reads=(bVt, bDsb, bankk), writes=(bOt,))
                for hh in range(NH):
                    g = hh * 2 + sub
                    sbank = pfb[4 + hh // 2]
                    spf = PF[4 + hh // 2]
                    VE("scalar_tensor_tensor", Snew[:, hh * 256:(hh + 1) * 256],
                       Scur[:, hh * 256:(hh + 1) * 256], eneg[:, g * 65 + 64:g * 65 + 65],
                       spf[:, (hh % 2) * 256:(hh % 2) * 256 + 256], ALU.mult, ALU.add,
                       reads=(bScur, bEneg, sbank), writes=(bSnew,))
                VE("tensor_copy", S_bf, Snew, reads=(bSnew,), writes=(bSbf,))
                sparity = 1 - sparity
            if dbg:
                SD("dma_start", out=dbg_am[i], in_=am_b, reads=(bAm,), writes=(NB("dbgd"),))
                SD("dma_start", out=dbg_kt[i], in_=kt_b, reads=(bKt,), writes=(NB("dbgd"),))
                SD("dma_start", out=dbg_rt[i], in_=rt_b, reads=(bRt,), writes=(NB("dbgd"),))
                SD("dma_start", out=dbg_kht[i], in_=khat_tm, reads=(bKht,), writes=(NB("dbgd"),))
                SD("dma_start", out=dbg_rh[i], in_=rhat_b, reads=(bRh,), writes=(NB("dbgd"),))
                SD("dma_start", out=dbg_ds[i], in_=d_sb, reads=(bDsb,), writes=(NB("dbgd"),))
                SD("dma_start", out=dbg_S[i], in_=S_bf, reads=(bSbf,), writes=(NB("dbgd"),))
                SD("dma_start", out=dbg_pc[i], in_=pcum, reads=(bPcum,), writes=(NB("dbgd"),))
            SD("dma_start", out=os_s[i * TILE:(i + 1) * TILE, :], in_=o_sb, reads=(bOt,), writes=(bOS,))
            SD("dma_start", out=gs_s[i * TILE:(i + 1) * TILE, :], in_=g_tm, reads=(bGt,), writes=(bGS,))

        GD("dma_start", out=agp_i, in_=Ss[sparity], reads=(bS[sparity],), writes=(bAGpi,))
        GD("collective_compute", "AllGather", ALU.bypass, [[0, 1], [2, 3], [4, 5], [6, 7]],
           [agp_i], [agp_o], reads=(bAGpi,), writes=(bAGpo,), ch="c")

        P.barrier()

        for w in range(7):
            if w < 3:
                src = wgn_in[w * 128:(w + 1) * 128, :]
                kk = H
            elif w < 6:
                src = wun_in[(w - 3) * 128:(w - 2) * 128, :]
                kk = H
            else:
                src = wdn_in
                kk = IFFN
            GD("dma_start", out=qnat[:, 0:kk], in_=src, writes=(bQn,))
            VE("reduce_max", cols[:, 36:37], qnat[:, 0:kk], axis=AX,
               apply_absolute_value=True, reads=(bQn,), writes=(bCols,))
            VE("reciprocal", cols[:, 37:38], cols[:, 36:37], reads=(bCols,), writes=(bCols,))
            VE("tensor_scalar", cols[:, 38:39], cols[:, 37:38], 127.0, None, ALU.mult,
               reads=(bCols,), writes=(bCols,))
            VE("tensor_scalar", cols[:, 39:40], cols[:, 36:37], 1.0 / 127.0, None, ALU.mult,
               reads=(bCols,), writes=(bCols,))
            T("transpose", PF[0][0:1, 0:128], cols[:, 39:40], identf,
              reads=(bCols, bConst), writes=(pfb[0],), force_drain=True)
            VE("tensor_scalar", qnat[:, 0:kk], qnat[:, 0:kk], cols[:, 38:39], CRND,
               ALU.mult, ALU.add, reads=(bQn, bCols), writes=(bQn,))
            VE("tensor_scalar", qint[:, 0:kk], qnat[:, 0:kk], CRND, None, ALU.subtract,
               reads=(bQn,), writes=(bQi,))
            for c in range(kk // 128):
                pbi = c % 2
                slot = (c // 2) % 4
                T("transpose", PB[pbi][:, slot * 128:(slot + 1) * 128],
                  qint[:, c * 128:(c + 1) * 128], identb, reads=(bQi, bConst), writes=(pbb[pbi],))
                SE("copy", qtrn[:, c * 128:(c + 1) * 128], PB[pbi][:, slot * 128:(slot + 1) * 128],
                   reads=(pbb[pbi],), writes=(bQt,))
            SE("copy", stage, PF[0][0:1, 0:128], reads=(pfb[0], pbb[0]), writes=(bStage,),
               force_drain=True)
            GD("dma_start", out=ags_i[w * 128:(w + 1) * 128], in_=stage,
               reads=(bStage,), writes=(bAGsi,))
            if w < 6:
                GD("dma_start", out=agw_i[:, w * 1024:(w + 1) * 1024], in_=qtrn[:, 0:1024],
                   reads=(bQt,), writes=(bAGwi,))
            else:
                GD("dma_start", out=agw_i[:, 6144:6144 + NGT * 128], in_=qtrn[:, 0:NGT * 128],
                   reads=(bQt,), writes=(bAGwi,))
        GD("collective_compute", "AllGather", ALU.bypass, [list(range(NCORES))],
           [agw_i], [agw_o], reads=(bAGwi,), writes=(bAGwo,), ch="c")
        GD("collective_compute", "AllGather", ALU.bypass, [list(range(NCORES))],
           [ags_i], [ags_o], reads=(bAGsi,), writes=(bAGso,), ch="c")
        GD("dma_start", out=qnat[:, 0:1024],
           in_=agp_o[0:1, :].rearrange("o (p f) -> (o p) f", p=128),
           reads=(bAGpo,), writes=(bQn,))
        VE("tensor_copy", Sin_bf, qnat[:, 0:1024], reads=(bQn,), writes=(bSin,))

        P.barrier()

        order = list(range(NCORR, NT)) + list(range(NCORR))
        for idx, i in enumerate(order):
            par = idx % 2
            ot, bOt3 = a3os[par], bA3o[par]
            gt_, bGt3 = a3gs[par], bA3g[par]
            xt3, bXt3 = a3xs[par], bA3x[par]
            SD("dma_start", out=ot, in_=os_s[i * TILE:(i + 1) * TILE, :], reads=(bOS,), writes=(bOt3,))
            SD("dma_start", out=gt_, in_=gs_s[i * TILE:(i + 1) * TILE, :], reads=(bGS,), writes=(bGt3,))
            SD("dma_start", out=xt3, in_=x_in[i * TILE:(i + 1) * TILE, :], writes=(bXt3,))
            if i < NCORR:
                rft, bRft = a3rfs[par], bA3rf[par]
                SD("dma_start", out=rft, in_=rfs_s[i * TILE:(i + 1) * TILE, :],
                   reads=(bRFS,), writes=(bRft,))
                for sub in range(2):
                    for hh in range(NH):
                        bankk = pfb[hh // 2]
                        pf = PF[hh // 2]
                        kw = {"tile_position": (0, 64)} if sub == 1 else {}
                        T("matmul", pf[sub * 64:(sub + 1) * 64, (hh % 2) * 256:(hh % 2) * 256 + 256],
                          rft[:, hh * 128 + sub * 64:hh * 128 + (sub + 1) * 64],
                          Sin_bf[:, hh * 256:(hh + 1) * 256], start=True, stop=True,
                          reads=(bRft, bSin), writes=(bankk,), **kw)
                for hh in range(NH):
                    bankk = pfb[hh // 2]
                    pf = PF[hh // 2]
                    VE("tensor_tensor", ot[:, hh * 256:(hh + 1) * 256],
                       ot[:, hh * 256:(hh + 1) * 256], pf[:, (hh % 2) * 256:(hh % 2) * 256 + 256],
                       ALU.add, reads=(bankk, bOt3), writes=(bOt3,))
            for hh in range(NH):
                VE("reduce_sum", cols[:, 40 + hh:41 + hh], ot[:, hh * 256:(hh + 1) * 256],
                   axis=AX, reads=(bOt3,), writes=(bCols,))
                SE("activation", a3sq[:, hh * 256:(hh + 1) * 256], ot[:, hh * 256:(hh + 1) * 256],
                   AF.Square, accum_out=cols[:, 44 + hh:45 + hh], reads=(bOt3,),
                   writes=(bA3sq, bCols))
            VE("tensor_scalar", cols[:, 40:44], cols[:, 40:44], 1.0 / DV, None, ALU.mult,
               reads=(bCols,), writes=(bCols,))
            VE("tensor_scalar", cols[:, 44:48], cols[:, 44:48], 1.0 / DV, None, ALU.mult,
               reads=(bCols,), writes=(bCols,))
            VE("tensor_scalar", cols[:, 36:40], cols[:, 40:44], -1.0, None, ALU.mult,
               reads=(bCols,), writes=(bCols,))
            for hh in range(NH):
                VE("scalar_tensor_tensor", cols[:, 32 + hh:33 + hh], cols[:, 40 + hh:41 + hh],
                   cols[:, 36 + hh:37 + hh], cols[:, 44 + hh:45 + hh], ALU.mult, ALU.add,
                   reads=(bCols,), writes=(bCols,))
            rsqrt_newton(cols[:, 28:32], cols[:, 32:36], 4, iters=10)
            VE("tensor_tensor", cols[:, 24:28], cols[:, 40:44], cols[:, 28:32], ALU.mult,
               reads=(bCols,), writes=(bCols,))
            VE("tensor_scalar", cols[:, 20:24], cols[:, 24:28], -1.0, None, ALU.mult,
               reads=(bCols,), writes=(bCols,))
            for hh in range(NH):
                SE("activation", xo_b[:, hh * 256:(hh + 1) * 256], ot[:, hh * 256:(hh + 1) * 256],
                   AF.Identity, scale=cols[:, 28 + hh:29 + hh], bias=cols[:, 20 + hh:21 + hh],
                   reads=(bOt3, bCols), writes=(bXo,))
            SE("activation", th_b, gt_, AF.Tanh, scale=0.5, reads=(bGt3,), writes=(bTh,))
            VE("tensor_scalar", th_b, th_b, 0.5, 0.5, ALU.mult, ALU.add,
               reads=(bTh,), writes=(bTh,))
            VE("tensor_tensor", sg2_b, th_b, gt_, ALU.mult, reads=(bTh, bGt3), writes=(bSg2,))
            VE("tensor_tensor", of_b, xo_b, gnwh_b, ALU.mult, reads=(bXo, bMod), writes=(bOf,))
            VE("tensor_tensor", of_b, of_b, gnbh_b, ALU.add, reads=(bOf, bMod), writes=(bOf,))
            VE("tensor_tensor", of_b, of_b, sg2_b, ALU.mult, reads=(bOf, bSg2), writes=(bOf,))
            for c in range(8):
                T("transpose", PB[0][:, c * 128:(c + 1) * 128], of_b[:, c * 128:(c + 1) * 128],
                  identb, reads=(bOf, bConst), writes=(pbb[0],))
            SE("copy", of_fm, PB[0][:, :], reads=(pbb[0],), writes=(bOffm,))
            for half in range(2):
                bankk = pfb[4 + half]
                pf = PF[4 + half]
                for kc in range(8):
                    T("matmul", pf[:, 0:512], of_fm[:, kc * 128:(kc + 1) * 128],
                      woT[:, kc * H + half * 512:kc * H + (half + 1) * 512],
                      start=(kc == 0), stop=(kc == 7), reads=(bOffm, bW), writes=(bankk,))
            for half in range(2):
                bankk = pfb[4 + half]
                pf = PF[4 + half]
                VE("tensor_tensor", t1_s[:, half * 512:(half + 1) * 512], pf[:, 0:512],
                   ga_b[:, half * 512:(half + 1) * 512], ALU.mult,
                   reads=(bankk, bMod), writes=(bT1,))
            VE("tensor_tensor", x2_t, t1_s, xt3, ALU.add, reads=(bT1, bXt3), writes=(bX2t,))
            ln_stats(x2_t, bX2t, cols[:, 18:19], cols[:, 19:20], a3sq, bA3sq)
            SE("activation", t1_s, x2_t, AF.Identity, scale=cols[:, 18:19], bias=cols[:, 19:20],
               reads=(bX2t, bCols), writes=(bT1,))
            VE("tensor_tensor", h2_t, t1_s, cb1_b, ALU.mult, reads=(bT1, bMod), writes=(bH2t,))
            VE("tensor_tensor", h2_t, h2_t, sb_b, ALU.add, reads=(bH2t, bMod), writes=(bH2t,))
            VE("reduce_max", cols[:, 17:18], h2_t, axis=AX, apply_absolute_value=True,
               reads=(bH2t,), writes=(bCols,))
            VE("tensor_tensor", runmax[:, 0:1], runmax[:, 0:1], cols[:, 17:18], ALU.max,
               reads=(bCols, bRmx), writes=(bRmx,))
            SD("dma_start", out=x2_s[i * TILE:(i + 1) * TILE, :], in_=x2_t,
               reads=(bX2t,), writes=(bX2S,))
            SD("dma_start", out=h2_s[i * TILE:(i + 1) * TILE, :], in_=h2_t,
               reads=(bH2t,), writes=(bH2S,))

        T("transpose", PF[0][0:1, 0:128], runmax[:, 0:1], identf, reads=(bRmx, bConst),
          writes=(pfb[0],), force_drain=True)
        SE("copy", stage, PF[0][0:1, 0:128], reads=(pfb[0],), writes=(bStage,), force_drain=True)
        VE("reduce_max", scrow[0:1, 14:15], stage, axis=AX, reads=(bStage,), writes=(bRows,))
        GD("dma_start", out=arh_i, in_=scrow[0:1, 14:15], reads=(bRows,), writes=(bARh,))
        GD("collective_compute", "AllReduce", ALU.max, [list(range(NCORES))],
           [arh_i], [arh_o], reads=(bARh,), writes=(bARho,), ch="c")

        P.barrier()

        for r in range(NCORES):
            for l in range(3):
                gt_i = 3 * r + l
                if gt_i >= NGT:
                    continue
                SD("dma_start",
                   out=wqg.rearrange("p (c f) -> p c f", c=8)[:, :, gt_i * 128:(gt_i + 1) * 128],
                   in_=agw_o[r, :, l * 1024:(l + 1) * 1024].rearrange("p (c j) -> p c j", c=8),
                   reads=(bAGwo,), writes=(bWq,))
                SD("dma_start",
                   out=wqu.rearrange("p (c f) -> p c f", c=8)[:, :, gt_i * 128:(gt_i + 1) * 128],
                   in_=agw_o[r, :, (3 + l) * 1024:(4 + l) * 1024].rearrange("p (c j) -> p c j", c=8),
                   reads=(bAGwo,), writes=(bWq,))
                SD("dma_start", out=sg_row[0:1, gt_i * 128:(gt_i + 1) * 128],
                   in_=ags_o[r, l * 128:(l + 1) * 128], reads=(bAGso,), writes=(bRows,))
                SD("dma_start", out=su_row[0:1, gt_i * 128:(gt_i + 1) * 128],
                   in_=ags_o[r, (3 + l) * 128:(4 + l) * 128], reads=(bAGso,), writes=(bRows,))
        SD("dma_start", out=cpx[0:1, 2:3], in_=arh_o, reads=(bARho,), writes=(bRows,))
        SE("activation", cpx[0:1, 4:5], cpx[0:1, 0:1], AF.Tanh, scale=0.5,
           reads=(bRows,), writes=(bRows,))
        VE("tensor_scalar", cpx[0:1, 5:6], cpx[0:1, 4:5], 0.5, 0.5, ALU.mult, ALU.add,
           reads=(bRows,), writes=(bRows,))
        VE("tensor_tensor", cpx[0:1, 6:7], cpx[0:1, 5:6], cpx[0:1, 2:3], ALU.mult,
           reads=(bRows,), writes=(bRows,))
        VE("tensor_scalar", cpx[0:1, 6:7], cpx[0:1, 6:7], 1.0 / 127.0, None, ALU.mult,
           reads=(bRows,), writes=(bRows,))
        VE("reciprocal", cpx[0:1, 7:8], cpx[0:1, 6:7], reads=(bRows,), writes=(bRows,))
        T("matmul", PF[0][:, 0:1], onesrow, cpx[0:1, 7:8], start=True, stop=True,
          reads=(bRows, bConst), writes=(pfb[0],))
        SE("copy", cols[:, 16:17], PF[0][:, 0:1], reads=(pfb[0],), writes=(bCols,))
        VE("tensor_scalar", sg_row, sg_row, cpx[0:1, 6:7], None, ALU.mult,
           reads=(bRows,), writes=(bRows,))
        VE("tensor_scalar", su_row, su_row, cpx[0:1, 6:7], None, ALU.mult,
           reads=(bRows,), writes=(bRows,))
        VE("tensor_scalar", su_row, su_row, 0.5, None, ALU.mult, reads=(bRows,), writes=(bRows,))
        for (row, dstb) in ((sg_row, sxg_b), (su_row, sxu_b)):
            for (s0, s1) in SLABS:
                T("matmul", PF[1][:, 0:s1 - s0], onesrow, row[0:1, s0:s1],
                  start=True, stop=True, reads=(bRows, bConst), writes=(pfb[1],))
                SE("copy", dstb[:, s0:s1], PF[1][:, 0:s1 - s0], reads=(pfb[1],), writes=(bSx,))

        SD("dma_start", out=bhs[0], in_=h2_s[0:TILE, :], reads=(bH2S,), writes=(bBh[0],))
        for i in range(NT):
            par = i % 2
            ht, bHt = bhs[par], bBh[par]
            tt, bTt = bts[par], bBt[par]
            if i + 1 < NT:
                SD("dma_start", out=bhs[(i + 1) % 2], in_=h2_s[(i + 1) * TILE:(i + 2) * TILE, :],
                   reads=(bH2S,), writes=(bBh[(i + 1) % 2],))
            VE("tensor_scalar", bq1, ht, cols[:, 16:17], CRND, ALU.mult, ALU.add,
               reads=(bHt, bCols), writes=(bBq1,))
            VE("tensor_scalar", bq1, bq1, CRND, 127.0, ALU.subtract, ALU.min,
               reads=(bBq1,), writes=(bBq1,))
            VE("tensor_scalar", bxq, bq1, -128.0, None, ALU.max, reads=(bBq1,), writes=(bBxq,))
            for c in range(8):
                T("transpose", PB[0][:, c * 128:(c + 1) * 128], bxq[:, c * 128:(c + 1) * 128],
                  identb, reads=(bBxq, bConst), writes=(pbb[0],))
            SE("copy", bxqf, PB[0][:, :], reads=(pbb[0],), writes=(bBxqf,))
            if dbg and i < 4:
                SD("dma_start", out=dbg_bxq[i], in_=bxq, reads=(bBxq,), writes=(NB("dbgd"),))
            if dbg and i == 0:
                SD("dma_start", out=dbg_wqg, in_=wqg, reads=(bWq,), writes=(NB("dbgd"),))
            for si_, (s0, s1) in enumerate(SLABS):
                n = s1 - s0
                gb_i = 2 + 2 * (si_ % 2)
                ub_i = 3 + 2 * (si_ % 2)
                for kc in range(8):
                    T("matmul", PF[gb_i][:, 0:n], bxqf[:, kc * 128:(kc + 1) * 128],
                      wqg.rearrange("p (c f) -> p c f", c=8)[:, kc, s0:s1],
                      start=(kc == 0), stop=(kc == 7), reads=(bBxqf, bWq), writes=(pfb[gb_i],))
                for kc in range(8):
                    T("matmul", PF[ub_i][:, 0:n], bxqf[:, kc * 128:(kc + 1) * 128],
                      wqu.rearrange("p (c f) -> p c f", c=8)[:, kc, s0:s1],
                      start=(kc == 0), stop=(kc == 7), reads=(bBxqf, bWq), writes=(pfb[ub_i],))
                VE("tensor_tensor", bgr[:, 0:n], PF[gb_i][:, 0:n], sxg_b[:, s0:s1], ALU.mult,
                   reads=(pfb[gb_i], bSx), writes=(bBgr,))
                VE("tensor_tensor", bur[:, 0:n], PF[ub_i][:, 0:n], sxu_b[:, s0:s1], ALU.mult,
                   reads=(pfb[ub_i], bSx), writes=(bBur,))
                SE("activation", bth[:, 0:n], bgr[:, 0:n], AF.Tanh, scale=0.5,
                   reads=(bBgr,), writes=(bBth,))
                VE("tensor_tensor", bur[:, 0:n], bur[:, 0:n], bgr[:, 0:n], ALU.mult,
                   reads=(bBur, bBgr), writes=(bBur,))
                VE("scalar_tensor_tensor", tt[:, s0:s1], bth[:, 0:n], 1.0, bur[:, 0:n],
                   ALU.add, ALU.mult, reads=(bBth, bBur),
                   writes=(bTt, bRows) if i < 2 and s0 == 0 else (bTt,))
                VE("reduce_max", runmax[:, 2 + si_:3 + si_], tt[:, s0:s1], axis=AX,
                   apply_absolute_value=True, reads=(bTt,), writes=(bRmx,))
            for si_ in range(6):
                VE("tensor_tensor", runmax[:, 1:2], runmax[:, 1:2], runmax[:, 2 + si_:3 + si_],
                   ALU.max, reads=(bRmx,), writes=(bRmx,))
            SD("dma_start", out=t_s[i * TILE:(i + 1) * TILE, :], in_=tt, reads=(bTt,), writes=(bTS,))
        T("transpose", PF[0][0:1, 0:128], runmax[:, 1:2], identf, reads=(bRmx, bConst),
          writes=(pfb[0],), force_drain=True)
        SE("copy", stage, PF[0][0:1, 0:128], reads=(pfb[0],), writes=(bStage,), force_drain=True)
        VE("reduce_max", scrow[0:1, 15:16], stage, axis=AX, reads=(bStage,), writes=(bRows,))
        GD("dma_start", out=art_i, in_=scrow[0:1, 15:16], reads=(bRows,), writes=(bARt,))
        GD("collective_compute", "AllReduce", ALU.max, [list(range(NCORES))],
           [art_i], [art_o], reads=(bARt,), writes=(bARto,), ch="c")

        P.barrier()

        for r in range(NCORES):
            SD("dma_start",
               out=wqd.rearrange("p (c f) -> p c f", c=NGT)[:, :, r * 128:(r + 1) * 128],
               in_=agw_o[r, :, 6144:6144 + NGT * 128].rearrange("p (c j) -> p c j", c=NGT),
               reads=(bAGwo,), writes=(bWq,))
        for r in range(NCORES):
            SD("dma_start", out=sd_row[0:1, r * 128:(r + 1) * 128],
               in_=ags_o[r, 6 * 128:7 * 128], reads=(bAGso,), writes=(bRows,))
        SD("dma_start", out=cpx[0:1, 3:4], in_=art_o, reads=(bARto,), writes=(bRows,))
        SE("activation", cpx[0:1, 8:9], cpx[0:1, 1:2], AF.Tanh, scale=0.5, reads=(bRows,), writes=(bRows,))
        VE("tensor_scalar", cpx[0:1, 9:10], cpx[0:1, 8:9], 0.5, 0.5, ALU.mult, ALU.add,
           reads=(bRows,), writes=(bRows,))
        VE("tensor_tensor", cpx[0:1, 10:11], cpx[0:1, 9:10], cpx[0:1, 3:4], ALU.mult,
           reads=(bRows,), writes=(bRows,))
        VE("tensor_scalar", cpx[0:1, 10:11], cpx[0:1, 10:11], 1.0 / 127.0, None, ALU.mult,
           reads=(bRows,), writes=(bRows,))
        VE("reciprocal", cpx[0:1, 11:12], cpx[0:1, 10:11], reads=(bRows,), writes=(bRows,))
        T("matmul", PF[0][:, 0:1], onesrow, cpx[0:1, 11:12], start=True, stop=True,
          reads=(bRows, bConst), writes=(pfb[0],))
        SE("copy", cols[:, 15:16], PF[0][:, 0:1], reads=(pfb[0],), writes=(bCols,))
        VE("tensor_scalar", sd_row, sd_row, cpx[0:1, 10:11], None, ALU.mult,
           reads=(bRows,), writes=(bRows,))
        for sl in range(2):
            T("matmul", PF[1][:, 0:512], onesrow, sd_row[0:1, sl * 512:(sl + 1) * 512],
              start=True, stop=True, reads=(bRows, bConst), writes=(pfb[1],))
            SE("copy", co_t[:, sl * 512:(sl + 1) * 512], PF[1][:, 0:512],
               reads=(pfb[1],), writes=(bCo,))
        VE("tensor_tensor", sdtgb_b, co_t, gb_b, ALU.mult, reads=(bCo, bMod), writes=(bSdt,))

        SD("dma_start", out=cts[0], in_=t_s[0:TILE, :], reads=(bTS,), writes=(bCt[0], bRows))
        SD("dma_start", out=cxs[0], in_=x2_s[0:TILE, :], reads=(bX2S,), writes=(bCx[0],))
        for i in range(NT):
            par = i % 2
            tt, bTt = cts[par], bCt[par]
            xx, bXx = cxs[par], bCx[par]
            if i + 1 < NT:
                SD("dma_start", out=cts[(i + 1) % 2], in_=t_s[(i + 1) * TILE:(i + 2) * TILE, :],
                   reads=(bTS,), writes=(bCt[(i + 1) % 2],))
                SD("dma_start", out=cxs[(i + 1) % 2], in_=x2_s[(i + 1) * TILE:(i + 2) * TILE, :],
                   reads=(bX2S,), writes=(bCx[(i + 1) % 2],))
            VE("tensor_scalar", cq1, tt, cols[:, 15:16], CRND, ALU.mult, ALU.add,
               reads=(bTt, bCols), writes=(bCq1,))
            VE("tensor_scalar", cq1, cq1, CRND, 127.0, ALU.subtract, ALU.min,
               reads=(bCq1,), writes=(bCq1,))
            VE("tensor_scalar", chq, cq1, -128.0, None, ALU.max, reads=(bCq1,), writes=(bChq,))
            for c in range(NGT):
                pbi = c % 2
                slot = (c // 2) % 4
                T("transpose", PB[pbi][:, slot * 128:(slot + 1) * 128],
                  chq[:, c * 128:(c + 1) * 128], identb, reads=(bChq, bConst), writes=(pbb[pbi],))
                SE("copy", chqf[:, c * 128:(c + 1) * 128], PB[pbi][:, slot * 128:(slot + 1) * 128],
                   reads=(pbb[pbi],), writes=(bChqf,))
            for half in range(2):
                bi_ = 2 + half + 2 * (i % 2)
                bankk = pfb[bi_]
                pf = PF[bi_]
                for kc in range(NGT):
                    T("matmul", pf[:, 0:512], chqf[:, kc * 128:(kc + 1) * 128],
                      wqd.rearrange("p (c f) -> p c f", c=NGT)[:, kc, half * 512:(half + 1) * 512],
                      start=(kc == 0), stop=(kc == NGT - 1), reads=(bChqf, bWq), writes=(bankk,))
                VE("tensor_tensor", co_t[:, half * 512:(half + 1) * 512], pf[:, 0:512],
                   sdtgb_b[:, half * 512:(half + 1) * 512], ALU.mult,
                   reads=(bankk, bSdt), writes=(bCo,))
            if dbg and i < 4:
                SD("dma_start", out=dbg_chq[i], in_=chq, reads=(bChq,), writes=(NB("dbgd"),))
            if dbg and i == 0:
                SD("dma_start", out=dbg_wqd, in_=wqd, reads=(bWq,), writes=(NB("dbgd"),))
                SD("dma_start", out=dbg_sdt, in_=sdtgb_b, reads=(bSdt,), writes=(NB("dbgd"),))
                SD("dma_start", out=dbg_cpx, in_=cpx, reads=(bRows,), writes=(NB("dbgd"),))
            VE("tensor_tensor", co_t, co_t, xx, ALU.add, reads=(bCo, bXx), writes=(bCo,))
            SD("dma_start", out=out[i * TILE:(i + 1) * TILE, :], in_=co_t,
               reads=(bCo,), writes=(bOutD,))
        for e in ("v", "s", "t", "g"):
            P._wait(e, bOutD.wr)
        P.final_wait("d")

    with ExitStack() as ctx:
        e = ctx.enter_context
        pool_t = e(nc.sbuf_tensor("pool", [128, POOLB], mybir.dt.uint8))
        pool = pool_t
        PF = [e(nc.psum_tensor(f"pf{i}", [128, 512], F32)) for i in range(6)]
        PB = [e(nc.psum_tensor(f"pb{i}", [128, 1024], BF16)) for i in range(2)]
        sems = {n: e(nc.semaphore(f"sem_{n}"))
                for n in ("v", "s", "t", "d", "g", "c", "de", "ge")}
        sem_pool = [e(nc.semaphore(f"sp{i}")) for i in range(40)]
        block = e(nc.Block())

        def mk_chans():
            ch = {n: CH(sems[n], n) for n in
                  ("v", "s", "t", "d", "g", "c", "de", "ge")}
            ch["de"].owner = "none"
            ch["ge"].owner = "none"
            return ch

        @block.tensor
        def _(eng):
            program(K("t", {"t": eng}, mk_chans(), sem_pool), pool, PF, PB)

        @block.vector
        def _(eng):
            program(K("v", {"v": eng}, mk_chans(), sem_pool), pool, PF, PB)

        @block.scalar
        def _(eng):
            program(K("s", {"s": eng}, mk_chans(), sem_pool), pool, PF, PB)

        @block.sync
        def _(eng):
            program(K("d", {"d": eng}, mk_chans(), sem_pool), pool, PF, PB)

        @block.gpsimd
        def _(eng):
            program(K("g", {"g": eng}, mk_chans(), sem_pool), pool, PF, PB)

    return nc


_NC_CACHE = {}


def _get_nc():
    if "nc" not in _NC_CACHE:
        _NC_CACHE["nc"] = build_nc(dbg=_NC_CACHE.get("dbg", False))
    return _NC_CACHE["nc"]


_RES_STASH = {}


def kernel(**inputs):
    f32 = np.float32
    x = np.asarray(inputs["L0_attn_hidden_states"], dtype=f32)
    W_x0 = np.asarray(inputs["L0_attn_W_x0"], f32)
    W_x2 = np.asarray(inputs["L0_attn_W_x2"], f32)
    x_bias = np.asarray(inputs["L0_attn_x_bias"], f32)
    W_r = np.asarray(inputs["L0_attn_W_r"], f32)
    W_w1 = np.asarray(inputs["L0_attn_W_w1"], f32)
    W_w2 = np.asarray(inputs["L0_attn_W_w2"], f32)
    b_w2 = np.asarray(inputs["L0_attn_b_w2"], f32)
    W_k = np.asarray(inputs["L0_attn_W_k"], f32)
    W_v = np.asarray(inputs["L0_attn_W_v"], f32)
    W_g = np.asarray(inputs["L0_attn_W_g"], f32)
    W_o = np.asarray(inputs["L0_attn_W_o"], f32)
    mu0 = np.asarray(inputs["L0_attn_mu0"], f32)
    bonus = np.asarray(inputs["L0_attn_bonus"], f32)
    gnw = np.asarray(inputs["L0_attn_gn_weight"], f32)
    gnb = np.asarray(inputs["L0_attn_gn_bias"], f32)
    clip_x = np.asarray(inputs["L0_ffn_clip_x"], f32)
    clip_d = np.asarray(inputs["L0_ffn_clip_down"], f32)
    w_gate = np.asarray(inputs["L0_ffn_w_gate"], f32)
    w_up = np.asarray(inputs["L0_ffn_w_up"], f32)
    w_down = np.asarray(inputs["L0_ffn_w_down"], f32)
    L0_c = np.asarray(inputs["L0_c"], f32)
    wmod = np.asarray(inputs["L0_wmod"], f32)

    def kchunk(wT):
        Kd, Fd = wT.shape
        return np.ascontiguousarray(
            wT.reshape(Kd // 128, 128, Fd).transpose(1, 0, 2).reshape(128, -1))

    wx0T = kchunk(np.ascontiguousarray(W_x0.T))
    wx2T = np.ascontiguousarray(
        np.concatenate([W_x2.reshape(H, 5, 32).transpose(1, 2, 0),
                        x_bias[:, None, :]], axis=1).transpose(1, 0, 2).reshape(33, 5 * H))
    wrT = kchunk(np.ascontiguousarray(W_r.T))
    ww1T = kchunk(np.ascontiguousarray(W_w1.T))
    ww2T = np.ascontiguousarray(np.concatenate([W_w2.T, b_w2[None, :]], axis=0))
    wkT = kchunk(np.ascontiguousarray(W_k.T))
    wvT = kchunk(np.ascontiguousarray(W_v.T))
    wgT = kchunk(np.ascontiguousarray(W_g.T))
    woT = kchunk(np.ascontiguousarray(W_o.T))
    mu0fm = np.ascontiguousarray(mu0.reshape(8, 128).T)
    bonusT = np.ascontiguousarray(bonus.T)
    identb = np.eye(128, dtype=ml_dtypes.bfloat16)
    identf = np.eye(128, dtype=np.float32)
    base = np.tril(np.ones((64, 64), dtype=f32), -1).T
    m128 = np.zeros((128, 512), dtype=f32)
    for g in range(8):
        m128[0:64, g * 64:(g + 1) * 64] = base
        m128[64:128, g * 64:(g + 1) * 64] = base
    m128 = m128.astype(np.uint8)
    onesrow = np.ones((1, 128), dtype=f32)
    wgp = np.concatenate([w_gate, np.ones((24 * 128 - IFFN, H), f32)], axis=0)
    wup = np.concatenate([w_up, np.ones((24 * 128 - IFFN, H), f32)], axis=0)

    nc = _get_nc()
    in_maps = []
    for c in range(NCORES):
        b, hf = c // 2, c % 2
        t0 = hf * TCORE
        xprev = np.zeros((1, H), f32) if hf == 0 else np.ascontiguousarray(x[b, t0 - 1:t0, :])
        in_maps.append({
            "x": np.ascontiguousarray(x[b, t0:t0 + TCORE, :]),
            "xprev": xprev,
            "cvec": np.ascontiguousarray(L0_c[b:b + 1, :]),
            "wmod": wmod,
            "gnw": np.ascontiguousarray(np.broadcast_to(gnw[None, :], (128, VD))),
            "gnb": np.ascontiguousarray(np.broadcast_to(gnb[None, :], (128, VD))),
            "mu0": mu0fm, "bonus": bonusT,
            "pmask": np.full((128, 1), float(hf), f32),
            "cmb": np.full((128, 1), 0.0 if hf == 1 else -1e9, f32),
            "clipx": clip_x.reshape(1, 1), "clipd": clip_d.reshape(1, 1),
            "identb": identb, "identf": identf, "mask128": m128, "onesrow": onesrow,
            "wx0T": wx0T, "wx2T": wx2T, "wrT": wrT, "ww1T": ww1T, "ww2T": ww2T,
            "wkT": wkT, "wvT": wvT, "wgT": wgT, "woT": woT,
            "wg_nat": np.ascontiguousarray(wgp[384 * c:384 * (c + 1)]),
            "wu_nat": np.ascontiguousarray(wup[384 * c:384 * (c + 1)]),
            "wd_nat": np.ascontiguousarray(w_down[128 * c:128 * (c + 1)]),
        })
    res = run_bass_kernel_spmd(nc, in_maps, core_ids=list(range(NCORES)))
    _RES_STASH["res"] = res
    outp = np.empty((B, TFULL, H), f32)
    for c in range(NCORES):
        b, hf = c // 2, c % 2
        outp[b, hf * TCORE:(hf + 1) * TCORE, :] = res.results[c]["out"]
    return outp


# version 07
# speedup vs baseline: 1.0080x; 1.0069x over previous
import numpy as np
import ml_dtypes
from contextlib import ExitStack

import concourse.bass as bass
import concourse.mybir as mybir
from concourse.bass_utils import run_bass_kernel_spmd

F32 = mybir.dt.float32
BF16 = mybir.dt.bfloat16
I32 = mybir.dt.int32
AF = mybir.ActivationFunctionType
ALU = mybir.AluOpType
AX = mybir.AxisListType.X

B, TFULL, H = 4, 8192, 1024
NCORES = 8
TCORE = 4096
TILE = 128
NT = TCORE // TILE
NCORR = 8
NH, DK, DV = 4, 128, 256
KD, VD = 512, 1024
IFFN = 2816
NGT = 22
CRND = 12582912.0
RSMAGIC = 1597463007
EPS = 1e-5
POOLB = 209920
SLABS = [(0, 512), (512, 1024), (1024, 1536), (1536, 2048), (2048, 2560), (2560, 2816)]


class CH:
    def __init__(self, sem, owner):
        self.sem = sem
        self.owner = owner
        self.n = 0


class Buf:
    __slots__ = ("name", "wr", "rd")

    def __init__(self, name):
        self.name = name
        self.wr = None
        self.rd = {}


class K:
    def __init__(self, active, handles, chans, sem_pool):
        self.active = active
        self.h = handles
        self.ch = chans
        self.sem_pool = sem_pool
        self.sem_next = 0
        self.buf_ch = {}
        self.qchans = {"d": [chans["d"]], "g": [chans["g"]]}
        self.waited = {e: {} for e in ("v", "s", "t", "d", "g")}

    def _buf_chan(self, name, qeng):
        if name not in self.buf_ch:
            assert self.sem_next < len(self.sem_pool), "sem pool exhausted"
            c_ = CH(self.sem_pool[self.sem_next], "none")
            self.sem_next += 1
            self.buf_ch[name] = c_
        c_ = self.buf_ch[name]
        if c_ not in self.qchans[qeng]:
            self.qchans[qeng].append(c_)
        return c_

    def _wait(self, eng, ticket):
        if ticket is None:
            return
        ch, val = ticket
        if ch.owner == eng and eng in ("v", "s", "t"):
            return
        w = self.waited[eng]
        if w.get(ch, 0) >= val:
            return
        w[ch] = val
        if self.active == eng:
            self.h[eng].wait_ge(ch.sem, val)

    def op(self, eng, f, *args, reads=(), writes=(), ch=None, force_drain=False, **kw):
        for b in reads:
            self._wait(eng, b.wr)
        for b in writes:
            self._wait(eng, b.wr)
            for c_, val in b.rd.items():
                self._wait(eng, (c_, val))
        chan = None
        if f == "dma_start" and ch is None and writes:
            try:
                if "DRam" not in type(kw["out"].tensor).__name__:
                    chan = self._buf_chan(writes[0].name, eng)
            except Exception:
                chan = None
        if chan is None:
            chan = self.ch[ch or eng]
        inc = 16 if f == "dma_start" else 1
        if self.active == eng:
            inst = getattr(self.h[eng], f)(*args, **kw)
            small = False
            if eng in ("v", "s", "t") and f != "dma_start":
                try:
                    small = args[0].free_size < 64
                except Exception:
                    small = True
            if small or force_drain:
                tail = self.h[eng].drain()
                tail.then_inc(chan.sem, inc)
            else:
                inst.then_inc(chan.sem, inc)
        chan.n += inc
        t = (chan, chan.n)
        for b in writes:
            b.wr = t
            b.rd = {}
        for b in reads:
            if b.rd.get(chan, 0) < chan.n:
                b.rd[chan] = chan.n
        return t

    def barrier(self):
        for qe, ech in (("d", "de"), ("g", "ge")):
            if self.active == qe:
                for c_ in self.qchans[qe]:
                    if c_.n > self.waited[qe].get(c_, 0):
                        self.waited[qe][c_] = c_.n
                        self.h[qe].wait_ge(c_.sem, c_.n)
                self.h[qe].sem_inc(self.ch[ech].sem, 1)
            self.ch[ech].n += 1
        for e in ("v", "s", "t", "d", "g"):
            for cn, c_ in self.ch.items():
                if cn in ("d", "g", "dl", "gl"):
                    continue
                self._wait(e, (c_, c_.n))

    def final_wait(self, eng):
        c_ = self.ch[eng]
        if self.active == eng:
            self.h[eng].wait_ge(c_.sem, c_.n)


def build_nc(dbg=False):
    nc = bass.Bass("TRN2", target_bir_lowering=False)
    d = {}

    def din(name, shape, dtype=F32):
        d[name] = nc.dram_tensor(name, list(shape), dtype, kind="ExternalInput").ap()
        return d[name]

    x_in = din("x", (TCORE, H))
    xprev_in = din("xprev", (1, H))
    cvec_in = din("cvec", (1, H))
    wmod_in = din("wmod", (H, 6 * H))
    gnw_in = din("gnw", (128, VD))
    gnb_in = din("gnb", (128, VD))
    mu0_in = din("mu0", (128, 8))
    mu0r_in = din("mu0r", (128, H), BF16)
    bonus_in = din("bonus", (128, NH))
    pmask_in = din("pmask", (128, 1))
    cmb_in = din("cmb", (128, 1))
    clipx_in = din("clipx", (1, 1))
    clipd_in = din("clipd", (1, 1))
    identb_in = din("identb", (128, 128), BF16)
    identf_in = din("identf", (128, 128))
    mask128_in = din("mask128", (128, 512), mybir.dt.uint8)
    onesrow_in = din("onesrow", (1, 128))
    wx0_in = din("wx0T", (128, 8 * 160))
    wx2_in = din("wx2T", (33, 5 * H))
    wr_in = din("wrT", (128, 8 * KD))
    ww1_in = din("ww1T", (128, 8 * 64))
    ww2_in = din("ww2T", (65, KD))
    wk_in = din("wkT", (128, 8 * KD))
    wv_in = din("wvT", (128, 8 * VD))
    wg_in = din("wgT", (128, 8 * VD))
    wo_in = din("woT", (128, 8 * H))
    wgn_in = din("wg_nat", (384, H))
    wun_in = din("wu_nat", (384, H))
    wdn_in = din("wd_nat", (128, IFFN))

    dk = dict(kind="ExternalOutput") if dbg else {}
    out = nc.dram_tensor("out", [TCORE, H], F32, kind="ExternalOutput").ap()
    os_s = nc.dram_tensor("os_s", [TCORE, H], F32, **dk).ap()
    gs_s = nc.dram_tensor("gs_s", [TCORE, H], BF16, **dk).ap()
    rfs_s = nc.dram_tensor("rfs_s", [NCORR * TILE, KD], BF16, **dk).ap()
    x2_s = nc.dram_tensor("x2_s", [TCORE, H], F32, **dk).ap()
    h2_s = nc.dram_tensor("h2_s", [TCORE, H], F32, **dk).ap()
    t_s = nc.dram_tensor("t_s", [TCORE, IFFN], F32, **dk).ap()
    agp_i = nc.dram_tensor("agp_i", [NH * DK * DV], F32).ap()
    agp_o = nc.dram_tensor("agp_o", [2, NH * DK * DV], F32).ap()
    WQB = 6 * 1024 + NGT * 128
    agw_i = nc.dram_tensor("agw_i", [128, WQB], BF16).ap()
    agw_o = nc.dram_tensor("agw_o", [NCORES, 128, WQB], BF16).ap()
    ags_i = nc.dram_tensor("ags_i", [7 * 128], F32).ap()
    ags_o = nc.dram_tensor("ags_o", [NCORES, 7 * 128], F32).ap()
    if dbg:
        dbg_h = nc.dram_tensor("dbg_h", [NT, 128, H], BF16, kind="ExternalOutput").ap()
        dbg_xp = nc.dram_tensor("dbg_xp", [NT, 33, 640], BF16, kind="ExternalOutput").ap()
        dbg_xi5 = nc.dram_tensor("dbg_xi5", [NT, 128, H], BF16, kind="ExternalOutput").ap()
        dbg_pw = nc.dram_tensor("dbg_pw", [NT, 128, 512], F32, kind="ExternalOutput").ap()
        dbg_r = nc.dram_tensor("dbg_r", [NT, 128, 512], BF16, kind="ExternalOutput").ap()
        dbg_v = nc.dram_tensor("dbg_v", [NT, 128, H], BF16, kind="ExternalOutput").ap()
        dbg_xt = nc.dram_tensor("dbg_xt", [NT, 128, H], F32, kind="ExternalOutput").ap()
        dbg_cols = nc.dram_tensor("dbg_cols", [NT, 128, 48], F32, kind="ExternalOutput").ap()
        dbg_xh = nc.dram_tensor("dbg_xh", [NT, 128, H], BF16, kind="ExternalOutput").ap()
        dbg_am = nc.dram_tensor("dbg_am", [NT, 128, 512], BF16, kind="ExternalOutput").ap()
        dbg_kt = nc.dram_tensor("dbg_kt", [NT, 128, 512], BF16, kind="ExternalOutput").ap()
        dbg_rt = nc.dram_tensor("dbg_rt", [NT, 128, 512], BF16, kind="ExternalOutput").ap()
        dbg_kht = nc.dram_tensor("dbg_kht", [NT, 128, 512], BF16, kind="ExternalOutput").ap()
        dbg_rh = nc.dram_tensor("dbg_rh", [NT, 128, 512], BF16, kind="ExternalOutput").ap()
        dbg_ds = nc.dram_tensor("dbg_ds", [NT, 128, 16], F32, kind="ExternalOutput").ap()
        dbg_S = nc.dram_tensor("dbg_S", [NT, 128, H], BF16, kind="ExternalOutput").ap()
        dbg_pc = nc.dram_tensor("dbg_pc", [NT, 128, 520], F32, kind="ExternalOutput").ap()
        dbg_chq = nc.dram_tensor("dbg_chq", [4, 128, IFFN], BF16, kind="ExternalOutput").ap()
        dbg_wqd = nc.dram_tensor("dbg_wqd", [128, NGT * H], BF16, kind="ExternalOutput").ap()
        dbg_sdt = nc.dram_tensor("dbg_sdt", [128, H], BF16, kind="ExternalOutput").ap()
        dbg_cpx = nc.dram_tensor("dbg_cpx", [1, 16], F32, kind="ExternalOutput").ap()
        dbg_bxq = nc.dram_tensor("dbg_bxq", [4, 128, H], BF16, kind="ExternalOutput").ap()
        dbg_wqg = nc.dram_tensor("dbg_wqg", [128, 8 * IFFN], BF16, kind="ExternalOutput").ap()
        dbg_mod = nc.dram_tensor("dbg_mod", [6, 128, H], BF16, kind="ExternalOutput").ap()
        dbg_fm = nc.dram_tensor("dbg_fm", [128, 16], F32, kind="ExternalOutput").ap()
    arh_i = nc.dram_tensor("arh_i", [1], F32).ap()
    arh_o = nc.dram_tensor("arh_o", [1], F32).ap()
    art_i = nc.dram_tensor("art_i", [1], F32).ap()
    art_o = nc.dram_tensor("art_o", [1], F32).ap()

    def program(P, pool, PF, PB):
        def V(off, parts, n, dt):
            esz = 4 if dt in (F32, I32) else (1 if dt == mybir.dt.uint8 else 2)
            assert off + n * esz <= POOLB
            return pool[0:parts, off:off + n * esz].bitcast(dt)

        class Reg:
            def __init__(self, base):
                self.off = base

            def a(self, nbytes, align=64):
                o = (self.off + align - 1) // align * align
                self.off = o + nbytes
                assert self.off <= POOLB, f"pool overflow at {self.off}"
                return o

        pr = Reg(0)
        identb = V(pr.a(256), 128, 128, BF16)
        identf = V(pr.a(512), 128, 128, F32)
        mask128 = V(pr.a(512), 128, 512, mybir.dt.uint8)
        zeros64 = V(pr.a(256), 128, 64, F32)
        zb512 = V(pr.a(1024), 128, 512, BF16)
        ones128 = V(pr.a(512), 128, 128, F32)
        onesrow = V(pr.a(512), 1, 128, F32)
        mu0 = V(pr.a(32), 128, 8, F32)
        mu0rep = V(pr.a(H * 2), 128, H, BF16)
        u_bf = V(pr.a(8), 128, NH, BF16)
        pmaskc = V(pr.a(4), 128, 1, F32)
        cmbc = V(pr.a(4), 128, 1, F32)
        sa_fm = V(pr.a(32), 128, 8, F32)
        ca1_fm = V(pr.a(32), 128, 8, F32)
        ga_b = V(pr.a(H * 2), 128, H, BF16)
        gb_b = V(pr.a(H * 2), 128, H, BF16)
        cb1_b = V(pr.a(H * 2), 128, H, BF16)
        sb_b = V(pr.a(H * 2), 128, H, BF16)
        gnwh_b = V(pr.a(H * 2), 128, H, BF16)
        gnbh_b = V(pr.a(H * 2), 128, H, BF16)
        oS0 = pr.a(H * 4); S0 = V(oS0, 128, H, F32)
        oS1 = pr.a(H * 4); S1 = V(oS1, 128, H, F32)
        S_bf = V(pr.a(H * 2), 128, H, BF16)
        Sin_bf = V(pr.a(H * 2), 128, H, BF16)
        pcum = V(pr.a(520 * 4), 128, 520, F32)
        eneg = V(pr.a(520 * 4), 128, 520, F32)
        negrun = V(pr.a(16), 128, 4, F32)
        prevcol = V(pr.a(16), 128, 8, BF16)
        ost = pr.a(256)
        stats = V(ost, 128, 64, F32)
        statsi = V(ost, 128, 64, I32)
        cols = V(pr.a(192), 128, 48, F32)
        runmax = V(pr.a(32), 128, 8, F32)
        osc = pr.a(64)
        scrow = V(osc, 1, 16, F32)
        scrowi = V(osc, 1, 16, I32)
        stage = V(pr.a(512), 1, 128, F32)
        cpx = V(pr.a(64), 1, 16, F32)

        WBASE = (pr.off + 63) // 64 * 64
        wa = Reg(WBASE)
        wx0 = V(wa.a(8 * 160 * 2), 128, 8 * 160, BF16)
        wx2 = V(wa.a(5 * H * 2), 33, 5 * H, BF16)
        wrT = V(wa.a(8 * KD * 2), 128, 8 * KD, BF16)
        ww1 = V(wa.a(8 * 64 * 2), 128, 8 * 64, BF16)
        ww2 = V(wa.a(KD * 2), 65, KD, BF16)
        wkT = V(wa.a(8 * KD * 2), 128, 8 * KD, BF16)
        wvT = V(wa.a(8 * VD * 2), 128, 8 * VD, BF16)
        wgT = V(wa.a(8 * VD * 2), 128, 8 * VD, BF16)
        assert wa.off <= WBASE + 90112, wa.off
        woT = V(WBASE + 90112, 128, 8 * H, BF16)
        wqg = V(WBASE, 128, 8 * IFFN, BF16)
        wqu = V(WBASE + 45056, 128, 8 * IFFN, BF16)
        wqd = V(WBASE, 128, NGT * H, BF16)
        WEND = WBASE + 90112 + 16384

        aa = Reg(WEND + 64)
        ABASE = aa.off
        oxt0 = aa.a(H * 4); xt0 = V(oxt0, 128, H, F32)
        oxt1 = aa.a(H * 4); xt1 = V(oxt1, 128, H, F32)
        xts = (xt0, xt1)
        osq = aa.a(H * 4); sqbuf = V(osq, 128, H, F32)
        oxh = aa.a(H * 2); xhat = V(oxh, 128, H, BF16)
        ohf = aa.a(H * 2); h_fm = V(ohf, 128, H, BF16)
        odl = aa.a(H * 2); delta = V(odl, 128, H, BF16)
        omu0 = aa.a(H * 2); omu1 = aa.a(H * 2)
        mu_fm = (V(omu0, 128, H, BF16), V(omu1, 128, H, BF16))
        oxi = [aa.a(H * 2) for _ in range(6)]
        xi = [V(o, 128, H, BF16) for o in oxi]
        xp_sb = V(aa.a(640 * 2), 33, 640, BF16)
        w1_sb = V(aa.a(128 * 2), 65, 128, BF16)
        pw = V(aa.a(512 * 4), 128, 512, F32)
        r_cm = V(aa.a(512 * 2), 128, 512, BF16)
        k_cm = V(aa.a(512 * 2), 128, 512, BF16)
        facR = V(aa.a(512 * 2), 128, 512, BF16)
        facK = V(aa.a(512 * 2), 128, 512, BF16)
        rt_b = V(aa.a(512 * 2), 128, 512, BF16)
        kt_b = V(aa.a(512 * 2), 128, 512, BF16)
        khc_b = V(aa.a(512 * 2), 128, 512, BF16)
        khat_tm = V(aa.a(512 * 2), 128, 512, BF16)
        rhat_b = V(aa.a(512 * 2), 128, 512, BF16)
        rf_b = V(aa.a(512 * 2), 128, 512, BF16)
        rk_b = V(aa.a(512 * 2), 128, 512, BF16)
        am_b = V(aa.a(512 * 2), 128, 512, BF16)
        v_tm = V(aa.a(H * 2), 128, H, BF16)
        g_tm = V(aa.a(H * 2), 128, H, BF16)
        o_sb = V(aa.a(H * 4), 128, H, F32)
        d_sb = V(aa.a(64), 128, 16, F32)
        modt = (V(oxt0, 128, 512, F32), V(oxt1, 128, 512, F32))
        srep = V(osq, 128, 1024, F32)
        crow = V(ohf, 1, H, F32)
        xprow = V(omu0, 1, H, F32)
        mscrA = V(oxi[0], 128, H, F32)
        mscrB = V(oxi[2], 128, H, F32)
        mscrC = V(oxi[4], 128, H, F32)
        qnat = V(oxt0, 128, IFFN, F32)
        qint = V(ohf, 128, IFFN, BF16)
        qtrn = V(oxi[0], 128, IFFN, BF16)
        a3 = Reg(ABASE)
        a3os = (V(a3.a(H * 4), 128, H, F32), V(a3.a(H * 4), 128, H, F32))
        a3gs = (V(a3.a(H * 2), 128, H, BF16), V(a3.a(H * 2), 128, H, BF16))
        a3xs = (V(a3.a(H * 4), 128, H, F32), V(a3.a(H * 4), 128, H, F32))
        a3rfs = (V(a3.a(KD * 2), 128, KD, BF16), V(a3.a(KD * 2), 128, KD, BF16))
        a3sq = V(a3.a(H * 4), 128, H, F32)
        xo_b = V(a3.a(H * 2), 128, H, BF16)
        th_b = V(a3.a(H * 2), 128, H, BF16)
        sg2_b = V(a3.a(H * 2), 128, H, BF16)
        of_b = V(a3.a(H * 2), 128, H, BF16)
        of_fm = V(a3.a(H * 2), 128, H, BF16)
        x2_t = V(a3.a(H * 4), 128, H, F32)
        h2_t = V(a3.a(H * 4), 128, H, F32)
        t1_s = V(a3.a(H * 4), 128, H, F32)
        bb_ = Reg(ABASE)
        bhs = (V(bb_.a(H * 4), 128, H, F32), V(bb_.a(H * 4), 128, H, F32))
        bxq = V(bb_.a(H * 2), 128, H, BF16)
        bxqf = V(bb_.a(H * 2), 128, H, BF16)
        obt0 = bb_.a(IFFN * 4)
        obt1 = bb_.a(IFFN * 4)
        bts = (V(obt0, 128, IFFN, F32), V(obt1, 128, IFFN, F32))
        sg_row = V(obt0, 1, IFFN, F32)
        su_row = V(obt1, 1, IFFN, F32)
        bgr = V(bb_.a(512 * 4), 128, 512, F32)
        bur = V(bb_.a(512 * 4), 128, 512, F32)
        bth = V(bb_.a(512 * 2), 128, 512, BF16)
        bq1 = V(bb_.a(H * 4), 128, H, F32)
        sxg_b = V(bb_.a(IFFN * 2), 128, IFFN, BF16)
        sxu_b = V(bb_.a(IFFN * 2), 128, IFFN, BF16)
        cr = Reg(ABASE)
        oct0 = cr.a(IFFN * 4)
        cts = (V(oct0, 128, IFFN, F32), V(cr.a(IFFN * 4), 128, IFFN, F32))
        sd_row = V(oct0, 1, H, F32)
        cxs = (V(cr.a(H * 4), 128, H, F32), V(cr.a(H * 4), 128, H, F32))
        chq = V(cr.a(IFFN * 2), 128, IFFN, BF16)
        chqf = V(cr.a(IFFN * 2), 128, IFFN, BF16)
        cq1 = V(cr.a(IFFN * 4), 128, IFFN, F32)
        co_t = V(cr.a(H * 4), 128, H, F32)
        sdtgb_b = V(cr.a(H * 2), 128, H, BF16)

        BUFS = {}

        def NB(name):
            if name not in BUFS:
                BUFS[name] = Buf(name)
            return BUFS[name]

        pfb = [NB(f"PF{i}") for i in range(6)]
        pbb = [NB(f"PB{i}") for i in range(2)]

        def T(f, *a, **k):
            return P.op("t", f, *a, **k)

        def VE(f, *a, **k):
            return P.op("v", f, *a, **k)

        def SE(f, *a, **k):
            return P.op("s", f, *a, **k)

        def GD(f, *a, **k):
            return P.op("g", f, *a, **k)

        def SD(f, *a, **k):
            return P.op("d", f, *a, **k)

        bW = NB("wA"); bWq = NB("wQ"); bConst = NB("const"); bMod = NB("mod")
        bS = [NB("S0"), NB("S1")]
        bSbf = NB("Sbf"); bSin = NB("Sin"); bPcum = NB("pcum"); bEneg = NB("eneg")
        bNrun = NB("nrun"); bPrev = NB("prev"); bCols = NB("cols"); bStats = NB("stats")
        bRmx = NB("rmx"); bRows = NB("rows"); bStage = NB("stage")
        bXp = NB("xp"); bW1 = NB("w1"); bPw = NB("pw"); bRc = NB("rc"); bKc = NB("kc")
        bFacR = NB("facR"); bFacK = NB("facK"); bRt = NB("rt"); bKt = NB("kt")
        bKhc = NB("khc"); bKht = NB("kht"); bRh = NB("rh"); bRf = NB("rf")
        bRk = NB("rk"); bAm = NB("am"); bVt = NB("vt"); bGt = NB("gt"); bOt = NB("ot")
        bDsb = NB("dsb"); bHfm = NB("hfm"); bDelta = NB("dl"); bXhat = NB("xh")
        bSq = NB("sq"); bMu = [NB("mu0b"), NB("mu1b")]
        bXi = [NB(f"xi{i}") for i in range(6)]
        bXt = [NB("xt0b"), NB("xt1b")]
        bOS = NB("osS"); bGS = NB("gsS"); bRFS = NB("rfsS")
        bX2S = NB("x2S"); bH2S = NB("h2S"); bTS = NB("tS")
        bAGpi = NB("agpi"); bAGpo = NB("agpo"); bAGwi = NB("agwi"); bAGwo = NB("agwo")
        bAGsi = NB("agsi"); bAGso = NB("agso"); bARh = NB("arh"); bARho = NB("arho")
        bARt = NB("art"); bARto = NB("arto")
        bQn = NB("qn"); bQi = NB("qi"); bQt = NB("qt")
        bMs = [NB("msA"), NB("msB"), NB("msC")]
        bMt = [NB("mt0"), NB("mt1")]
        bSrep = NB("srep"); bCrow = NB("crow"); bXpr = NB("xpr")
        bA3o = [NB("a3o0"), NB("a3o1")]
        bA3g = [NB("a3g0"), NB("a3g1")]
        bA3x = [NB("a3x0"), NB("a3x1")]
        bA3rf = [NB("a3rf0"), NB("a3rf1")]
        bA3sq = NB("a3sq"); bXo = NB("xob"); bTh = NB("thb"); bSg2 = NB("sg2")
        bOf = NB("ofb"); bOffm = NB("offm"); bX2t = NB("x2t"); bH2t = NB("h2t")
        bT1 = NB("t1s")
        bBh = [NB("bh0"), NB("bh1")]; bBt = [NB("bt0"), NB("bt1")]
        bBxq = NB("bxq"); bBxqf = NB("bxqf"); bBgr = NB("bgr"); bBur = NB("bur")
        bBth = NB("bth"); bBq1 = NB("bq1"); bSx = NB("sxb")
        bCt = [NB("ct0"), NB("ct1")]; bCx = [NB("cx0"), NB("cx1")]
        bChq = NB("chq"); bChqf = NB("chqf"); bCq1 = NB("cq1"); bCo = NB("co")
        bSdt = NB("sdt"); bOutD = NB("outd")

        def rsqrt_newton(dst_col, var_col, ncols, iters=11):
            u = stats[:, 0:ncols]
            r_ = stats[:, 8:8 + ncols]
            h_ = stats[:, 12:12 + ncols]
            a_ = stats[:, 16:16 + ncols]
            ra = stats[:, 24:24 + ncols]
            t_ = stats[:, 32:32 + ncols]
            VE("tensor_scalar", u, var_col, EPS, None, ALU.add, reads=(bCols,), writes=(bStats,))
            VE("reciprocal", r_, u, reads=(bStats,), writes=(bStats,))
            VE("tensor_scalar", h_, r_, 0.5, None, ALU.mult, reads=(bStats,), writes=(bStats,))
            VE("tensor_scalar", a_, r_, 0.5, 0.5, ALU.mult, ALU.add, reads=(bStats,), writes=(bStats,))
            for _ in range(iters):
                VE("reciprocal", ra, a_, reads=(bStats,), writes=(bStats,))
                VE("tensor_tensor", t_, h_, ra, ALU.mult, reads=(bStats,), writes=(bStats,))
                VE("scalar_tensor_tensor", a_, a_, 0.5, t_, ALU.mult, ALU.add,
                   reads=(bStats,), writes=(bStats,))
            VE("tensor_copy", dst_col, a_, reads=(bStats,), writes=(bCols,))

        def ln_stats(src, b_src, rstd_col, negm_col, sqb, b_sq, ln_iters=7):
            VE("reduce_sum", cols[:, 40:41], src, axis=AX, reads=(b_src,), writes=(bCols,))
            SE("activation", sqb, src, AF.Square, accum_out=cols[:, 41:42],
               reads=(b_src,), writes=(b_sq, bCols))
            VE("tensor_scalar", cols[:, 42:43], cols[:, 40:41], 1.0 / H, None, ALU.mult,
               reads=(bCols,), writes=(bCols,))
            VE("tensor_scalar", cols[:, 43:44], cols[:, 41:42], 1.0 / H, None, ALU.mult,
               reads=(bCols,), writes=(bCols,))
            VE("tensor_scalar", cols[:, 44:45], cols[:, 42:43], -1.0, None, ALU.mult,
               reads=(bCols,), writes=(bCols,))
            VE("scalar_tensor_tensor", cols[:, 45:46], cols[:, 42:43], cols[:, 44:45],
               cols[:, 43:44], ALU.mult, ALU.add, reads=(bCols,), writes=(bCols,))
            rsqrt_newton(rstd_col, cols[:, 45:46], 1, iters=ln_iters)
            VE("tensor_tensor", cols[:, 46:47], cols[:, 42:43], rstd_col, ALU.mult,
               reads=(bCols,), writes=(bCols,))
            VE("tensor_scalar", negm_col, cols[:, 46:47], -1.0, None, ALU.mult,
               reads=(bCols,), writes=(bCols,))

        SD("dma_start", out=identb, in_=identb_in, writes=(bConst,))
        SD("dma_start", out=identf, in_=identf_in, writes=(bConst,))
        SD("dma_start", out=mask128, in_=mask128_in, writes=(bConst,))
        SD("dma_start", out=onesrow, in_=onesrow_in, writes=(bConst,))
        SD("dma_start", out=mu0, in_=mu0_in, writes=(bConst,))
        SD("dma_start", out=mu0rep, in_=mu0r_in, writes=(bConst,))
        SD("dma_start", out=pmaskc, in_=pmask_in, writes=(bConst,))
        SD("dma_start", out=cmbc, in_=cmb_in, writes=(bConst,))
        SD("dma_start", out=crow, in_=cvec_in, writes=(bCrow,))
        SD("dma_start", out=xprow, in_=xprev_in, writes=(bXpr,))
        GD("dma_start", out=gnwh_b, in_=gnw_in, writes=(bMod,))
        GD("dma_start", out=gnbh_b, in_=gnb_in, writes=(bMod,))
        SD("dma_start", out=cpx[0:1, 0:1], in_=clipx_in, writes=(bRows,))
        SD("dma_start", out=cpx[0:1, 1:2], in_=clipd_in, writes=(bRows,))
        GD("dma_start", out=wx0, in_=wx0_in, writes=(bW,))
        GD("dma_start", out=wx2, in_=wx2_in, writes=(bW,))
        GD("dma_start", out=wrT, in_=wr_in, writes=(bW,))
        GD("dma_start", out=ww1, in_=ww1_in, writes=(bW,))
        GD("dma_start", out=ww2, in_=ww2_in, writes=(bW,))
        GD("dma_start", out=wkT, in_=wk_in, writes=(bW,))
        GD("dma_start", out=wvT, in_=wv_in, writes=(bW,))
        GD("dma_start", out=wgT, in_=wg_in, writes=(bW,))
        GD("dma_start", out=woT, in_=wo_in, writes=(bW,))
        GD("dma_start", out=u_bf, in_=bonus_in, writes=(bW,))
        VE("memset", zeros64, 0.0, writes=(bConst,))
        VE("memset", zb512, 0.0, writes=(bConst,))
        VE("memset", ones128, 1.0, writes=(bConst,))
        VE("memset", S0, 0.0, writes=(bS[0],))
        VE("memset", S_bf, 0.0, writes=(bSbf,))
        VE("memset", runmax, 0.0, writes=(bRmx,))
        VE("memset", xp_sb[32:33, :], 1.0, writes=(bXp,))
        VE("memset", w1_sb[64:65, :], 1.0, writes=(bW1,))
        for g in range(8):
            VE("memset", pcum[:, g * 65:g * 65 + 1], 0.0, writes=(bPcum,))
        for hh in range(NH):
            VE("tensor_copy", negrun[:, hh:hh + 1], cmbc, reads=(bConst,), writes=(bNrun,))

        for c in range(8):
            T("transpose", PF[0][0:128, c:c + 1], crow[0:1, c * 128:(c + 1) * 128],
              identf[0:1, 0:1], reads=(bCrow, bConst), writes=(pfb[0],))
        SE("copy", stats[:, 48:56], PF[0][:, 0:8], reads=(pfb[0],), writes=(bStats,))
        cfm = stats[:, 48:56]
        SE("activation", stats[:, 56:64], cfm, AF.Tanh, scale=0.5, reads=(bStats,), writes=(bStats,))
        VE("tensor_scalar", cols[:, 0:8], stats[:, 56:64], 0.5, 0.5, ALU.mult, ALU.add,
           reads=(bStats,), writes=(bCols,))
        VE("tensor_tensor", cols[:, 8:16], cols[:, 0:8], cfm, ALU.mult,
           reads=(bCols, bStats), writes=(bCols,))
        for c in range(8):
            VE("tensor_scalar", srep[:, c * 128:(c + 1) * 128], ones128, cols[:, 8 + c:9 + c],
               None, ALU.mult, reads=(bConst, bCols), writes=(bSrep,))

        for s in range(12):
            mt = modt[s % 2]
            bmt = bMt[s % 2]
            for kc in range(8):
                SD("dma_start", out=mt,
                   in_=wmod_in[kc * 128:(kc + 1) * 128, s * 512:(s + 1) * 512], writes=(bmt,))
                if s < 4:
                    for grp in range(4):
                        T("matmul", PF[1 + grp][:, s * 8:(s + 1) * 8],
                          mt[:, grp * 128:(grp + 1) * 128], srep[:, 0:8],
                          start=(kc == 0), stop=(kc == 7),
                          reads=(bmt, bSrep), writes=(pfb[1 + grp],))
                else:
                    bankk = pfb[2 + (s % 2)]
                    T("matmul", PF[2 + (s % 2)][:, 0:512], srep[:, kc * 128:(kc + 1) * 128],
                      mt[:, 0:512], start=(kc == 0), stop=(kc == 7),
                      reads=(bmt, bSrep), writes=(bankk,))
            if s == 3:
                for c in range(8):
                    s_, grp = divmod(c, 4)
                    SE("copy", sa_fm[:, c:c + 1], PF[1 + grp][:, s_ * 8:s_ * 8 + 1],
                       reads=(pfb[1 + grp],), writes=(bMod,))
                for c in range(8):
                    s_, grp = divmod(8 + c, 4)
                    SE("activation", ca1_fm[:, c:c + 1], PF[1 + grp][:, s_ * 8:s_ * 8 + 1],
                       AF.Identity, bias=1.0, reads=(pfb[1 + grp],), writes=(bMod,))
            if s >= 4:
                bankk = pfb[2 + (s % 2)]
                pf = PF[2 + (s % 2)]
                j0 = (s - 4) * 512
                if j0 < 1024:
                    SE("copy", mscrA[:, j0:j0 + 512], pf[:, 0:512], reads=(bankk,), writes=(bMs[0],))
                elif j0 < 2048:
                    SE("copy", sb_b[:, j0 - 1024:j0 - 512], pf[:, 0:512], reads=(bankk,), writes=(bMod,))
                elif j0 < 3072:
                    SE("activation", cb1_b[:, j0 - 2048:j0 - 1536], pf[:, 0:512],
                       AF.Identity, bias=1.0, reads=(bankk,), writes=(bMod,))
                else:
                    SE("copy", mscrB[:, j0 - 3072:j0 - 2560], pf[:, 0:512], reads=(bankk,), writes=(bMs[1],))
        for (src, bsrc, dstb) in ((mscrA, bMs[0], ga_b), (mscrB, bMs[1], gb_b)):
            SE("activation", mscrC, src, AF.Abs, reads=(bsrc,), writes=(bMs[2],))
            VE("tensor_scalar", mscrC, mscrC, 0.25, None, ALU.max, reads=(bMs[2],), writes=(bMs[2],))
            VE("tensor_scalar", src, src, 0.0, None, ALU.is_ge, reads=(bsrc,), writes=(bsrc,))
            VE("tensor_scalar", src, src, 2.0, -1.0, ALU.mult, ALU.add, reads=(bsrc,), writes=(bsrc,))
            VE("tensor_tensor", dstb, mscrC, src, ALU.mult, reads=(bMs[2], bsrc), writes=(bMod,))

        VE("reduce_sum", scrow[0:1, 0:1], xprow, axis=AX, reads=(bXpr,), writes=(bRows,))
        SE("activation", mscrA[0:1, :], xprow, AF.Square, accum_out=scrow[0:1, 1:2],
           reads=(bXpr,), writes=(bMs[0], bRows))
        VE("tensor_scalar", scrow[0:1, 2:3], scrow[0:1, 0:1], 1.0 / H, None, ALU.mult,
           reads=(bRows,), writes=(bRows,))
        VE("tensor_scalar", scrow[0:1, 3:4], scrow[0:1, 1:2], 1.0 / H, None, ALU.mult,
           reads=(bRows,), writes=(bRows,))
        VE("tensor_scalar", scrow[0:1, 4:5], scrow[0:1, 2:3], -1.0, None, ALU.mult,
           reads=(bRows,), writes=(bRows,))
        VE("scalar_tensor_tensor", scrow[0:1, 5:6], scrow[0:1, 2:3], scrow[0:1, 4:5],
           scrow[0:1, 3:4], ALU.mult, ALU.add, reads=(bRows,), writes=(bRows,))
        VE("tensor_scalar", scrow[0:1, 6:7], scrow[0:1, 5:6], EPS, None, ALU.add,
           reads=(bRows,), writes=(bRows,))
        VE("reciprocal", scrow[0:1, 7:8], scrow[0:1, 6:7], reads=(bRows,), writes=(bRows,))
        VE("tensor_scalar", scrow[0:1, 8:9], scrow[0:1, 7:8], 0.5, 0.5, ALU.mult, ALU.add,
           reads=(bRows,), writes=(bRows,))
        for _ in range(11):
            VE("reciprocal", scrow[0:1, 10:11], scrow[0:1, 8:9], reads=(bRows,), writes=(bRows,))
            VE("tensor_tensor", scrow[0:1, 10:11], scrow[0:1, 10:11], scrow[0:1, 7:8], ALU.mult,
               reads=(bRows,), writes=(bRows,))
            VE("tensor_tensor", scrow[0:1, 8:9], scrow[0:1, 8:9], scrow[0:1, 10:11], ALU.add,
               reads=(bRows,), writes=(bRows,))
            VE("tensor_scalar", scrow[0:1, 8:9], scrow[0:1, 8:9], 0.5, None, ALU.mult,
               reads=(bRows,), writes=(bRows,))
        VE("tensor_tensor", scrow[0:1, 12:13], scrow[0:1, 2:3], scrow[0:1, 8:9], ALU.mult,
           reads=(bRows,), writes=(bRows,))
        VE("tensor_scalar", scrow[0:1, 13:14], scrow[0:1, 12:13], -1.0, None, ALU.mult,
           reads=(bRows,), writes=(bRows,))
        SE("activation", mscrB[0:1, :], xprow, AF.Identity, scale=scrow[0:1, 8:9],
           bias=scrow[0:1, 13:14], reads=(bXpr, bRows), writes=(bMs[1],))
        for c in range(8):
            T("transpose", PF[5][0:128, c:c + 1], mscrB[0:1, c * 128:(c + 1) * 128],
              identf[0:1, 0:1], reads=(bMs[1], bConst), writes=(pfb[5],))
        SE("copy", stats[:, 40:48], PF[5][:, 0:8], reads=(pfb[5],), writes=(bStats,))
        for c in range(8):
            VE("scalar_tensor_tensor", stats[:, 32 + c:33 + c], stats[:, 40 + c:41 + c],
               ca1_fm[:, c:c + 1], sa_fm[:, c:c + 1], ALU.mult, ALU.add,
               reads=(bStats, bMod), writes=(bStats,))
        VE("tensor_scalar", prevcol, stats[:, 32:40], pmaskc[:, 0:1], None, ALU.mult,
           reads=(bStats, bConst), writes=(bPrev,))

        if dbg:
            for j, (tb, bb2) in enumerate(((ga_b, bMod), (gb_b, bMod), (cb1_b, bMod),
                                           (sb_b, bMod), (gnwh_b, bMod), (gnbh_b, bMod))):
                SD("dma_start", out=dbg_mod[j], in_=tb, reads=(bb2,), writes=(NB("dbgd"),))
            SD("dma_start", out=dbg_fm[:, 0:8], in_=sa_fm, reads=(bMod,), writes=(NB("dbgd"),))
            SD("dma_start", out=dbg_fm[:, 8:16], in_=ca1_fm, reads=(bMod,), writes=(NB("dbgd"),))

        P.barrier()

        SD("dma_start", out=xts[0], in_=x_in[0:TILE, :], writes=(bXt[0],))
        sparity = 0
        Ss = (S0, S1)
        for i in range(NT):
            xt = xts[i % 2]
            bX = bXt[i % 2]
            if i + 1 < NT:
                SD("dma_start", out=xts[(i + 1) % 2], in_=x_in[(i + 1) * TILE:(i + 2) * TILE, :],
                   writes=(bXt[(i + 1) % 2],))
            if dbg:
                SD("dma_start", out=dbg_xt[i], in_=xt, reads=(bX,), writes=(NB("dbgd"),))
            ln_stats(xt, bX, cols[:, 20:21], cols[:, 21:22], sqbuf, bSq, ln_iters=5)
            SE("activation", xhat, xt, AF.Identity, scale=cols[:, 20:21], bias=cols[:, 21:22],
               reads=(bX, bCols), writes=(bXhat,))
            if dbg:
                SD("dma_start", out=dbg_cols[i], in_=cols, reads=(bCols,), writes=(NB("dbgd"),))
                SD("dma_start", out=dbg_xh[i], in_=xhat, reads=(bXhat,), writes=(NB("dbgd"),))
            for c in range(8):
                T("transpose", PB[0][:, c * 128:(c + 1) * 128], xhat[:, c * 128:(c + 1) * 128],
                  identb, reads=(bXhat, bConst), writes=(pbb[0],))
            for c in range(8):
                SE("activation", h_fm[:, c * 128:(c + 1) * 128], PB[0][:, c * 128:(c + 1) * 128],
                   AF.Identity, scale=ca1_fm[:, c:c + 1], bias=sa_fm[:, c:c + 1],
                   reads=(pbb[0], bMod), writes=(bHfm,))
            hh2 = h_fm.rearrange("p (c e) -> p c e", c=8)
            dd2 = delta.rearrange("p (c e) -> p c e", c=8)
            pc2 = prevcol.rearrange("p (c e) -> p c e", e=1)
            VE("tensor_tensor", dd2[:, :, 0:1], pc2, hh2[:, :, 0:1], ALU.subtract,
               reads=(bPrev, bHfm), writes=(bDelta,))
            VE("tensor_tensor", dd2[:, :, 1:128], hh2[:, :, 0:127], hh2[:, :, 1:128],
               ALU.subtract, reads=(bHfm,), writes=(bDelta,))
            VE("tensor_copy", prevcol, hh2[:, :, 127], reads=(bHfm,), writes=(bPrev,))
            VE("tensor_tensor", xi[0], delta, mu0rep, ALU.mult,
               reads=(bDelta, bConst), writes=(bXi[0],))
            VE("tensor_tensor", xi[0], xi[0], h_fm, ALU.add,
               reads=(bXi[0], bHfm), writes=(bXi[0],))
            for n in range(5):
                bankk = pfb[0] if n < 4 else pfb[1]
                pf = PF[0] if n < 4 else PF[1]
                col0 = (n % 4) * 128 if n < 4 else 0
                for kc in range(8):
                    T("matmul", pf[0:32, col0:col0 + 128],
                      wx0[:, kc * 160 + n * 32:kc * 160 + (n + 1) * 32],
                      xi[0][:, kc * 128:(kc + 1) * 128], start=(kc == 0), stop=(kc == 7),
                      reads=(bW, bXi[0]), writes=(bankk,))
            SE("activation", xp_sb[0:32, 0:512], PF[0][0:32, 0:512], AF.Tanh,
               reads=(pfb[0],), writes=(bXp,))
            SE("activation", xp_sb[0:32, 512:640], PF[1][0:32, 0:128], AF.Tanh,
               reads=(pfb[1],), writes=(bXp,))
            XIMAP = (1, 2, 3, 4, 5)

            def emit_mu_xi(n):
                bankk = pfb[2 + (n % 2)]
                pf = PF[2 + (n % 2)]
                mu = mu_fm[n % 2]
                bM = bMu[n % 2]
                for c8 in range(8):
                    T("matmul", pf[:, (c8 % 4) * 128:(c8 % 4 + 1) * 128],
                      wx2[0:33, n * H + c8 * 128:n * H + (c8 + 1) * 128],
                      xp_sb[0:33, n * 128:(n + 1) * 128], start=True, stop=True,
                      reads=(bW, bXp), writes=(bankk,))
                    if c8 == 3 or c8 == 7:
                        SE("copy", mu[:, (c8 - 3) * 128:(c8 + 1) * 128], pf[:, 0:512],
                           reads=(bankk,), writes=(bM,))
                xin = xi[XIMAP[n]]
                bxin = bXi[XIMAP[n]]
                VE("tensor_tensor", xin, delta, mu, ALU.mult, reads=(bDelta, bM), writes=(bxin,))
                VE("tensor_tensor", xin, xin, h_fm, ALU.add, reads=(bxin, bHfm), writes=(bxin,))

            def emit_rk(which):
                xin, bxin, wt, banki, dst, bdst = (
                    (xi[1], bXi[1], wrT, 4, r_cm, bRc) if which == 0 else
                    (xi[3], bXi[3], wkT, 5, k_cm, bKc))
                bankk = pfb[banki]
                pf = PF[banki]
                for fc in range(4):
                    for kc in range(8):
                        T("matmul", pf[:, fc * 128:(fc + 1) * 128],
                          wt[:, kc * KD + fc * 128:kc * KD + (fc + 1) * 128],
                          xin[:, kc * 128:(kc + 1) * 128], start=(kc == 0), stop=(kc == 7),
                          reads=(bW, bxin), writes=(bankk,))
                SE("copy", dst, pf[:, 0:512], reads=(bankk,), writes=(bdst,))

            def emit_w():
                for kc in range(8):
                    T("matmul", PF[0][0:64, 0:128], ww1[:, kc * 64:(kc + 1) * 64],
                      xi[2][:, kc * 128:(kc + 1) * 128], start=(kc == 0), stop=(kc == 7),
                      reads=(bW, bXi[2]), writes=(pfb[0],))
                SE("activation", w1_sb[0:64, :], PF[0][0:64, 0:128], AF.Tanh,
                   reads=(pfb[0],), writes=(bW1,))
                for fc in range(4):
                    T("matmul", PF[1][:, fc * 128:(fc + 1) * 128],
                      ww2[0:65, fc * 128:(fc + 1) * 128], w1_sb[0:65, :], start=True, stop=True,
                      reads=(bW, bW1), writes=(pfb[1],))
                SE("activation", pw, PF[1][:, 0:512], AF.Exp, reads=(pfb[1],), writes=(bPw,))

            def emit_vg(which, half):
                xin, bxin, wt, dst, bdst = (
                    (xi[4], bXi[4], wvT, v_tm, bVt) if which == 0 else
                    (xi[5], bXi[5], wgT, g_tm, bGt))
                bankk = pfb[2 + half]
                pf = PF[2 + half]
                for kc in range(8):
                    T("matmul", pf[:, 0:512], xin[:, kc * 128:(kc + 1) * 128],
                      wt[:, kc * VD + half * 512:kc * VD + (half + 1) * 512],
                      start=(kc == 0), stop=(kc == 7), reads=(bW, bxin), writes=(bankk,))
                SE("copy", dst[:, half * 512:(half + 1) * 512], pf[:, 0:512],
                   reads=(bankk,), writes=(bdst,))

            emit_mu_xi(0)
            emit_mu_xi(1)
            emit_rk(0)
            emit_mu_xi(2)
            emit_w()
            emit_mu_xi(3)
            emit_rk(1)
            emit_mu_xi(4)
            emit_vg(0, 0)
            emit_vg(0, 1)
            emit_vg(1, 0)
            emit_vg(1, 1)
            for g in range(8):
                hh, sub = divmod(g, 2)
                VE("tensor_tensor_scan", pcum[:, g * 65 + 1:g * 65 + 65],
                   pw[:, hh * 128 + sub * 64:hh * 128 + (sub + 1) * 64], zeros64, 0.0,
                   ALU.add, ALU.add, reads=(bPw, bConst), writes=(bPcum,))
            SE("activation", eneg, pcum, AF.Exp, scale=-1.0, reads=(bPcum,), writes=(bEneg,))
            plc = pcum.rearrange("p (g e) -> p g e", g=8)[:, :, 64]
            VE("tensor_scalar", cols[:, 0:8], plc, 0.5, None, ALU.mult, reads=(bPcum,), writes=(bCols,))
            VE("tensor_scalar", cols[:, 8:16], plc, -0.5, None, ALU.mult, reads=(bPcum,), writes=(bCols,))
            SE("activation", cols[:, 24:32], cols[:, 8:16], AF.Exp, reads=(bCols,), writes=(bCols,))
            for g in range(8):
                SE("activation", facR[:, g * 64:(g + 1) * 64], pcum[:, g * 65:g * 65 + 64],
                   AF.Exp, scale=-1.0, bias=cols[:, g:g + 1], reads=(bPcum, bCols), writes=(bFacR,))
                SE("activation", facK[:, g * 64:(g + 1) * 64], pcum[:, g * 65 + 1:g * 65 + 65],
                   AF.Exp, scale=1.0, bias=cols[:, 8 + g:9 + g], reads=(bPcum, bCols), writes=(bFacK,))
            VE("tensor_tensor", rt_b, r_cm, facR, ALU.mult, reads=(bRc, bFacR), writes=(bRt,))
            VE("tensor_tensor", kt_b, k_cm, facK, ALU.mult, reads=(bKc, bFacK), writes=(bKt,))
            for g in range(8):
                VE("tensor_scalar", khc_b[:, g * 64:(g + 1) * 64], kt_b[:, g * 64:(g + 1) * 64],
                   cols[:, 24 + g:25 + g], None, ALU.mult, reads=(bKt, bCols), writes=(bKhc,))
            VE("tensor_tensor", rhat_b.rearrange("p (g e) -> p g e", g=8), r_cm.rearrange("p (g e) -> p g e", g=8),
               eneg.rearrange("p (g e) -> p g e", g=8)[:, :, 0:64], ALU.mult,
               reads=(bRc, bEneg), writes=(bRh,))
            if i < NCORR:
                for hh in range(NH):
                    VE("tensor_tensor", cols[:, 32 + hh:33 + hh], negrun[:, hh:hh + 1],
                       pcum[:, (hh * 2) * 65 + 64:(hh * 2) * 65 + 65], ALU.subtract,
                       reads=(bNrun, bPcum), writes=(bCols,))
                for g in range(8):
                    hh, sub = divmod(g, 2)
                    bias = negrun[:, hh:hh + 1] if sub == 0 else cols[:, 32 + hh:33 + hh]
                    br = bNrun if sub == 0 else bCols
                    SE("activation", facR[:, g * 64:(g + 1) * 64], pcum[:, g * 65:g * 65 + 64],
                       AF.Exp, scale=-1.0, bias=bias, reads=(bPcum, br), writes=(bFacR,))
                VE("tensor_tensor", rf_b, r_cm, facR, ALU.mult, reads=(bRc, bFacR), writes=(bRf,))
                for hh in range(NH):
                    VE("tensor_tensor", negrun[:, hh:hh + 1], cols[:, 32 + hh:33 + hh],
                       pcum[:, (hh * 2 + 1) * 65 + 64:(hh * 2 + 1) * 65 + 65], ALU.subtract,
                       reads=(bCols, bPcum), writes=(bNrun,))
                SD("dma_start", out=rfs_s[i * TILE:(i + 1) * TILE, :], in_=rf_b,
                   reads=(bRf,), writes=(bRFS,))
            VE("tensor_tensor", rk_b, r_cm, k_cm, ALU.mult, reads=(bRc, bKc), writes=(bRk,))
            for g in range(8):
                hh, sub = divmod(g, 2)
                sl = slice(hh * 128 + sub * 64, hh * 128 + (sub + 1) * 64)
                kw = {"tile_position": (0, 64)} if sub == 1 else {}
                T("matmul", PF[0][sub * 64:(sub + 1) * 64, g * 64:(g + 1) * 64],
                  kt_b[:, sl], rt_b[:, sl], start=True, stop=True,
                  reads=(bKt, bRt), writes=(pfb[0],), **kw)
            VE("select", am_b, mask128, PF[0][:, 0:512], zb512,
               reads=(pfb[0], bConst), writes=(bAm,))
            for g in range(8):
                hh, sub = divmod(g, 2)
                sl = slice(hh * 128 + sub * 64, hh * 128 + (sub + 1) * 64)
                kw = {"tile_position": (0, 64)} if sub == 1 else {}
                T("matmul", PF[1][sub * 64:(sub + 1) * 64, hh:hh + 1], rk_b[:, sl],
                  u_bf[:, hh:hh + 1], start=True, stop=True, reads=(bRk, bW),
                  writes=(pfb[1],), **kw)
            for g in range(8):
                hh, sub = divmod(g, 2)
                sl = slice(hh * 128 + sub * 64, hh * 128 + (sub + 1) * 64)
                kw = {"tile_position": (0, 64)} if sub == 1 else {}
                T("transpose", PB[1][sub * 64:(sub + 1) * 64, hh * 128:(hh + 1) * 128],
                  khc_b[:, sl], identb, reads=(bKhc, bConst), writes=(pbb[1],), **kw)
            SE("copy", khat_tm, PB[1][:, 0:512], reads=(pbb[1],), writes=(bKht,))
            SE("copy", d_sb[:, 0:NH], PF[1][:, 0:NH], reads=(pfb[1], pbb[1]), writes=(bDsb,))
            for sub in range(2):
                Scur, bScur = Ss[sparity], bS[sparity]
                Snew, bSnew = Ss[1 - sparity], bS[1 - sparity]
                r0, r1 = sub * 64, sub * 64 + 64
                for hh in range(NH):
                    g = hh * 2 + sub
                    bankk = pfb[2 + hh // 2]
                    pf = PF[2 + hh // 2]
                    ocol = (hh % 2) * 256
                    sl = slice(hh * 128 + sub * 64, hh * 128 + (sub + 1) * 64)
                    kw1 = {"tile_position": (0, 64)} if sub == 1 else {}
                    kw2 = {"tile_position": (64, 64)} if sub == 1 else {}
                    T("matmul", pf[r0:r1, ocol:ocol + 256], rhat_b[:, sl],
                      S_bf[:, hh * 256:(hh + 1) * 256], start=True, stop=False,
                      reads=(bRh, bSbf), writes=(bankk,), **kw1)
                    T("matmul", pf[r0:r1, ocol:ocol + 256], am_b[r0:r1, g * 64:(g + 1) * 64],
                      v_tm[r0:r1, hh * 256:(hh + 1) * 256], start=False, stop=True,
                      reads=(bAm, bVt), writes=(bankk,), **kw2)
                    sbank = pfb[4 + hh // 2]
                    spf = PF[4 + hh // 2]
                    kw3 = {"tile_position": (64, 0)} if sub == 1 else {}
                    T("matmul", spf[:, (hh % 2) * 256:(hh % 2) * 256 + 256],
                      khat_tm[r0:r1, hh * 128:(hh + 1) * 128],
                      v_tm[r0:r1, hh * 256:(hh + 1) * 256], start=True, stop=True,
                      reads=(bKht, bVt), writes=(sbank,), **kw3)
                for hh in range(NH):
                    pf = PF[2 + hh // 2]
                    bankk = pfb[2 + hh // 2]
                    ocol = (hh % 2) * 256
                    VE("scalar_tensor_tensor", o_sb[r0:r1, hh * 256:(hh + 1) * 256],
                       v_tm[r0:r1, hh * 256:(hh + 1) * 256], d_sb[r0:r1, hh:hh + 1],
                       pf[r0:r1, ocol:ocol + 256], ALU.mult, ALU.add,
                       reads=(bVt, bDsb, bankk), writes=(bOt,))
                for hh in range(NH):
                    g = hh * 2 + sub
                    sbank = pfb[4 + hh // 2]
                    spf = PF[4 + hh // 2]
                    VE("scalar_tensor_tensor", Snew[:, hh * 256:(hh + 1) * 256],
                       Scur[:, hh * 256:(hh + 1) * 256], eneg[:, g * 65 + 64:g * 65 + 65],
                       spf[:, (hh % 2) * 256:(hh % 2) * 256 + 256], ALU.mult, ALU.add,
                       reads=(bScur, bEneg, sbank), writes=(bSnew,))
                VE("tensor_copy", S_bf, Snew, reads=(bSnew,), writes=(bSbf,))
                sparity = 1 - sparity
            if dbg:
                SD("dma_start", out=dbg_am[i], in_=am_b, reads=(bAm,), writes=(NB("dbgd"),))
                SD("dma_start", out=dbg_kt[i], in_=kt_b, reads=(bKt,), writes=(NB("dbgd"),))
                SD("dma_start", out=dbg_rt[i], in_=rt_b, reads=(bRt,), writes=(NB("dbgd"),))
                SD("dma_start", out=dbg_kht[i], in_=khat_tm, reads=(bKht,), writes=(NB("dbgd"),))
                SD("dma_start", out=dbg_rh[i], in_=rhat_b, reads=(bRh,), writes=(NB("dbgd"),))
                SD("dma_start", out=dbg_ds[i], in_=d_sb, reads=(bDsb,), writes=(NB("dbgd"),))
                SD("dma_start", out=dbg_S[i], in_=S_bf, reads=(bSbf,), writes=(NB("dbgd"),))
                SD("dma_start", out=dbg_pc[i], in_=pcum, reads=(bPcum,), writes=(NB("dbgd"),))
            SD("dma_start", out=os_s[i * TILE:(i + 1) * TILE, :], in_=o_sb, reads=(bOt,), writes=(bOS,))
            SD("dma_start", out=gs_s[i * TILE:(i + 1) * TILE, :], in_=g_tm, reads=(bGt,), writes=(bGS,))

        GD("dma_start", out=agp_i, in_=Ss[sparity], reads=(bS[sparity],), writes=(bAGpi,))
        GD("collective_compute", "AllGather", ALU.bypass, [[0, 1], [2, 3], [4, 5], [6, 7]],
           [agp_i], [agp_o], reads=(bAGpi,), writes=(bAGpo,), ch="c")

        P.barrier()

        for w in range(7):
            if w < 3:
                src = wgn_in[w * 128:(w + 1) * 128, :]
                kk = H
            elif w < 6:
                src = wun_in[(w - 3) * 128:(w - 2) * 128, :]
                kk = H
            else:
                src = wdn_in
                kk = IFFN
            GD("dma_start", out=qnat[:, 0:kk], in_=src, writes=(bQn,))
            VE("reduce_max", cols[:, 36:37], qnat[:, 0:kk], axis=AX,
               apply_absolute_value=True, reads=(bQn,), writes=(bCols,))
            VE("reciprocal", cols[:, 37:38], cols[:, 36:37], reads=(bCols,), writes=(bCols,))
            VE("tensor_scalar", cols[:, 38:39], cols[:, 37:38], 127.0, None, ALU.mult,
               reads=(bCols,), writes=(bCols,))
            VE("tensor_scalar", cols[:, 39:40], cols[:, 36:37], 1.0 / 127.0, None, ALU.mult,
               reads=(bCols,), writes=(bCols,))
            T("transpose", PF[0][0:1, 0:128], cols[:, 39:40], identf,
              reads=(bCols, bConst), writes=(pfb[0],), force_drain=True)
            VE("tensor_scalar", qnat[:, 0:kk], qnat[:, 0:kk], cols[:, 38:39], CRND,
               ALU.mult, ALU.add, reads=(bQn, bCols), writes=(bQn,))
            VE("tensor_scalar", qint[:, 0:kk], qnat[:, 0:kk], CRND, None, ALU.subtract,
               reads=(bQn,), writes=(bQi,))
            for c in range(kk // 128):
                pbi = c % 2
                slot = (c // 2) % 4
                T("transpose", PB[pbi][:, slot * 128:(slot + 1) * 128],
                  qint[:, c * 128:(c + 1) * 128], identb, reads=(bQi, bConst), writes=(pbb[pbi],))
                SE("copy", qtrn[:, c * 128:(c + 1) * 128], PB[pbi][:, slot * 128:(slot + 1) * 128],
                   reads=(pbb[pbi],), writes=(bQt,))
            SE("copy", stage, PF[0][0:1, 0:128], reads=(pfb[0], pbb[0]), writes=(bStage,),
               force_drain=True)
            GD("dma_start", out=ags_i[w * 128:(w + 1) * 128], in_=stage,
               reads=(bStage,), writes=(bAGsi,))
            if w < 6:
                GD("dma_start", out=agw_i[:, w * 1024:(w + 1) * 1024], in_=qtrn[:, 0:1024],
                   reads=(bQt,), writes=(bAGwi,))
            else:
                GD("dma_start", out=agw_i[:, 6144:6144 + NGT * 128], in_=qtrn[:, 0:NGT * 128],
                   reads=(bQt,), writes=(bAGwi,))
        GD("collective_compute", "AllGather", ALU.bypass, [list(range(NCORES))],
           [agw_i], [agw_o], reads=(bAGwi,), writes=(bAGwo,), ch="c")
        GD("collective_compute", "AllGather", ALU.bypass, [list(range(NCORES))],
           [ags_i], [ags_o], reads=(bAGsi,), writes=(bAGso,), ch="c")
        GD("dma_start", out=qnat[:, 0:1024],
           in_=agp_o[0:1, :].rearrange("o (p f) -> (o p) f", p=128),
           reads=(bAGpo,), writes=(bQn,))
        VE("tensor_copy", Sin_bf, qnat[:, 0:1024], reads=(bQn,), writes=(bSin,))

        P.barrier()

        order = list(range(NCORR, NT)) + list(range(NCORR))
        for idx, i in enumerate(order):
            par = idx % 2
            ot, bOt3 = a3os[par], bA3o[par]
            gt_, bGt3 = a3gs[par], bA3g[par]
            xt3, bXt3 = a3xs[par], bA3x[par]
            SD("dma_start", out=ot, in_=os_s[i * TILE:(i + 1) * TILE, :], reads=(bOS,), writes=(bOt3,))
            SD("dma_start", out=gt_, in_=gs_s[i * TILE:(i + 1) * TILE, :], reads=(bGS,), writes=(bGt3,))
            SD("dma_start", out=xt3, in_=x_in[i * TILE:(i + 1) * TILE, :], writes=(bXt3,))
            if i < NCORR:
                rft, bRft = a3rfs[par], bA3rf[par]
                SD("dma_start", out=rft, in_=rfs_s[i * TILE:(i + 1) * TILE, :],
                   reads=(bRFS,), writes=(bRft,))
                for sub in range(2):
                    for hh in range(NH):
                        bankk = pfb[hh // 2]
                        pf = PF[hh // 2]
                        kw = {"tile_position": (0, 64)} if sub == 1 else {}
                        T("matmul", pf[sub * 64:(sub + 1) * 64, (hh % 2) * 256:(hh % 2) * 256 + 256],
                          rft[:, hh * 128 + sub * 64:hh * 128 + (sub + 1) * 64],
                          Sin_bf[:, hh * 256:(hh + 1) * 256], start=True, stop=True,
                          reads=(bRft, bSin), writes=(bankk,), **kw)
                for hh in range(NH):
                    bankk = pfb[hh // 2]
                    pf = PF[hh // 2]
                    VE("tensor_tensor", ot[:, hh * 256:(hh + 1) * 256],
                       ot[:, hh * 256:(hh + 1) * 256], pf[:, (hh % 2) * 256:(hh % 2) * 256 + 256],
                       ALU.add, reads=(bankk, bOt3), writes=(bOt3,))
            for hh in range(NH):
                VE("reduce_sum", cols[:, 40 + hh:41 + hh], ot[:, hh * 256:(hh + 1) * 256],
                   axis=AX, reads=(bOt3,), writes=(bCols,))
                SE("activation", a3sq[:, hh * 256:(hh + 1) * 256], ot[:, hh * 256:(hh + 1) * 256],
                   AF.Square, accum_out=cols[:, 44 + hh:45 + hh], reads=(bOt3,),
                   writes=(bA3sq, bCols))
            VE("tensor_scalar", cols[:, 40:44], cols[:, 40:44], 1.0 / DV, None, ALU.mult,
               reads=(bCols,), writes=(bCols,))
            VE("tensor_scalar", cols[:, 44:48], cols[:, 44:48], 1.0 / DV, None, ALU.mult,
               reads=(bCols,), writes=(bCols,))
            VE("tensor_scalar", cols[:, 36:40], cols[:, 40:44], -1.0, None, ALU.mult,
               reads=(bCols,), writes=(bCols,))
            for hh in range(NH):
                VE("scalar_tensor_tensor", cols[:, 32 + hh:33 + hh], cols[:, 40 + hh:41 + hh],
                   cols[:, 36 + hh:37 + hh], cols[:, 44 + hh:45 + hh], ALU.mult, ALU.add,
                   reads=(bCols,), writes=(bCols,))
            rsqrt_newton(cols[:, 28:32], cols[:, 32:36], 4, iters=10)
            VE("tensor_tensor", cols[:, 24:28], cols[:, 40:44], cols[:, 28:32], ALU.mult,
               reads=(bCols,), writes=(bCols,))
            VE("tensor_scalar", cols[:, 20:24], cols[:, 24:28], -1.0, None, ALU.mult,
               reads=(bCols,), writes=(bCols,))
            for hh in range(NH):
                SE("activation", xo_b[:, hh * 256:(hh + 1) * 256], ot[:, hh * 256:(hh + 1) * 256],
                   AF.Identity, scale=cols[:, 28 + hh:29 + hh], bias=cols[:, 20 + hh:21 + hh],
                   reads=(bOt3, bCols), writes=(bXo,))
            SE("activation", th_b, gt_, AF.Tanh, scale=0.5, reads=(bGt3,), writes=(bTh,))
            VE("tensor_scalar", th_b, th_b, 0.5, 0.5, ALU.mult, ALU.add,
               reads=(bTh,), writes=(bTh,))
            VE("tensor_tensor", sg2_b, th_b, gt_, ALU.mult, reads=(bTh, bGt3), writes=(bSg2,))
            VE("tensor_tensor", of_b, xo_b, gnwh_b, ALU.mult, reads=(bXo, bMod), writes=(bOf,))
            VE("tensor_tensor", of_b, of_b, gnbh_b, ALU.add, reads=(bOf, bMod), writes=(bOf,))
            VE("tensor_tensor", of_b, of_b, sg2_b, ALU.mult, reads=(bOf, bSg2), writes=(bOf,))
            for c in range(8):
                T("transpose", PB[0][:, c * 128:(c + 1) * 128], of_b[:, c * 128:(c + 1) * 128],
                  identb, reads=(bOf, bConst), writes=(pbb[0],))
            SE("copy", of_fm, PB[0][:, :], reads=(pbb[0],), writes=(bOffm,))
            for half in range(2):
                bankk = pfb[4 + half]
                pf = PF[4 + half]
                for kc in range(8):
                    T("matmul", pf[:, 0:512], of_fm[:, kc * 128:(kc + 1) * 128],
                      woT[:, kc * H + half * 512:kc * H + (half + 1) * 512],
                      start=(kc == 0), stop=(kc == 7), reads=(bOffm, bW), writes=(bankk,))
            for half in range(2):
                bankk = pfb[4 + half]
                pf = PF[4 + half]
                VE("tensor_tensor", t1_s[:, half * 512:(half + 1) * 512], pf[:, 0:512],
                   ga_b[:, half * 512:(half + 1) * 512], ALU.mult,
                   reads=(bankk, bMod), writes=(bT1,))
            VE("tensor_tensor", x2_t, t1_s, xt3, ALU.add, reads=(bT1, bXt3), writes=(bX2t,))
            ln_stats(x2_t, bX2t, cols[:, 18:19], cols[:, 19:20], a3sq, bA3sq)
            SE("activation", t1_s, x2_t, AF.Identity, scale=cols[:, 18:19], bias=cols[:, 19:20],
               reads=(bX2t, bCols), writes=(bT1,))
            VE("tensor_tensor", h2_t, t1_s, cb1_b, ALU.mult, reads=(bT1, bMod), writes=(bH2t,))
            VE("tensor_tensor", h2_t, h2_t, sb_b, ALU.add, reads=(bH2t, bMod), writes=(bH2t,))
            VE("reduce_max", cols[:, 17:18], h2_t, axis=AX, apply_absolute_value=True,
               reads=(bH2t,), writes=(bCols,))
            VE("tensor_tensor", runmax[:, 0:1], runmax[:, 0:1], cols[:, 17:18], ALU.max,
               reads=(bCols, bRmx), writes=(bRmx,))
            SD("dma_start", out=x2_s[i * TILE:(i + 1) * TILE, :], in_=x2_t,
               reads=(bX2t,), writes=(bX2S,))
            SD("dma_start", out=h2_s[i * TILE:(i + 1) * TILE, :], in_=h2_t,
               reads=(bH2t,), writes=(bH2S,))

        T("transpose", PF[0][0:1, 0:128], runmax[:, 0:1], identf, reads=(bRmx, bConst),
          writes=(pfb[0],), force_drain=True)
        SE("copy", stage, PF[0][0:1, 0:128], reads=(pfb[0],), writes=(bStage,), force_drain=True)
        VE("reduce_max", scrow[0:1, 14:15], stage, axis=AX, reads=(bStage,), writes=(bRows,))
        GD("dma_start", out=arh_i, in_=scrow[0:1, 14:15], reads=(bRows,), writes=(bARh,))
        GD("collective_compute", "AllReduce", ALU.max, [list(range(NCORES))],
           [arh_i], [arh_o], reads=(bARh,), writes=(bARho,), ch="c")

        P.barrier()

        for r in range(NCORES):
            for l in range(3):
                gt_i = 3 * r + l
                if gt_i >= NGT:
                    continue
                SD("dma_start",
                   out=wqg.rearrange("p (c f) -> p c f", c=8)[:, :, gt_i * 128:(gt_i + 1) * 128],
                   in_=agw_o[r, :, l * 1024:(l + 1) * 1024].rearrange("p (c j) -> p c j", c=8),
                   reads=(bAGwo,), writes=(bWq,))
                SD("dma_start",
                   out=wqu.rearrange("p (c f) -> p c f", c=8)[:, :, gt_i * 128:(gt_i + 1) * 128],
                   in_=agw_o[r, :, (3 + l) * 1024:(4 + l) * 1024].rearrange("p (c j) -> p c j", c=8),
                   reads=(bAGwo,), writes=(bWq,))
                SD("dma_start", out=sg_row[0:1, gt_i * 128:(gt_i + 1) * 128],
                   in_=ags_o[r, l * 128:(l + 1) * 128], reads=(bAGso,), writes=(bRows,))
                SD("dma_start", out=su_row[0:1, gt_i * 128:(gt_i + 1) * 128],
                   in_=ags_o[r, (3 + l) * 128:(4 + l) * 128], reads=(bAGso,), writes=(bRows,))
        SD("dma_start", out=cpx[0:1, 2:3], in_=arh_o, reads=(bARho,), writes=(bRows,))
        SE("activation", cpx[0:1, 4:5], cpx[0:1, 0:1], AF.Tanh, scale=0.5,
           reads=(bRows,), writes=(bRows,))
        VE("tensor_scalar", cpx[0:1, 5:6], cpx[0:1, 4:5], 0.5, 0.5, ALU.mult, ALU.add,
           reads=(bRows,), writes=(bRows,))
        VE("tensor_tensor", cpx[0:1, 6:7], cpx[0:1, 5:6], cpx[0:1, 2:3], ALU.mult,
           reads=(bRows,), writes=(bRows,))
        VE("tensor_scalar", cpx[0:1, 6:7], cpx[0:1, 6:7], 1.0 / 127.0, None, ALU.mult,
           reads=(bRows,), writes=(bRows,))
        VE("reciprocal", cpx[0:1, 7:8], cpx[0:1, 6:7], reads=(bRows,), writes=(bRows,))
        T("matmul", PF[0][:, 0:1], onesrow, cpx[0:1, 7:8], start=True, stop=True,
          reads=(bRows, bConst), writes=(pfb[0],))
        SE("copy", cols[:, 16:17], PF[0][:, 0:1], reads=(pfb[0],), writes=(bCols,))
        VE("tensor_scalar", sg_row, sg_row, cpx[0:1, 6:7], None, ALU.mult,
           reads=(bRows,), writes=(bRows,))
        VE("tensor_scalar", su_row, su_row, cpx[0:1, 6:7], None, ALU.mult,
           reads=(bRows,), writes=(bRows,))
        VE("tensor_scalar", su_row, su_row, 0.5, None, ALU.mult, reads=(bRows,), writes=(bRows,))
        for (row, dstb) in ((sg_row, sxg_b), (su_row, sxu_b)):
            for (s0, s1) in SLABS:
                T("matmul", PF[1][:, 0:s1 - s0], onesrow, row[0:1, s0:s1],
                  start=True, stop=True, reads=(bRows, bConst), writes=(pfb[1],))
                SE("copy", dstb[:, s0:s1], PF[1][:, 0:s1 - s0], reads=(pfb[1],), writes=(bSx,))

        SD("dma_start", out=bhs[0], in_=h2_s[0:TILE, :], reads=(bH2S,), writes=(bBh[0],))
        for i in range(NT):
            par = i % 2
            ht, bHt = bhs[par], bBh[par]
            tt, bTt = bts[par], bBt[par]
            if i + 1 < NT:
                SD("dma_start", out=bhs[(i + 1) % 2], in_=h2_s[(i + 1) * TILE:(i + 2) * TILE, :],
                   reads=(bH2S,), writes=(bBh[(i + 1) % 2],))
            VE("tensor_scalar", bq1, ht, cols[:, 16:17], CRND, ALU.mult, ALU.add,
               reads=(bHt, bCols), writes=(bBq1,))
            VE("tensor_scalar", bq1, bq1, CRND, 127.0, ALU.subtract, ALU.min,
               reads=(bBq1,), writes=(bBq1,))
            VE("tensor_scalar", bxq, bq1, -128.0, None, ALU.max, reads=(bBq1,), writes=(bBxq,))
            for c in range(8):
                T("transpose", PB[0][:, c * 128:(c + 1) * 128], bxq[:, c * 128:(c + 1) * 128],
                  identb, reads=(bBxq, bConst), writes=(pbb[0],))
            SE("copy", bxqf, PB[0][:, :], reads=(pbb[0],), writes=(bBxqf,))
            if dbg and i < 4:
                SD("dma_start", out=dbg_bxq[i], in_=bxq, reads=(bBxq,), writes=(NB("dbgd"),))
            if dbg and i == 0:
                SD("dma_start", out=dbg_wqg, in_=wqg, reads=(bWq,), writes=(NB("dbgd"),))
            for si_, (s0, s1) in enumerate(SLABS):
                n = s1 - s0
                gb_i = 2 + 2 * (si_ % 2)
                ub_i = 3 + 2 * (si_ % 2)
                for kc in range(8):
                    T("matmul", PF[gb_i][:, 0:n], bxqf[:, kc * 128:(kc + 1) * 128],
                      wqg.rearrange("p (c f) -> p c f", c=8)[:, kc, s0:s1],
                      start=(kc == 0), stop=(kc == 7), reads=(bBxqf, bWq), writes=(pfb[gb_i],))
                for kc in range(8):
                    T("matmul", PF[ub_i][:, 0:n], bxqf[:, kc * 128:(kc + 1) * 128],
                      wqu.rearrange("p (c f) -> p c f", c=8)[:, kc, s0:s1],
                      start=(kc == 0), stop=(kc == 7), reads=(bBxqf, bWq), writes=(pfb[ub_i],))
                VE("tensor_tensor", bgr[:, 0:n], PF[gb_i][:, 0:n], sxg_b[:, s0:s1], ALU.mult,
                   reads=(pfb[gb_i], bSx), writes=(bBgr,))
                VE("tensor_tensor", bur[:, 0:n], PF[ub_i][:, 0:n], sxu_b[:, s0:s1], ALU.mult,
                   reads=(pfb[ub_i], bSx), writes=(bBur,))
                SE("activation", bth[:, 0:n], bgr[:, 0:n], AF.Tanh, scale=0.5,
                   reads=(bBgr,), writes=(bBth,))
                VE("tensor_tensor", bur[:, 0:n], bur[:, 0:n], bgr[:, 0:n], ALU.mult,
                   reads=(bBur, bBgr), writes=(bBur,))
                VE("scalar_tensor_tensor", tt[:, s0:s1], bth[:, 0:n], 1.0, bur[:, 0:n],
                   ALU.add, ALU.mult, reads=(bBth, bBur),
                   writes=(bTt, bRows) if i < 2 and s0 == 0 else (bTt,))
            VE("reduce_max", runmax[:, 2:3], tt, axis=AX,
               apply_absolute_value=True, reads=(bTt,), writes=(bRmx,))
            VE("tensor_tensor", runmax[:, 1:2], runmax[:, 1:2], runmax[:, 2:3],
               ALU.max, reads=(bRmx,), writes=(bRmx,))
            SD("dma_start", out=t_s[i * TILE:(i + 1) * TILE, :], in_=tt, reads=(bTt,), writes=(bTS,))
        T("transpose", PF[0][0:1, 0:128], runmax[:, 1:2], identf, reads=(bRmx, bConst),
          writes=(pfb[0],), force_drain=True)
        SE("copy", stage, PF[0][0:1, 0:128], reads=(pfb[0],), writes=(bStage,), force_drain=True)
        VE("reduce_max", scrow[0:1, 15:16], stage, axis=AX, reads=(bStage,), writes=(bRows,))
        GD("dma_start", out=art_i, in_=scrow[0:1, 15:16], reads=(bRows,), writes=(bARt,))
        GD("collective_compute", "AllReduce", ALU.max, [list(range(NCORES))],
           [art_i], [art_o], reads=(bARt,), writes=(bARto,), ch="c")

        P.barrier()

        for r in range(NCORES):
            SD("dma_start",
               out=wqd.rearrange("p (c f) -> p c f", c=NGT)[:, :, r * 128:(r + 1) * 128],
               in_=agw_o[r, :, 6144:6144 + NGT * 128].rearrange("p (c j) -> p c j", c=NGT),
               reads=(bAGwo,), writes=(bWq,))
        for r in range(NCORES):
            SD("dma_start", out=sd_row[0:1, r * 128:(r + 1) * 128],
               in_=ags_o[r, 6 * 128:7 * 128], reads=(bAGso,), writes=(bRows,))
        SD("dma_start", out=cpx[0:1, 3:4], in_=art_o, reads=(bARto,), writes=(bRows,))
        SE("activation", cpx[0:1, 8:9], cpx[0:1, 1:2], AF.Tanh, scale=0.5, reads=(bRows,), writes=(bRows,))
        VE("tensor_scalar", cpx[0:1, 9:10], cpx[0:1, 8:9], 0.5, 0.5, ALU.mult, ALU.add,
           reads=(bRows,), writes=(bRows,))
        VE("tensor_tensor", cpx[0:1, 10:11], cpx[0:1, 9:10], cpx[0:1, 3:4], ALU.mult,
           reads=(bRows,), writes=(bRows,))
        VE("tensor_scalar", cpx[0:1, 10:11], cpx[0:1, 10:11], 1.0 / 127.0, None, ALU.mult,
           reads=(bRows,), writes=(bRows,))
        VE("reciprocal", cpx[0:1, 11:12], cpx[0:1, 10:11], reads=(bRows,), writes=(bRows,))
        T("matmul", PF[0][:, 0:1], onesrow, cpx[0:1, 11:12], start=True, stop=True,
          reads=(bRows, bConst), writes=(pfb[0],))
        SE("copy", cols[:, 15:16], PF[0][:, 0:1], reads=(pfb[0],), writes=(bCols,))
        VE("tensor_scalar", sd_row, sd_row, cpx[0:1, 10:11], None, ALU.mult,
           reads=(bRows,), writes=(bRows,))
        for sl in range(2):
            T("matmul", PF[1][:, 0:512], onesrow, sd_row[0:1, sl * 512:(sl + 1) * 512],
              start=True, stop=True, reads=(bRows, bConst), writes=(pfb[1],))
            SE("copy", co_t[:, sl * 512:(sl + 1) * 512], PF[1][:, 0:512],
               reads=(pfb[1],), writes=(bCo,))
        VE("tensor_tensor", sdtgb_b, co_t, gb_b, ALU.mult, reads=(bCo, bMod), writes=(bSdt,))

        SD("dma_start", out=cts[0], in_=t_s[0:TILE, :], reads=(bTS,), writes=(bCt[0], bRows))
        SD("dma_start", out=cxs[0], in_=x2_s[0:TILE, :], reads=(bX2S,), writes=(bCx[0],))
        for i in range(NT):
            par = i % 2
            tt, bTt = cts[par], bCt[par]
            xx, bXx = cxs[par], bCx[par]
            if i + 1 < NT:
                SD("dma_start", out=cts[(i + 1) % 2], in_=t_s[(i + 1) * TILE:(i + 2) * TILE, :],
                   reads=(bTS,), writes=(bCt[(i + 1) % 2],))
                SD("dma_start", out=cxs[(i + 1) % 2], in_=x2_s[(i + 1) * TILE:(i + 2) * TILE, :],
                   reads=(bX2S,), writes=(bCx[(i + 1) % 2],))
            VE("tensor_scalar", cq1, tt, cols[:, 15:16], CRND, ALU.mult, ALU.add,
               reads=(bTt, bCols), writes=(bCq1,))
            VE("tensor_scalar", cq1, cq1, CRND, 127.0, ALU.subtract, ALU.min,
               reads=(bCq1,), writes=(bCq1,))
            VE("tensor_scalar", chq, cq1, -128.0, None, ALU.max, reads=(bCq1,), writes=(bChq,))
            for c in range(NGT):
                pbi = c % 2
                slot = (c // 2) % 4
                T("transpose", PB[pbi][:, slot * 128:(slot + 1) * 128],
                  chq[:, c * 128:(c + 1) * 128], identb, reads=(bChq, bConst), writes=(pbb[pbi],))
                SE("copy", chqf[:, c * 128:(c + 1) * 128], PB[pbi][:, slot * 128:(slot + 1) * 128],
                   reads=(pbb[pbi],), writes=(bChqf,))
            for half in range(2):
                bi_ = 2 + half + 2 * (i % 2)
                bankk = pfb[bi_]
                pf = PF[bi_]
                for kc in range(NGT):
                    T("matmul", pf[:, 0:512], chqf[:, kc * 128:(kc + 1) * 128],
                      wqd.rearrange("p (c f) -> p c f", c=NGT)[:, kc, half * 512:(half + 1) * 512],
                      start=(kc == 0), stop=(kc == NGT - 1), reads=(bChqf, bWq), writes=(bankk,))
                VE("tensor_tensor", co_t[:, half * 512:(half + 1) * 512], pf[:, 0:512],
                   sdtgb_b[:, half * 512:(half + 1) * 512], ALU.mult,
                   reads=(bankk, bSdt), writes=(bCo,))
            if dbg and i < 4:
                SD("dma_start", out=dbg_chq[i], in_=chq, reads=(bChq,), writes=(NB("dbgd"),))
            if dbg and i == 0:
                SD("dma_start", out=dbg_wqd, in_=wqd, reads=(bWq,), writes=(NB("dbgd"),))
                SD("dma_start", out=dbg_sdt, in_=sdtgb_b, reads=(bSdt,), writes=(NB("dbgd"),))
                SD("dma_start", out=dbg_cpx, in_=cpx, reads=(bRows,), writes=(NB("dbgd"),))
            VE("tensor_tensor", co_t, co_t, xx, ALU.add, reads=(bCo, bXx), writes=(bCo,))
            SD("dma_start", out=out[i * TILE:(i + 1) * TILE, :], in_=co_t,
               reads=(bCo,), writes=(bOutD,))
        for e in ("v", "s", "t", "g"):
            P._wait(e, bOutD.wr)
        P.final_wait("d")

    with ExitStack() as ctx:
        e = ctx.enter_context
        pool_t = e(nc.sbuf_tensor("pool", [128, POOLB], mybir.dt.uint8))
        pool = pool_t
        PF = [e(nc.psum_tensor(f"pf{i}", [128, 512], F32)) for i in range(6)]
        PB = [e(nc.psum_tensor(f"pb{i}", [128, 1024], BF16)) for i in range(2)]
        sems = {n: e(nc.semaphore(f"sem_{n}"))
                for n in ("v", "s", "t", "d", "g", "c", "de", "ge")}
        sem_pool = [e(nc.semaphore(f"sp{i}")) for i in range(40)]
        block = e(nc.Block())

        def mk_chans():
            ch = {n: CH(sems[n], n) for n in
                  ("v", "s", "t", "d", "g", "c", "de", "ge")}
            ch["de"].owner = "none"
            ch["ge"].owner = "none"
            return ch

        @block.tensor
        def _(eng):
            program(K("t", {"t": eng}, mk_chans(), sem_pool), pool, PF, PB)

        @block.vector
        def _(eng):
            program(K("v", {"v": eng}, mk_chans(), sem_pool), pool, PF, PB)

        @block.scalar
        def _(eng):
            program(K("s", {"s": eng}, mk_chans(), sem_pool), pool, PF, PB)

        @block.sync
        def _(eng):
            program(K("d", {"d": eng}, mk_chans(), sem_pool), pool, PF, PB)

        @block.gpsimd
        def _(eng):
            program(K("g", {"g": eng}, mk_chans(), sem_pool), pool, PF, PB)

    return nc


_NC_CACHE = {}


def _get_nc():
    if "nc" not in _NC_CACHE:
        _NC_CACHE["nc"] = build_nc(dbg=_NC_CACHE.get("dbg", False))
    return _NC_CACHE["nc"]


_RES_STASH = {}


def kernel(**inputs):
    f32 = np.float32
    x = np.asarray(inputs["L0_attn_hidden_states"], dtype=f32)
    W_x0 = np.asarray(inputs["L0_attn_W_x0"], f32)
    W_x2 = np.asarray(inputs["L0_attn_W_x2"], f32)
    x_bias = np.asarray(inputs["L0_attn_x_bias"], f32)
    W_r = np.asarray(inputs["L0_attn_W_r"], f32)
    W_w1 = np.asarray(inputs["L0_attn_W_w1"], f32)
    W_w2 = np.asarray(inputs["L0_attn_W_w2"], f32)
    b_w2 = np.asarray(inputs["L0_attn_b_w2"], f32)
    W_k = np.asarray(inputs["L0_attn_W_k"], f32)
    W_v = np.asarray(inputs["L0_attn_W_v"], f32)
    W_g = np.asarray(inputs["L0_attn_W_g"], f32)
    W_o = np.asarray(inputs["L0_attn_W_o"], f32)
    mu0 = np.asarray(inputs["L0_attn_mu0"], f32)
    bonus = np.asarray(inputs["L0_attn_bonus"], f32)
    gnw = np.asarray(inputs["L0_attn_gn_weight"], f32)
    gnb = np.asarray(inputs["L0_attn_gn_bias"], f32)
    clip_x = np.asarray(inputs["L0_ffn_clip_x"], f32)
    clip_d = np.asarray(inputs["L0_ffn_clip_down"], f32)
    w_gate = np.asarray(inputs["L0_ffn_w_gate"], f32)
    w_up = np.asarray(inputs["L0_ffn_w_up"], f32)
    w_down = np.asarray(inputs["L0_ffn_w_down"], f32)
    L0_c = np.asarray(inputs["L0_c"], f32)
    wmod = np.asarray(inputs["L0_wmod"], f32)

    def kchunk(wT):
        Kd, Fd = wT.shape
        return np.ascontiguousarray(
            wT.reshape(Kd // 128, 128, Fd).transpose(1, 0, 2).reshape(128, -1))

    wx0T = kchunk(np.ascontiguousarray(W_x0.T))
    wx2T = np.ascontiguousarray(
        np.concatenate([W_x2.reshape(H, 5, 32).transpose(1, 2, 0),
                        x_bias[:, None, :]], axis=1).transpose(1, 0, 2).reshape(33, 5 * H))
    wrT = kchunk(np.ascontiguousarray(W_r.T))
    ww1T = kchunk(np.ascontiguousarray(W_w1.T))
    ww2T = np.ascontiguousarray(np.concatenate([W_w2.T, b_w2[None, :]], axis=0))
    wkT = kchunk(np.ascontiguousarray(W_k.T))
    wvT = kchunk(np.ascontiguousarray(W_v.T))
    wgT = kchunk(np.ascontiguousarray(W_g.T))
    woT = kchunk(np.ascontiguousarray(W_o.T))
    mu0fm = np.ascontiguousarray(mu0.reshape(8, 128).T)
    bonusT = np.ascontiguousarray(bonus.T)
    identb = np.eye(128, dtype=ml_dtypes.bfloat16)
    identf = np.eye(128, dtype=np.float32)
    base = np.tril(np.ones((64, 64), dtype=f32), -1).T
    m128 = np.zeros((128, 512), dtype=f32)
    for g in range(8):
        m128[0:64, g * 64:(g + 1) * 64] = base
        m128[64:128, g * 64:(g + 1) * 64] = base
    m128 = m128.astype(np.uint8)
    onesrow = np.ones((1, 128), dtype=f32)
    wgp = np.concatenate([w_gate, np.ones((24 * 128 - IFFN, H), f32)], axis=0)
    wup = np.concatenate([w_up, np.ones((24 * 128 - IFFN, H), f32)], axis=0)

    nc = _get_nc()
    in_maps = []
    for c in range(NCORES):
        b, hf = c // 2, c % 2
        t0 = hf * TCORE
        xprev = np.zeros((1, H), f32) if hf == 0 else np.ascontiguousarray(x[b, t0 - 1:t0, :])
        in_maps.append({
            "x": np.ascontiguousarray(x[b, t0:t0 + TCORE, :]),
            "xprev": xprev,
            "cvec": np.ascontiguousarray(L0_c[b:b + 1, :]),
            "wmod": wmod,
            "gnw": np.ascontiguousarray(np.broadcast_to(gnw[None, :], (128, VD))),
            "gnb": np.ascontiguousarray(np.broadcast_to(gnb[None, :], (128, VD))),
            "mu0": mu0fm,
            "mu0r": np.ascontiguousarray(np.broadcast_to(mu0fm[:, :, None], (128, 8, 128)).reshape(128, 1024).astype(ml_dtypes.bfloat16)),
            "bonus": bonusT,
            "pmask": np.full((128, 1), float(hf), f32),
            "cmb": np.full((128, 1), 0.0 if hf == 1 else -1e9, f32),
            "clipx": clip_x.reshape(1, 1), "clipd": clip_d.reshape(1, 1),
            "identb": identb, "identf": identf, "mask128": m128, "onesrow": onesrow,
            "wx0T": wx0T, "wx2T": wx2T, "wrT": wrT, "ww1T": ww1T, "ww2T": ww2T,
            "wkT": wkT, "wvT": wvT, "wgT": wgT, "woT": woT,
            "wg_nat": np.ascontiguousarray(wgp[384 * c:384 * (c + 1)]),
            "wu_nat": np.ascontiguousarray(wup[384 * c:384 * (c + 1)]),
            "wd_nat": np.ascontiguousarray(w_down[128 * c:128 * (c + 1)]),
        })
    res = run_bass_kernel_spmd(nc, in_maps, core_ids=list(range(NCORES)))
    _RES_STASH["res"] = res
    outp = np.empty((B, TFULL, H), f32)
    for c in range(NCORES):
        b, hf = c // 2, c % 2
        outp[b, hf * TCORE:(hf + 1) * TCORE, :] = res.results[c]["out"]
    return outp
